# version 10
# baseline (speedup 1.0000x reference)
import numpy as np
from contextlib import ExitStack
import concourse.bass as bass
import concourse.mybir as mybir

F32 = mybir.dt.float32
BF16 = mybir.dt.bfloat16
F32R = mybir.dt.float32r
I32 = mybir.dt.int32
AF = mybir.ActivationFunctionType
ALU = mybir.AluOpType
AX = mybir.AxisListType

ENGINES = ("tensor", "vector", "scalar", "sync", "gpsimd")

D = 1024
T = 2048
NH = 16
DH = 64
NI = 8
NT = 16
NG = 4
NE = 8
HF = 4096
NFI = 32
NEG_BIG = -1.0e30
QS = 0.35355339059327373
LN_EPS = 1e-5


def make_consts():
    c = {}
    c["c_ident_f"] = np.eye(128, dtype=np.float32)
    c["c_ident_b"] = np.eye(128, dtype=np.float32)
    c["c_tri_b"] = np.triu(np.ones((128, 128), np.float32), 1) * NEG_BIG
    c["c_onesrow_f"] = np.ones((1, 128), dtype=np.float32)
    c["c_onesrow_r"] = np.ones((1, 128), dtype=np.float32)
    c["c_halfcol_b"] = np.full((128, 1), 0.5, dtype=np.float32)
    c["c_onescol_b"] = np.ones((128, 1), dtype=np.float32)
    c["c_iota_row_b"] = np.arange(T, dtype=np.float32)[None, :]
    c["c_half_row_b"] = np.full((1, 512), 0.5, dtype=np.float32)
    c["c_iotacol16"] = (np.arange(16)[None, :] * 128 + np.arange(128)[:, None]).astype(np.float32)
    c["c_epscol"] = np.full((128, 1), LN_EPS, dtype=np.float32)
    sel = np.zeros((8, 8 * 128), dtype=np.float32)
    for e in range(8):
        sel[e, e * 128:(e + 1) * 128] = 1.0
    c["c_sel"] = sel
    return c


CONST_DTYPES = {
    "c_ident_f": F32, "c_ident_b": BF16, "c_tri_b": BF16,
    "c_onesrow_f": F32, "c_onesrow_r": F32R, "c_halfcol_b": BF16, "c_onescol_b": BF16,
    "c_iota_row_b": BF16, "c_half_row_b": BF16,
    "c_iotacol16": F32, "c_sel": F32, "c_epscol": F32,
}


def np_for(dt):
    import ml_dtypes
    return {F32: np.float32, BF16: ml_dtypes.bfloat16, F32R: np.float32,
            I32: np.int32}[dt]


class MK:
    def __init__(self, nbatch=2, layers=(0, 1), sublayers=("attn", "ffn")):
        self.nbatch = nbatch
        self.layers = tuple(layers)
        self.sublayers = tuple(sublayers)

    def sem(self, name):
        if name not in self.sems:
            h = self.es.enter_context(self.nc.semaphore(name))
            self.sems[name] = [h, 0]
        return self.sems[name]

    def on(self, eng, fn, inc=None, n=1, waits=()):
        w = self.pending[eng] + list(waits)
        self.pending[eng] = []
        if eng in ("vector", "scalar"):
            own = self.OWN[eng]
            c = self.sem(own)[1]
            if c > 0:
                w.append((own, c))
            if inc is None:
                inc = own
        lw = self.lastwait[eng]
        best = {}
        for t in w:
            if t is None:
                continue
            s, v = t
            if v > best.get(s, -1):
                best[s] = v
        w2 = []
        for s, v in best.items():
            if v > lw.get(s, -1) and v > 0:
                w2.append((s, v))
                lw[s] = v
        tok = None
        if inc is not None:
            c = self.sem(inc)
            c[1] += n
            tok = (inc, c[1])
        self.ops[eng].append((tuple(w2), fn, inc, n))
        self.nops += 1
        return tok

    def tok(self, sem):
        return (sem, self.sems[sem][1])

    def dma(self, queue, stream, fn, self_wait=False):
        tok = self.on(queue, fn, inc="d_" + stream, n=16)
        if self_wait:
            self.on(queue, lambda e: e.nop(), waits=[tok])
        return tok

    def lp(self, fn):
        def wrapped(e):
            with self.nc.allow_low_precision(reason="f32r row for broadcast matmul"):
                return fn(e)
        return wrapped

    def wait(self, eng, tok):
        if tok is not None:
            self.pending[eng].append(tok)

    def waits(self, eng, toks):
        for t in toks:
            self.wait(eng, t)

    OWN = {"tensor": "spe", "vector": "sv", "scalar": "ss", "sync": "sio", "gpsimd": "sdw"}

    def bar(self, engines=ENGINES):
        snap = [(s, c[1]) for s, c in self.sems.items() if c[1] > 0]
        for e in engines:
            for s, v in snap:
                self.pending[e].append((s, v))

    def replay(self, ename, eng):
        sems = self.sems
        for w, fn, inc, n in self.ops[ename]:
            for s, v in w:
                eng.wait_ge(sems[s][0], v)
            instr = fn(eng)
            if inc is not None:
                instr.then_inc(sems[inc][0], n)

    def pbank(self, k, lo=0, hi=128, w0=0, w1=512):
        if k < 4:
            return self.pbA[lo:hi, k * 512 + w0: k * 512 + w1]
        if k < 8:
            return self.pbB[lo:hi, (k - 4) * 512 + w0: (k - 4) * 512 + w1]
        raise KeyError(k)

    def pb_wait(self, k):
        g = self.bank_guard.get(k)
        if g:
            if isinstance(g[0], str):
                self.pending["tensor"].append(g)
            else:
                for t in g:
                    self.pending["tensor"].append(t)

    def pb_set(self, k, tok):
        self.bank_guard[k] = tok

    def build(self, nc, es):
        self.nc = nc
        self.es = es
        self.ops = {e: [] for e in ENGINES}
        self.pending = {e: [] for e in ENGINES}
        self.lastwait = {e: {} for e in ENGINES}
        self.sems = {}
        self.bank_guard = {}
        self.guards = {}
        self.nops = 0
        for s in ("spe", "sv", "ss", "sio", "sdw"):
            self.sem(s)

        dram = nc.dram_tensor
        self.moe_cached = set()
        self.w1c = dram("w1cache", [NE, D, HF], BF16, kind="Internal").ap()
        self.w2c = dram("w2cache", [NE, HF, D], BF16, kind="Internal").ap()
        self.hx = dram("x_in", [self.nbatch, T, D], F32, kind="ExternalInput").ap()
        self.hout = dram("x_out", [self.nbatch, T, D], F32, kind="ExternalOutput").ap()
        self.hw = {}

        def win(name, shape):
            self.hw[name] = dram(name, shape, F32, kind="ExternalInput").ap()
        if 0 in self.layers:
            if "attn" in self.sublayers:
                win("L0_attn_w_attn", [D, 3 * D]); win("L0_attn_b_attn", [3 * D])
                win("L0_attn_w_proj", [D, D]); win("L0_attn_b_proj", [D])
            if "ffn" in self.sublayers:
                win("L0_ffn_Wr", [NE, D]); win("L0_ffn_W1", [NE, D, HF])
                win("L0_ffn_b1", [NE, HF]); win("L0_ffn_W2", [NE, HF, D])
                win("L0_ffn_b2", [NE, D])
        if 1 in self.layers:
            if "attn" in self.sublayers:
                win("L1_attn_W_kqv", [D, 3 * D]); win("L1_attn_b_kqv", [3 * D])
                win("L1_attn_W_proj", [D, D]); win("L1_attn_b_proj", [D])
                win("L1_attn_slopes", [NH])
            if "ffn" in self.sublayers:
                win("L1_ffn_weight", [D, D]); win("L1_ffn_bias", [D])
        self.hc = {}
        cvals = make_consts()
        for name, dt in CONST_DTYPES.items():
            self.hc[name] = dram(name, list(cvals[name].shape), dt, kind="ExternalInput").ap()

        sb = lambda name, shape, dt: es.enter_context(nc.sbuf_tensor(name, shape, dt))

        self.R = sb("R", [128, NI, T], F32)
        self.XB = sb("XB", [128, NI, T], BF16)
        self.yT = sb("yT", [128, NI, T], BF16)
        self.rows = sb("rows", [128, T], F32)
        self.rows2 = sb("rows2", [128, 512], F32R)
        self.lnb = sb("lnb", [128, 512], F32)
        self.bsc = self.lnb
        self.kcell = sb("kcell", [1, 2], BF16)
        self.nrcell = sb("nrcell", [128, 16], BF16)
        self.bcol = sb("bcol", [128, 24], F32)
        self.bqcol = sb("bqcol", [128, 16], F32)
        self.bkcol = sb("bkcol", [128, 16], F32)
        self.bpcol = sb("bpcol", [128, 8], F32)
        self.slope_bc = sb("slope_bc", [128, 16], F32)
        self.scol = sb("scol", [128, 32], F32)
        self.b24 = sb("b24", [32, 128], F32)
        self.bfcol = sb("bfcol", [128, 8], F32)
        self.pcol = sb("pcol", [128, 4], F32)
        self.qsc = sb("qsc", [128, D // 32], F32)
        self.qscr = sb("qscr", [128, D // 32], F32)
        self.consts = {}
        for name, dt in CONST_DTYPES.items():
            self.consts[name] = sb("s" + name, list(cvals[name].shape), dt)
        ARENA = 56320
        pad = (-nc.sbuf_base) % 32
        base0 = nc.sbuf_base + pad
        self.arena = sb("arena", [128, ARENA + pad], mybir.dt.uint8)
        at = lambda name, shape, dt, off: nc.alloc_sbuf_tensor_at(name, shape, dt, offset=base0 + off)
        self.vg = at("vg", [128, NT, 4 * 65], BF16, 0)
        self.Qh = at("Qh", [128, 2, T], BF16, 8320)
        self.Kh = at("Kh", [128, 2, T], BF16, 16512)
        self.asb = at("asb", [128, 4, 512], BF16, 24704)
        self.lnsq = at("lnsq", [128, T], BF16, 24704)
        self.wqk = at("wqk", [128, 2, 2, NI, 128], BF16, 28800)
        self.wv = at("wv", [128, NI, 256], BF16, 36992)
        self.vbias = at("vbias", [128, D], F32, 41088)
        self.wpj = at("wpj", [128, 2, NI, 128], BF16, 28800)
        self.tload = at("tload", [128, 2, D], F32, 0)
        self.w1s = at("w1s", [128, 2, NI, 256], BF16, 0)
        self.w2s = at("w2s", [128, 2, 16, 128], BF16, 8192)
        self.cbe = at("cbe", [128, 2, 512], F32, 16384)
        self.wrt = at("wrt", [128, NI, 8], F32, 20480)
        self.probs = at("probs", [128, NT, NE], F32, 20736)
        self.ptmp = at("ptmp", [128, NT, NE], F32, 21248)
        self.comb = at("comb", [128, NT, NE], F32, 21760)
        self.b1col = at("b1col", [128, NE, NFI], F32, 22272)
        self.b2col = at("b2col", [128, NE, NI], F32, 23296)
        self.h2 = at("h2", [128, NI, T], BF16, 23552)
        self.wdT = at("wdT", [128, NI, D], BF16, 0)
        self.wdc = nc.dram_tensor("wdcache", [128, NI * D], BF16, kind="Internal").ap()
        self.dscr = nc.dram_tensor("dbcast", [2, 1, 512], F32, kind="Internal").ap()
        self.qnat = at("qnat", [128, 2, D], F32, 16384)
        self.qy = at("qy", [128, D], F32, 24576)
        self.qa = at("qa", [128, D], F32, 28672)
        self.qd = at("qd", [128, D], F32, 32768)
        self.qm2 = at("qm2", [128, D], F32, 36864)
        self.qm4 = at("qm4", [128, D], F32, 40960)
        self.qzi = at("qzi", [128, D], F32, 45056)
        self.qout = at("qout", [128, D], BF16, 32768)
        self.pbA = es.enter_context(nc.psum_tensor("pbA", [128, 2048], F32))
        self.pbB = es.enter_context(nc.psum_tensor("pbB", [128, 2048], F32))
        self.ptb = self.pbB[:, 1536:2048].bitcast(BF16)

        self.plan_consts_load()
        for b in range(self.nbatch):
            self.plan_batch(b)
        fin = [(s, c[1]) for s, c in self.sems.items() if s.startswith("d_") and c[1] > 0]
        self.on("sync", lambda e: e.nop(), waits=fin)

        block = es.enter_context(nc.Block())
        for ename in ENGINES:
            getattr(block, ename)(lambda eng, _e=ename: self.replay(_e, eng))
        return nc

    def in_map(self, x_shard, weights):
        m = {"x_in": np.ascontiguousarray(x_shard, dtype=np.float32)}
        for k in self.hw:
            m[k] = np.ascontiguousarray(np.asarray(weights[k]), dtype=np.float32)
        cv = make_consts()
        for k, dt in CONST_DTYPES.items():
            m[k] = np.ascontiguousarray(cv[k].astype(np_for(dt)))
        return m

    def plan_consts_load(self):
        for name in CONST_DTYPES:
            self.dma("sync", "consts", lambda e, d=self.consts[name], s=self.hc[name]:
                     e.dma_start(d[:], s))
        self.bar()

    def plan_batch(self, b):
        self.plan_prologue(b)
        for L in self.layers:
            if "attn" in self.sublayers:
                self.plan_attn(b, L)
                self.plan_ln(b)
            if "ffn" in self.sublayers:
                if L == 0:
                    self.plan_moe(b)
                else:
                    self.plan_qffn(b)
                self.plan_ln(b, final=(L == self.layers[-1]))
        self.plan_epilogue(b)

    def transpose_f32(self, src_ap_fn, dst_ap_fn, bank, eng):
        idf = self.consts["c_ident_f"]
        self.pb_wait(bank)
        pe = self.on("tensor", lambda e: e.matmul(
            self.pbank(bank, w1=128), src_ap_fn(), idf[:], is_transpose=True,
            start=True, stop=True), inc="spe")
        self.wait(eng, pe)
        if eng == "vector":
            ev = self.on("vector", lambda e: e.tensor_copy(dst_ap_fn(), self.pbank(bank, w1=128)), inc="sv")
        else:
            ev = self.on("scalar", lambda e: e.copy(dst_ap_fn(), self.pbank(bank, w1=128)), inc="ss")
        self.pb_set(bank, ev)
        self._last_tp = pe
        return ev

    def plan_prologue(self, b):
        import os as _os
        _nt = int(_os.environ.get("MK_DEBUG_NT", str(NT)))
        self.wait("sync", self.guards.pop("wdc_store", None))
        guards = {}
        for tau in range(_nt):
            par = tau % 2
            g = guards.get(par)
            if g:
                self.pending["sync"].append(g)
            ld = self.dma("sync", f"x{par}", lambda e, p=par, t=tau: e.dma_start(
                self.tload[:, p, :], self.hx[b, t * 128:(t + 1) * 128, :]))
            self.wait("tensor", ld)
            for i in range(NI):
                bank = i % 2
                eng = "vector" if (i % 4) < 2 else "scalar"
                ev = self.transpose_f32(
                    lambda p=par, i=i: self.tload[:, p, i * 128:(i + 1) * 128],
                    lambda i=i, t=tau: self.R[:, i, t * 128:(t + 1) * 128],
                    bank, eng)
                eng2 = "scalar" if eng == "vector" else "vector"
                self.wait(eng2, ev)
                if eng2 == "vector":
                    self.on("vector", lambda e, i=i, t=tau: e.tensor_copy(
                        self.XB[:, i, t * 128:(t + 1) * 128],
                        self.R[:, i, t * 128:(t + 1) * 128]), inc="sv")
                else:
                    self.on("scalar", lambda e, i=i, t=tau: e.copy(
                        self.XB[:, i, t * 128:(t + 1) * 128],
                        self.R[:, i, t * 128:(t + 1) * 128]), inc="ss")
            guards[par] = self.tok("spe")
        self.bar()

    def plan_epilogue(self, b):
        st_guard = {}
        for tau in range(NT):
            par = tau % 2
            evs = []
            for i in range(NI):
                bank = i % 2
                eng = "vector" if (i % 4) < 2 else "scalar"
                g = st_guard.get(par)
                if g:
                    self.pending[eng].append(g)
                ev = self.transpose_f32(
                    lambda i=i, t=tau: self.R[:, i, t * 128:(t + 1) * 128],
                    lambda p=par, i=i: self.qnat[:, p, i * 128:(i + 1) * 128],
                    bank, eng)
                evs.append(ev)
            self.waits("sync", evs)
            st = self.dma("sync", f"out{par}", lambda e, p=par, t=tau: e.dma_start(
                self.hout[b, t * 128:(t + 1) * 128, :], self.qnat[:, p, :]))
            st_guard[par] = st
        self.bar()

    def bias_cols(self, hvec, n, dst, bank=0):
        nr = n // 128
        assert nr <= 32
        g = self.guards.get("b24")
        if g:
            self.pending["sync"].append(g)
        ld = self.dma("sync", "misc", lambda e, nr=nr: e.dma_start(
            self.b24[0:nr, :], hvec.rearrange("(j p) -> j p", p=128)), self_wait=True)
        self.wait("tensor", ld)
        idf = self.consts["c_ident_f"]
        self.pb_wait(bank)
        pe = self.on("tensor", lambda e, nr=nr: e.matmul(
            self.pbank(bank, w1=nr), self.b24[0:nr, :], idf[0:nr, 0:nr],
            is_transpose=True, start=True, stop=True), inc="spe")
        self.guards["b24"] = pe
        self.wait("vector", pe)
        ev = self.on("vector", lambda e, nr=nr: e.tensor_copy(dst[:, 0:nr], self.pbank(bank, w1=nr)), inc="sv")
        self.pb_set(bank, ev)
        return ev

    def plan_attn(self, b, L):
        alibi = (L == 1)
        W = self.hw["L1_attn_W_kqv" if alibi else "L0_attn_w_attn"]
        Bv = self.hw["L1_attn_b_kqv" if alibi else "L0_attn_b_attn"]
        WP = self.hw["L1_attn_W_proj" if alibi else "L0_attn_w_proj"]
        BP = self.hw["L1_attn_b_proj" if alibi else "L0_attn_b_proj"]
        qoff = D if alibi else 0
        koff = 0 if alibi else D
        voff = 2 * D
        onesr = self.consts["c_onesrow_f"]

        ev = self.bias_cols(Bv, 3 * D, self.bcol, bank=0)
        self.wait("vector", ev)
        for h in range(NH):
            jq, rq = divmod(qoff + 64 * h, 128)
            jk, rk = divmod(koff + 64 * h, 128)
            self.on("vector", lambda e, h=h, jq=jq, rq=rq: e.tensor_scalar(
                self.bqcol[0:64, h:h + 1], self.bcol[rq:rq + 64, jq:jq + 1], QS, None,
                op0=ALU.mult), inc="sv")
            self.on("vector", lambda e, h=h, jk=jk, rk=rk: e.tensor_scalar(
                self.bkcol[0:64, h:h + 1], self.bcol[rk:rk + 64, jk:jk + 1], QS, None,
                op0=ALU.mult), inc="sv")
        self.bias_cols(BP, D, self.bpcol, bank=1)
        ldv = self.dma("sync", "misc", lambda e: e.dma_start(
            self.rows[0:1, 0:D], Bv[voff:voff + D].rearrange("(o f) -> o f", o=1)), self_wait=True)
        self.wait("tensor", ldv)
        for jj in range(8):
            bank = 2 + jj % 2
            self.pb_wait(bank)
            pe = self.on("tensor", lambda e, jj=jj, bank=bank: e.matmul(
                self.pbank(bank, w1=128), onesr[:],
                self.rows[0:1, jj * 128:(jj + 1) * 128],
                start=True, stop=True), inc="spe")
            self.wait("vector", pe)
            evv = self.on("vector", lambda e, jj=jj, bank=bank: e.tensor_copy(
                self.vbias[:, jj * 128:(jj + 1) * 128], self.pbank(bank, w1=128)), inc="sv")
            self.pb_set(bank, evv)
        if alibi:
            self.pending["sync"].append(self.tok("spe"))
            lds = self.dma("sync", "misc", lambda e: e.dma_start(
                self.rows[0:1, 0:16], self.hw["L1_attn_slopes"].rearrange("(o s) -> o s", o=1)),
                self_wait=True)
            self.wait("tensor", lds)
            self.pb_wait(4)
            pe = self.on("tensor", lambda e: e.matmul(
                self.pbank(4, w1=16), onesr[:], self.rows[0:1, 0:16],
                start=True, stop=True), inc="spe")
            self.wait("vector", pe)
            self.on("vector", lambda e: e.tensor_copy(self.slope_bc[:], self.pbank(4, w1=16)), inc="sv")
            e2 = self.on("vector", lambda e: e.tensor_scalar(
                self.nrcell[:], self.pbank(4, w1=16), 1.0, 0.0, op0=ALU.mult, op1=ALU.max), inc="sv")
            self.pb_set(4, e2)
        self.on("vector", lambda e: e.memset(self.Kh[64:65, :], 1.0), inc="sv")
        self.bar()

        _nh = NH
        halfc = self.consts["c_halfcol_b"]
        hctx = {}
        for h in range(_nh + 1):
            if h < _nh:
                p = h % 2
                par = (h // 2) % 2
                hp = h % 2
                if hp == 0:
                    g = self.guards.get(("wqk", par))
                    if g:
                        self.pending["gpsimd"].append(g)
                    self.dma("gpsimd", f"wqk{par}", lambda e, h=h, pq=par: e.dma_start(
                        self.wqk[:, pq, 0, :, :],
                        W[:, qoff + 64 * h: qoff + 64 * (h + 2)].rearrange("(i p) f -> p i f", p=128)))
                    lk = self.dma("gpsimd", f"wqk{par}", lambda e, h=h, pq=par: e.dma_start(
                        self.wqk[:, pq, 1, :, :],
                        W[:, koff + 64 * h: koff + 64 * (h + 2)].rearrange("(i p) f -> p i f", p=128)))
                    self.guards[("wqk_tok", par)] = lk
                lk = self.guards[("wqk_tok", par)]
                if h % 4 == 0:
                    g = self.guards.get("wv")
                    if g:
                        self.pending["gpsimd"].append(g)
                    lv = self.dma("gpsimd", "wv", lambda e, h=h: e.dma_start(
                        self.wv[:],
                        W[:, voff + 64 * h: voff + 64 * (h + 4)].rearrange("(i p) f -> p i f", p=128)))
                qk_evacs = []
                for qk in (0, 1):
                    col = self.bqcol if qk == 0 else self.bkcol
                    dst = self.Qh if qk == 0 else self.Kh
                    for g4 in range(NG):
                        self.pb_wait(g4)
                    self.wait("tensor", lk)
                    for i in range(NI):
                        for g4 in range(NG):
                            self.on("tensor", lambda e, pq=par, qk=qk, i=i, g4=g4, hp=hp: e.matmul(
                                self.pbank(g4, hi=64), self.wqk[:, pq, qk, i, hp * 64:(hp + 1) * 64],
                                self.XB[:, i, g4 * 512:(g4 + 1) * 512],
                                start=(i == 0), stop=(i == NI - 1)),
                                inc="spe" if (i == NI - 1 and g4 == NG - 1) else None)
                    mm = self.tok("spe")
                    self.guards[("wqk", par)] = mm
                    self.wait("vector", mm)
                    for g4 in range(NG):
                        evq = self.on("vector", lambda e, h=h, p=p, g4=g4, dst=dst, col=col: e.tensor_scalar(
                            dst[0:64, p, g4 * 512:(g4 + 1) * 512], self.pbank(g4, hi=64),
                            QS, col[0:64, h:h + 1], op0=ALU.mult, op1=ALU.add), inc="sv")
                        self.pb_set(g4, evq)
                        qk_evacs.append(evq)
            if h > 0:
                hh = h - 1
                c = hctx[hh]
                p1 = c["p"]
                hloc = hh % 4
                self.wait("tensor", self.guards.get("vg_ready"))
                for G in range(NG):
                    yb = 6 + G % 2
                    self.pb_wait(yb)
                    ilist = list(range(4 * G + 3, -1, -1))
                    if alibi:
                        sched = [(idx,) for idx in range(len(ilist))]
                    elif G == 0:
                        sched = [(0,), (1,), (2,), (3,)]
                    else:
                        sched = [(idx,) for idx in range(4)] + [(a, a + 1) for a in range(4, len(ilist), 2)]
                    for grp in sched:
                        abanks = []
                        for gidx, idx in enumerate(grp):
                            i = ilist[idx]
                            diag = i - 4 * G
                            woff = 128 * diag if diag >= 0 else 0
                            wlen = 512 - woff
                            ab = 4 + (idx % 2)
                            abanks.append((idx, i, diag, woff, wlen, ab))
                            self.pb_wait(ab)
                            self.wait("tensor", c["rowtok"])
                            if alibi:
                                self.wait("tensor", c["sc"])
                            self.on("tensor", lambda e, i=i, G=G, ab=ab, woff=woff, wlen=wlen, diag=diag, p1=p1: e.matmul(
                                self.pbank(ab, w0=woff, w1=woff + wlen),
                                self.Kh[0:65, p1, i * 128:(i + 1) * 128],
                                self.Qh[0:65, p1, G * 512 + woff: G * 512 + woff + wlen],
                                start=True, stop=(diag < 0)), inc=None if diag >= 0 else "spe")
                            if diag >= 0:
                                self.on("tensor", lambda e, ab=ab, woff=woff: e.matmul(
                                    self.pbank(ab, w0=woff, w1=woff + 128),
                                    self.consts["c_tri_b"][:], self.consts["c_ident_b"][:],
                                    start=False, stop=True), inc="spe")
                        qkmm = self.tok("spe")
                        if len(grp) == 1:
                            idx, i, diag, woff, wlen, ab = abanks[0]
                            aslot = idx % 4
                            slot_g = self.guards.get(("asb", aslot))
                            if slot_g:
                                self.pending["scalar"].append(slot_g)
                            mstok = None
                            if idx == 0 and woff > 0:
                                if slot_g:
                                    self.pending["vector"].append(slot_g)
                                mstok = self.on("vector", lambda e, aslot=aslot, woff=woff: e.memset(
                                    self.asb[:, aslot, 0:woff], 0.0), inc="sv")
                            self.wait("scalar", qkmm)
                            bias = (lambda ee, i=i, p1=p1: None)
                            if alibi:
                                ex = self.on("scalar", lambda e, i=i, ab=ab, aslot=aslot, woff=woff, wlen=wlen, p1=p1:
                                             e.activation(self.asb[:, aslot, woff:woff + wlen],
                                                          self.pbank(ab, w0=woff, w1=woff + wlen),
                                                          AF.Exp, bias=self.scol[:, p1 * 16 + i:p1 * 16 + i + 1],
                                                          scale=1.0), inc="ss")
                            else:
                                ex = self.on("scalar", lambda e, i=i, ab=ab, aslot=aslot, woff=woff, wlen=wlen:
                                             e.activation(self.asb[:, aslot, woff:woff + wlen],
                                                          self.pbank(ab, w0=woff, w1=woff + wlen),
                                                          AF.Exp, bias=0.0, scale=1.0), inc="ss")
                            self.pb_set(ab, ex)
                            pairs = [(idx, i, woff, wlen, ab, aslot, ex, mstok)]
                        else:
                            (idxA, iA, _, _, _, abA), (idxB, iB, _, _, _, abB) = abanks
                            aslot = idxA % 4
                            assert abA == 4 and abB == 5 and aslot in (0, 2)
                            for s_ in (aslot, aslot + 1):
                                slot_g = self.guards.get(("asb", s_))
                                if slot_g:
                                    self.pending["scalar"].append(slot_g)
                            self.wait("scalar", qkmm)
                            ex = self.on("scalar", lambda e, aslot=aslot:
                                         e.activation(self.asb[:, aslot, :].rearrange("p a b -> p (a b)") if False else self.asb[:, aslot:aslot + 2, :].rearrange("p a b -> p (a b)"),
                                                      self.pbB[:, 0:1024],
                                                      AF.Exp, bias=0.0, scale=1.0), inc="ss")
                            self.pb_set(4, ex)
                            self.pb_set(5, ex)
                            pairs = [(idxA, iA, 0, 512, 4, aslot, ex, None),
                                     (idxB, iB, 0, 512, 5, aslot + 1, ex, None)]
                        for idx, i, woff, wlen, ab, aslot, ex, mstok in pairs:
                            self.wait("tensor", ex)
                            self.wait("tensor", mstok)
                            pw = 0 if idx == 0 else woff
                            pv = self.on("tensor", lambda e, i=i, idx=idx, aslot=aslot, hloc=hloc, pw=pw, yb=yb: e.matmul(
                                self.pbank(yb, hi=65, w0=pw, w1=512),
                                self.vg[:, i, hloc * 65:(hloc + 1) * 65],
                                self.asb[:, aslot, pw:512],
                                start=(idx == 0), stop=(i == 0)), inc="spe")
                            self.guards[("asb", aslot)] = pv
                    pvmm = self.tok("spe")
                    lw = self.guards.get("lnbw")
                    if (hh + G) % 2 == 0:
                        self.wait("scalar", pvmm)
                        self.wait("scalar", lw)
                        rc = self.on("scalar", lambda e, yb=yb: e.activation(
                            self.lnb[64:65, 0:512], self.pbank(yb, lo=64, hi=65),
                            AF.Ln, bias=0.0, scale=1.0), inc="ss")
                        self.on("scalar", lambda e: e.activation(
                            self.lnb[64:65, 0:512], self.lnb[64:65, 0:512],
                            AF.Exp, bias=0.0, scale=-1.0), inc="ss")
                        rce = self.tok("ss")
                    else:
                        self.wait("vector", pvmm)
                        self.wait("vector", lw)
                        rc = self.on("vector", lambda e, yb=yb: e.reciprocal(
                            self.lnb[64:65, 0:512], self.pbank(yb, lo=64, hi=65)), inc="sv")
                        rce = rc
                    self.wait("sync", rce)
                    ot = self.dma("sync", "bcast", lambda e, G=G: e.dma_start(
                        self.dscr[G % 2], self.lnb[64:65, 0:512]), self_wait=True)
                    self.guards["lnbw"] = ot
                    self.wait("sync", self.guards.get("bscw"))
                    it = self.dma("sync", "bcast", lambda e, G=G: e.dma_start(
                        self.bsc[0:64, :], self.dscr[G % 2].broadcast_to((64, 512))))
                    self.wait("vector", it)
                    self.wait("vector", rc)
                    ynorm = self.on("vector", lambda e, hh=hh, G=G, yb=yb: e.tensor_tensor(
                        self.yT[(hh % 2) * 64:(hh % 2) * 64 + 64, hh // 2, G * 512:(G + 1) * 512],
                        self.pbank(yb, hi=64), self.bsc[0:64, :], op=ALU.mult), inc="sv")
                    self.guards["bscw"] = ynorm
                    self.pb_set(yb, ynorm)
            if h < _nh and h % 4 == 0:
                self.plan_vgen(b, lv, first=(h == 0), h0=h)
            if h < _nh:
                self.waits("gpsimd", qk_evacs)
                sk = self.on("gpsimd", lambda e, p=p: e.tensor_tensor(
                    self.Qh[64:128, p, :], self.Kh[0:64, p, :], self.Kh[0:64, p, :], op=ALU.mult), inc="sdw")
                sqq = self.on("gpsimd", lambda e, p=p, h=h: e.tensor_tensor(
                    self.yT[(h % 2) * 64:(h % 2) * 64 + 64, h // 2, :],
                    self.Qh[0:64, p, :], self.Qh[0:64, p, :], op=ALU.mult), inc="sdw")
                self.wait("tensor", sk)
                for g4 in range(NG):
                    self.pb_wait(g4)
                    self.on("tensor", lambda e, g4=g4, p=p: e.matmul(
                        self.pbank(g4, lo=0, hi=1), halfc[64:128, :],
                        self.Qh[64:128, p, g4 * 512:(g4 + 1) * 512], start=True, stop=True),
                        inc="spe" if g4 == NG - 1 else None)
                mmk = self.tok("spe")
                self.wait("vector", mmk)
                rmx = self.on("vector", lambda e, p=p: e.tensor_reduce(
                    self.kcell[0:1, p:p + 1], self.pbA[0:1, :], axis=AX.X, op=ALU.max), inc="sv")
                self.wait("tensor", rmx)
                self.wait("tensor", sqq)
                qb = (h % 2) * 64
                for g4 in range(NG):
                    self.on("tensor", lambda e, g4=g4, h=h, qb=qb: e.matmul(
                        self.pbank(g4, lo=0, hi=1), halfc[qb:qb + 64, :],
                        self.yT[qb:qb + 64, h // 2, g4 * 512:(g4 + 1) * 512], start=True, stop=False))
                    self.on("tensor", lambda e, g4=g4, p=p: e.matmul(
                        self.pbank(g4, lo=0, hi=1), self.kcell[0:1, p:p + 1],
                        self.consts["c_half_row_b"][0:1, 0:512],
                        start=False, stop=(not alibi)),
                        inc=None if alibi else ("spe" if g4 == NG - 1 else None))
                    if alibi:
                        self.on("tensor", lambda e, g4=g4, h=h: e.matmul(
                            self.pbank(g4, lo=0, hi=1), self.nrcell[0:1, h:h + 1],
                            self.consts["c_iota_row_b"][0:1, g4 * 512:(g4 + 1) * 512],
                            start=False, stop=True),
                            inc="spe" if g4 == NG - 1 else None)
                mmq = self.tok("spe")
                self.wait("vector", mmq)
                rowtok = self.on("vector", lambda e, p=p: e.tensor_scalar(
                    self.Qh[64:65, p, :], self.pbA[0:1, :], -1.0, None, op0=ALU.mult), inc="sv")
                for g4 in range(NG):
                    self.pb_set(g4, rowtok)
                sc = None
                if alibi:
                    sc = self.on("vector", lambda e, h=h, p=p: e.tensor_scalar(
                        self.scol[:, p * 16:(p + 1) * 16], self.consts["c_iotacol16"][:],
                        self.slope_bc[:, h:h + 1], None, op0=ALU.mult), inc="sv")
                hctx[h] = {"p": p, "rowtok": rowtok, "sc": sc}
        self.bar()
        for ft in range(NI):
            par = ft % 2
            g = self.guards.get(("wpj", par))
            if g:
                self.pending["gpsimd"].append(g)
            lp = self.dma("gpsimd", f"wpj{par}", lambda e, ft=ft, p=par: e.dma_start(
                self.wpj[:, p, :, :],
                WP[:, ft * 128:(ft + 1) * 128].rearrange("(i p) f -> p i f", p=128)))
            for tg in range(NG):
                self.pb_wait(tg)
            self.wait("tensor", lp)
            for ic in range(NI):
                for tg in range(NG):
                    self.on("tensor", lambda e, ft=ft, p=par, ic=ic, tg=tg: e.matmul(
                        self.pbank(tg), self.wpj[:, p, ic, :],
                        self.yT[:, ic, tg * 512:(tg + 1) * 512],
                        start=(ic == 0), stop=(ic == NI - 1)),
                        inc="spe" if (ic == NI - 1 and tg == NG - 1) else None)
            mm = self.tok("spe")
            self.guards[("wpj", par)] = mm
            self.wait("vector", mm)
            for tg in range(NG):
                evp = self.on("vector", lambda e, ft=ft, tg=tg: e.scalar_tensor_tensor(
                    self.R[:, ft, tg * 512:(tg + 1) * 512], self.pbank(tg),
                    self.bpcol[:, ft:ft + 1], self.R[:, ft, tg * 512:(tg + 1) * 512],
                    op0=ALU.add, op1=ALU.add), inc="sv")
                self.pb_set(tg, evp)
        self.bar()

    def plan_vgen(self, b, lv_tok, first, h0=0):
        if first:
            ms = self.on("vector", lambda e: e.memset(
                self.vg.rearrange("p t (hh c) -> p t hh c", c=65)[:, :, :, 64:65], 1.0), inc="sv")
            self.wait("tensor", ms)
        ev = None
        for tau in range(NT):
            bank = tau % 2
            self.pb_wait(bank)
            self.wait("tensor", lv_tok)
            for i in range(NI):
                self.on("tensor", lambda e, tau=tau, i=i, bank=bank: e.matmul(
                    self.pbank(bank, w1=256), self.XB[:, i, tau * 128:(tau + 1) * 128],
                    self.wv[:, i, 0:256], start=(i == 0), stop=(i == NI - 1)),
                    inc="spe" if i == NI - 1 else None)
            mm = self.tok("spe")
            self.wait("vector", mm)
            ev = self.on("vector", lambda e, tau=tau, bank=bank, h0=h0: e.tensor_tensor(
                self.vg[:, tau, :].rearrange("p (hh c) -> p hh c", c=65)[:, :, 0:64],
                self.pbank(bank, w1=256).rearrange("p (hh c) -> p hh c", c=64),
                self.vbias[:, h0 * 64:(h0 + 4) * 64].rearrange("p (hh c) -> p hh c", c=64),
                op=ALU.add), inc="sv")
            self.pb_set(bank, ev)
        self.guards["vg_ready"] = ev
        self.guards["wv"] = self.tok("spe")

    def plan_ln(self, b, final=False):
        onescb = self.consts["c_onescol_b"]
        onesr = self.consts["c_onesrow_f"]
        self.bar()
        prev_sq = None
        for i in range(NI):
            srcx = self.on("vector", lambda e, i=i: e.tensor_copy(
                self.XB[:, i, :], self.R[:, i, :]), inc="sv")
            if prev_sq:
                self.pending["scalar"].append(prev_sq)
            srcs = self.on("scalar", lambda e, i=i: e.activation(
                self.lnsq[:, :], self.R[:, i, :], AF.Square, bias=0.0, scale=1.0), inc="ss")
            self.wait("tensor", srcx)
            self.wait("tensor", srcs)
            for g4 in range(NG):
                if i == 0:
                    self.pb_wait(g4)
                    self.pb_wait(4 + g4)
                self.on("tensor", lambda e, g4=g4, i=i: e.matmul(
                    self.pbank(g4, lo=0, hi=1), onescb[:],
                    self.XB[:, i, g4 * 512:(g4 + 1) * 512],
                    start=(i == 0), stop=(i == NI - 1)))
                self.on("tensor", lambda e, g4=g4, i=i: e.matmul(
                    self.pbank(4 + g4, lo=0, hi=1), onescb[:],
                    self.lnsq[:, g4 * 512:(g4 + 1) * 512],
                    start=(i == 0), stop=(i == NI - 1)),
                    inc="spe" if g4 == NG - 1 else None)
            prev_sq = self.tok("spe")
        mm = self.tok("spe")
        self.wait("scalar", mm)
        evt0 = self.on("scalar", lambda e: e.activation(
            self.rows[0:1, :], self.pbA[0:1, :], AF.Identity,
            bias=0.0, scale=1.0 / D), inc="ss")
        for g4 in range(NG):
            self.pb_set(g4, evt0)
        self.wait("vector", evt0)
        v1 = self.on("vector", lambda e: e.tensor_tensor(
            self.rows[64:65, :], self.rows[0:1, :], self.rows[0:1, :], op=ALU.mult), inc="sv")
        self.wait("vector", mm)
        v2 = self.on("vector", lambda e: e.scalar_tensor_tensor(
            self.rows[32:33, :], self.pbB[0:1, :], 1.0 / D, self.rows[64:65, :],
            op0=ALU.mult, op1=ALU.subtract), inc="sv")
        for g4 in range(NG):
            self.pb_set(4 + g4, v2)
        mmb = None
        for g4 in range(NG):
            self.wait("scalar", v2)
            self.wait("scalar", mmb)
            self.on("scalar", self.lp(lambda e, g4=g4: e.activation(
                self.rows2[0:1, 0:512], self.rows[32:33, g4 * 512:(g4 + 1) * 512], AF.Ln,
                bias=self.consts["c_epscol"][0:1, 0:1], scale=1.0)), inc="ss")
            s2 = self.on("scalar", self.lp(lambda e: e.activation(
                self.rows2[0:1, 0:512], self.rows2[0:1, 0:512], AF.Exp, bias=0.0, scale=-0.5)), inc="ss")
            self.wait("tensor", s2)
            self.pb_wait(4)
            self.on("tensor", lambda e, g4=g4: e.matmul(
                self.pbank(4), onesr[:],
                self.rows[0:1, g4 * 512:(g4 + 1) * 512],
                start=True, stop=True))
            self.pb_wait(5)
            self.on("tensor", lambda e: e.matmul(
                self.pbank(5), self.consts["c_onesrow_r"][:],
                self.rows2[0:1, 0:512],
                start=True, stop=True), inc="spe")
            mmb = self.tok("spe")
            self.wait("vector", mmb)
            t2 = None
            for i in range(NI):
                self.on("vector", lambda e, i=i, g4=g4: e.tensor_tensor(
                    self.lnb[:], self.R[:, i, g4 * 512:(g4 + 1) * 512], self.pbank(4),
                    op=ALU.subtract), inc="sv")
                t2 = self.on("vector", lambda e, i=i, g4=g4: e.tensor_tensor(
                    self.R[:, i, g4 * 512:(g4 + 1) * 512], self.lnb[:], self.pbank(5),
                    op=ALU.mult), inc="sv")
                if not final:
                    self.wait("scalar", t2)
                    self.on("scalar", lambda e, i=i, g4=g4: e.copy(
                        self.XB[:, i, g4 * 512:(g4 + 1) * 512],
                        self.R[:, i, g4 * 512:(g4 + 1) * 512]), inc="ss")
            self.pb_set(4, t2)
            self.pb_set(5, t2)
        self.bar()

    def plan_moe(self, b):
        Wr = self.hw["L0_ffn_Wr"]; W1 = self.hw["L0_ffn_W1"]; W2 = self.hw["L0_ffn_W2"]
        B1 = self.hw["L0_ffn_b1"]; B2 = self.hw["L0_ffn_b2"]
        idf = self.consts["c_ident_f"]
        selc = self.consts["c_sel"]
        for i in range(NI):
            g = self.guards.get("b24")
            if g:
                self.pending["sync"].append(g)
            ldi = self.dma("sync", "misc", lambda e, i=i: e.dma_start(
                self.b24[0:8, :], Wr[:, i * 128:(i + 1) * 128]), self_wait=True)
            self.wait("tensor", ldi)
            self.pb_wait(0)
            pe = self.on("tensor", lambda e: e.matmul(
                self.pbank(0, w1=8), self.b24[0:8, :], idf[0:8, 0:8],
                is_transpose=True, start=True, stop=True), inc="spe")
            self.guards["b24"] = pe
            self.wait("vector", pe)
            evw = self.on("vector", lambda e, i=i: e.tensor_copy(
                self.wrt[:, i, :], self.pbank(0, lo=0, hi=128, w1=8)), inc="sv")
            self.pb_set(0, evw)
        for e_ in range(NE):
            self.bias_cols(B1[e_, :], HF, self.b1col[:, e_, :], bank=1)
            self.bias_cols(B2[e_, :], D, self.b2col[:, e_, :], bank=2)
        self.bar()
        for tau in range(NT):
            self.pb_wait(3)
            for i in range(NI):
                self.on("tensor", lambda e, tau=tau, i=i: e.matmul(
                    self.pbank(3, w1=8), self.R[:, i, tau * 128:(tau + 1) * 128],
                    self.wrt[:, i, :], start=(i == 0), stop=(i == NI - 1)),
                    inc="spe" if i == NI - 1 else None)
            mm = self.tok("spe")
            self.wait("vector", mm)
            self.on("vector", lambda e, tau=tau: e.tensor_reduce(
                self.pcol[:, 0:1], self.pbank(3, w1=8), axis=AX.X, op=ALU.max), inc="sv")
            v1b = self.on("vector", lambda e: e.tensor_scalar(
                self.pcol[:, 1:2], self.pcol[:, 0:1], -1.0, None, op0=ALU.mult), inc="sv")
            self.wait("scalar", v1b)
            s1 = self.on("scalar", lambda e, tau=tau: e.activation(
                self.ptmp[:, tau, :], self.pbank(3, w1=8), AF.Exp,
                bias=self.pcol[:, 1:2], scale=1.0), inc="ss")
            self.pb_set(3, s1)
            self.wait("vector", s1)
            self.on("vector", lambda e, tau=tau: e.tensor_reduce(
                self.pcol[:, 2:3], self.ptmp[:, tau, :], axis=AX.X, op=ALU.add), inc="sv")
            self.on("vector", lambda e: e.reciprocal(
                self.pcol[:, 3:4], self.pcol[:, 2:3]), inc="sv")
            self.on("vector", lambda e, tau=tau: e.tensor_scalar(
                self.probs[:, tau, :], self.ptmp[:, tau, :], self.pcol[:, 3:4], None,
                op0=ALU.mult), inc="sv")
            self.on("vector", lambda e, tau=tau: e.tensor_reduce(
                self.pcol[:, 0:1], self.probs[:, tau, :], axis=AX.X, op=ALU.max), inc="sv")
            self.on("vector", lambda e, tau=tau: e.tensor_scalar(
                self.ptmp[:, tau, :], self.probs[:, tau, :], self.pcol[:, 0:1], None,
                op0=ALU.subtract), inc="sv")
            self.on("vector", lambda e, tau=tau: e.tensor_scalar(
                self.ptmp[:, tau, :], self.ptmp[:, tau, :], 0.0, None, op0=ALU.is_ge), inc="sv")
            self.on("vector", lambda e, tau=tau: e.scalar_tensor_tensor(
                self.ptmp[:, tau, :], self.ptmp[:, tau, :], -2.0, self.probs[:, tau, :],
                op0=ALU.mult, op1=ALU.add), inc="sv")
            self.on("vector", lambda e, tau=tau: e.tensor_reduce(
                self.pcol[:, 1:2], self.ptmp[:, tau, :], axis=AX.X, op=ALU.max), inc="sv")
            self.on("vector", lambda e, tau=tau: e.tensor_scalar(
                self.ptmp[:, tau, :], self.probs[:, tau, :], self.pcol[:, 1:2], None,
                op0=ALU.subtract), inc="sv")
            self.on("vector", lambda e, tau=tau: e.tensor_scalar(
                self.ptmp[:, tau, :], self.ptmp[:, tau, :], 0.0, None, op0=ALU.is_ge), inc="sv")
            v10 = self.on("vector", lambda e, tau=tau: e.tensor_tensor(
                self.comb[:, tau, :], self.probs[:, tau, :], self.ptmp[:, tau, :],
                op=ALU.mult), inc="sv")
            self.wait("tensor", v10)
            self.pb_wait(3)
            pt = self.on("tensor", lambda e, tau=tau: e.matmul(
                self.pbank(3, lo=0, hi=8, w1=128), self.comb[:, tau, :], idf[:],
                is_transpose=True, start=True, stop=True), inc="spe")
            self.wait("vector", pt)
            evt = self.on("vector", lambda e, tau=tau: e.tensor_copy(
                self.rows[0:8, tau * 128:(tau + 1) * 128], self.pbank(3, lo=0, hi=8, w1=128)), inc="sv")
            self.pb_set(3, evt)
        self.bar()
        yflat = self.yT.rearrange("p a t -> p (a t)")
        h2flat = self.h2.rearrange("p a t -> p (a t)")

        def h_ap(fi, tg):
            base = fi * 1024 + tg * 512
            tens = yflat if base < 16384 else h2flat
            return tens[:, base % 16384: base % 16384 + 512]

        selc = self.consts["c_sel"]
        for th in range(2):
            if self.guards.get("wcache_done") is None and "d_wcache" in self.sems and th == 1:
                self.guards["wcache_done"] = self.tok("d_wcache")
            for e_ in range(NE):
                evc = []
                for tg, bkc in ((0, 3), (1, 6)):
                    self.pb_wait(bkc)
                    mmc = self.on("tensor", lambda e, e_=e_, th=th, tg=tg, bkc=bkc: e.matmul(
                        self.pbank(bkc), selc[:, e_ * 128:(e_ + 1) * 128],
                        self.rows[0:8, th * 1024 + tg * 512: th * 1024 + (tg + 1) * 512],
                        start=True, stop=True), inc="spe")
                    self.wait("vector", mmc)
                    ec = self.on("vector", lambda e, tg=tg, bkc=bkc: e.tensor_copy(
                        self.cbe[:, tg, :], self.pbank(bkc)), inc="sv")
                    self.pb_set(bkc, ec)
                    evc.append(ec)
                for slab in range(16):
                    spar = slab % 2
                    g = self.guards.get(("w1s", spar))
                    if g:
                        self.pending["gpsimd"].append(g)
                        self.pending["sync"].append(g)
                    gst = self.guards.get(("w1st", spar))
                    if gst:
                        self.pending["gpsimd"].append(gst)
                        self.pending["sync"].append(gst)
                    ck = ("w1", e_, slab)
                    if ck not in self.moe_cached:
                        self.moe_cached.add(ck)
                        l1 = self.dma("gpsimd", f"w1s{spar}", lambda e, e_=e_, slab=slab, spar=spar: e.dma_start(
                            self.w1s[:, spar, :, :],
                            W1[e_, :, slab * 256:(slab + 1) * 256].rearrange("(i p) f -> p i f", p=128)))
                        self.wait("sync", l1)
                        st = self.dma("sync", "wcache", lambda e, e_=e_, slab=slab, spar=spar: e.dma_start(
                            self.w1c[e_, :, slab * 256:(slab + 1) * 256].rearrange("(i p) f -> p i f", p=128),
                            self.w1s[:, spar, :, :]))
                        self.guards[("w1st", spar)] = st
                    else:
                        self.wait("sync", self.guards.get("wcache_done"))
                        l1 = self.dma("sync", f"w1r{spar}", lambda e, e_=e_, slab=slab, spar=spar: e.dma_start(
                            self.w1s[:, spar, :, :],
                            self.w1c[e_, :, slab * 256:(slab + 1) * 256].rearrange("(i p) f -> p i f", p=128)))
                        self.guards[("w1st", spar)] = None
                    for f2 in range(2):
                        ft = slab * 2 + f2
                        bks = ((ft % 2) * 2, (ft % 2) * 2 + 1)
                        for tg in range(2):
                            self.pb_wait(bks[tg])
                        self.wait("tensor", l1)
                        for i in range(NI):
                            for tg in range(2):
                                self.on("tensor", lambda e, spar=spar, i=i, f2=f2, th=th, tg=tg, bk=bks[tg]: e.matmul(
                                    self.pbank(bk), self.w1s[:, spar, i, f2 * 128:(f2 + 1) * 128],
                                    self.XB[:, i, th * 1024 + tg * 512: th * 1024 + (tg + 1) * 512],
                                    start=(i == 0), stop=(i == NI - 1)),
                                    inc="spe" if (i == NI - 1 and tg == 1) else None)
                        mm = self.tok("spe")
                        self.guards[("w1s", spar)] = mm
                        self.wait("scalar", mm)
                        for tg in range(2):
                            ge = self.on("scalar", lambda e, e_=e_, ft=ft, tg=tg, bk=bks[tg]: e.activation(
                                h_ap(ft, tg), self.pbank(bk), AF.Gelu,
                                bias=self.b1col[:, e_, ft:ft + 1], scale=1.0), inc="ss")
                            self.pb_set(bks[tg], ge)
                h_ready = self.tok("ss")
                for dt in range(NI):
                    bank0 = 4 if dt % 2 == 0 else 0
                    ltoks = []
                    for fh in range(2):
                        wpar = fh
                        g = self.guards.get(("w2s", wpar))
                        if g:
                            self.pending["gpsimd"].append(g)
                            self.pending["sync"].append(g)
                        gst = self.guards.get(("w2st", wpar))
                        if gst:
                            self.pending["gpsimd"].append(gst)
                            self.pending["sync"].append(gst)
                        ck = ("w2", e_, dt, fh)
                        if ck not in self.moe_cached:
                            self.moe_cached.add(ck)
                            l2 = self.dma("gpsimd", f"w2s{wpar}", lambda e, e_=e_, dt=dt, fh=fh, wpar=wpar: e.dma_start(
                                self.w2s[:, wpar, :, :],
                                W2[e_, fh * 2048:(fh + 1) * 2048, dt * 128:(dt + 1) * 128]
                                .rearrange("(fc p) d -> p fc d", p=128)))
                            self.wait("sync", l2)
                            st = self.dma("sync", "wcache", lambda e, e_=e_, dt=dt, fh=fh, wpar=wpar: e.dma_start(
                                self.w2c[e_, fh * 2048:(fh + 1) * 2048, dt * 128:(dt + 1) * 128]
                                .rearrange("(fc p) d -> p fc d", p=128),
                                self.w2s[:, wpar, :, :]))
                            self.guards[("w2st", wpar)] = st
                        else:
                            self.wait("sync", self.guards.get("wcache_done"))
                            l2 = self.dma("sync", f"w2r{wpar}", lambda e, e_=e_, dt=dt, fh=fh, wpar=wpar: e.dma_start(
                                self.w2s[:, wpar, :, :],
                                self.w2c[e_, fh * 2048:(fh + 1) * 2048, dt * 128:(dt + 1) * 128]
                                .rearrange("(fc p) d -> p fc d", p=128)))
                            self.guards[("w2st", wpar)] = None
                        ltoks.append((l2, wpar))
                    for tg in range(2):
                        self.pb_wait(bank0 + tg)
                    self.wait("tensor", h_ready)
                    for fh in range(2):
                        l2, wpar = ltoks[fh]
                        self.wait("tensor", l2)
                        for fc in range(16):
                            for tg in range(2):
                                self.on("tensor", lambda e, wpar=wpar, fh=fh, fc=fc, tg=tg, bk=bank0 + tg, th=th: e.matmul(
                                    self.pbank(bk), self.w2s[:, wpar, fc, :],
                                    h_ap(fh * 16 + fc, tg), start=(fh == 0 and fc == 0),
                                    stop=(fh == 1 and fc == 15)),
                                    inc="spe" if (fc == 15 and tg == 1) else None)
                        self.guards[("w2s", wpar)] = self.tok("spe")
                    mm = self.tok("spe")
                    self.wait("vector", mm)
                    self.waits("vector", evc)
                    for tg in range(2):
                        self.on("vector", lambda e, e_=e_, dt=dt, tg=tg, bk=bank0 + tg: e.scalar_tensor_tensor(
                            self.lnb[:], self.pbank(bk), self.b2col[:, e_, dt:dt + 1],
                            self.cbe[:, tg, :], op0=ALU.add, op1=ALU.mult), inc="sv")
                        ts2 = self.on("vector", lambda e, dt=dt, tg=tg, th=th: e.tensor_tensor(
                            self.R[:, dt, th * 1024 + tg * 512: th * 1024 + (tg + 1) * 512],
                            self.R[:, dt, th * 1024 + tg * 512: th * 1024 + (tg + 1) * 512],
                            self.lnb[:], op=ALU.add), inc="sv")
                        self.pb_set(bank0 + tg, ts2)
        self.bar()

    def plan_qffn(self, b):
        WQ = self.hw["L1_ffn_weight"]; BQ = self.hw["L1_ffn_bias"]
        self.bias_cols(BQ, D, self.bfcol, bank=0)
        self.bar()
        if b > 0:
            ld = self.dma("sync", "misc", lambda e: e.dma_start(
                self.wdT[:, :, :].rearrange("p a b -> p (a b)"), self.wdc), self_wait=True)
            self.guards["wdt_ld"] = ld
        for r in (range(NI) if b == 0 else ()):
            g = self.guards.get("qnat0")
            if g:
                self.pending["sync"].append(g)
            ld = self.dma("sync", "misc", lambda e, r=r: e.dma_start(
                self.qnat[:, 0, :], WQ[r * 128:(r + 1) * 128, :]), self_wait=True)
            qtok = self.plan_quant_tile(lambda: self.qnat[:, 0, :], ld)
            self.guards["qnat0"] = qtok
            for i in range(NI):
                self.pb_wait(7)
                self.wait("tensor", qtok)
                pe = self.on("tensor", lambda e, i=i: e.matmul(
                    self.ptb[:, 0:128], self.qout[:, i * 128:(i + 1) * 128],
                    self.consts["c_ident_b"][:],
                    is_transpose=True, start=True, stop=True), inc="spe")
                self.wait("vector", pe)
                ev = self.on("vector", lambda e, i=i, r=r: e.tensor_copy(
                    self.wdT[:, i, r * 128:(r + 1) * 128], self.ptb[:, 0:128]), inc="sv")
                self.pb_set(7, ev)
            self.pending["vector"].append(self.tok("spe"))
        if b == 0:
            self.pending["sync"].append(self.tok("sv"))
            st = self.dma("sync", "misc", lambda e: e.dma_start(
                self.wdc, self.wdT[:, :, :].rearrange("p a b -> p (a b)")), self_wait=True)
            self.guards["wdc_store"] = st
        self.bar()
        for tgc in range(NG):
            for tau in range(4 * tgc, 4 * tgc + 4):
                par = tau % 2
                evs = []
                for i in range(NI):
                    bank = 4 + i % 2
                    eng = "vector" if i % 2 == 0 else "scalar"
                    g = self.guards.get(("qnat", par))
                    if g:
                        self.pending[eng].append(g)
                    ev = self.transpose_f32(
                        lambda i=i, t=tau: self.R[:, i, t * 128:(t + 1) * 128],
                        lambda p=par, i=i: self.qnat[:, p, i * 128:(i + 1) * 128],
                        bank, eng)
                    evs.append(ev)
                qtok = self.plan_quant_tile(lambda p=par: self.qnat[:, p, :], None, pre=evs)
                self.guards[("qnat", par)] = qtok
                for i in range(NI):
                    self.pb_wait(7)
                    self.wait("tensor", qtok)
                    pe = self.on("tensor", lambda e, i=i: e.matmul(
                        self.ptb[:, 0:128], self.qout[:, i * 128:(i + 1) * 128],
                        self.consts["c_ident_b"][:],
                        is_transpose=True, start=True, stop=True), inc="spe")
                    self.wait("vector", pe)
                    ev = self.on("vector", lambda e, i=i, t=tau: e.tensor_copy(
                        self.XB[:, i, t * 128:(t + 1) * 128], self.ptb[:, 0:128]), inc="sv")
                    self.pb_set(7, ev)
                self.pending["vector"].append(self.tok("spe"))
            xbtok = self.tok("sv")
            for mt in range(NI):
                bank = mt % 2
                self.pb_wait(bank)
                self.wait("tensor", xbtok)
                self.wait("tensor", self.guards.pop("wdt_ld", None))
                for i in range(NI):
                    self.on("tensor", lambda e, mt=mt, i=i, tgc=tgc: e.matmul(
                        self.pbank(mt % 2),
                        self.wdT[:, i, mt * 128:(mt + 1) * 128],
                        self.XB[:, i, tgc * 512:(tgc + 1) * 512],
                        start=(i == 0), stop=(i == NI - 1)),
                        inc="spe" if i == NI - 1 else None)
                mm = self.tok("spe")
                self.wait("vector", mm)
                ev = self.on("vector", lambda e, mt=mt, tgc=tgc: e.scalar_tensor_tensor(
                    self.R[:, mt, tgc * 512:(tgc + 1) * 512], self.pbank(mt % 2),
                    self.bfcol[:, mt:mt + 1], self.R[:, mt, tgc * 512:(tgc + 1) * 512],
                    op0=ALU.add, op1=ALU.add), inc="sv")
                self.pb_set(mt % 2, ev)
        self.bar()

    def plan_quant_tile(self, src_fn, ld_tok, pre=()):
        if ld_tok is not None:
            self.pending["vector"].append(ld_tok)
        for t in pre:
            self.wait("vector", t)
        self.on("vector", lambda e: e.tensor_reduce(
            self.qsc[:], src_fn().rearrange("p (blk c) -> p blk c", c=32),
            axis=AX.X, op=ALU.max, apply_absolute_value=True), inc="sv")
        self.on("vector", lambda e: e.tensor_scalar(
            self.qsc[:], self.qsc[:], 1.0 / 7.5, 1e-30, op0=ALU.mult, op1=ALU.max), inc="sv")
        self.on("vector", lambda e: e.reciprocal(self.qscr[:], self.qsc[:]), inc="sv")
        self.on("vector", lambda e: e.tensor_tensor(
            self.qy[:].rearrange("p (blk c) -> p blk c", c=32),
            src_fn().rearrange("p (blk c) -> p blk c", c=32),
            self.qscr[:].unsqueeze(2).broadcast_to((128, 32, 32)),
            op=ALU.mult), inc="sv")
        ytok = self.tok("sv")
        self.wait("scalar", ytok)
        atok = self.on("scalar", lambda e: e.activation(
            self.qa[:], self.qy[:], AF.Abs, bias=0.0, scale=1.0), inc="ss")
        self.wait("vector", atok)
        self.on("vector", lambda e: e.tensor_scalar(
            self.qm2[:], self.qa[:], 2.0, -4.0, op0=ALU.is_ge, op1=ALU.mult), inc="sv")
        self.on("vector", lambda e: e.tensor_scalar(
            self.qm4[:], self.qa[:], 4.0, -2.0, op0=ALU.is_ge, op1=ALU.mult), inc="sv")
        self.on("vector", lambda e: e.scalar_tensor_tensor(
            self.qd[:], self.qm2[:], 8.0, self.qm4[:],
            op0=ALU.add, op1=ALU.add), inc="sv")
        self.on("vector", lambda e: e.tensor_tensor(
            self.qzi[:], self.qy[:], self.qd[:], op=ALU.mult), inc="sv")
        self.on("vector", lambda e: e.tensor_scalar(
            self.qzi[:], self.qzi[:], 12582912.0, -12582912.0, op0=ALU.add, op1=ALU.add), inc="sv")
        self.on("vector", lambda e: e.tensor_scalar(
            self.qa[:], self.qm2[:], -0.03125, 0.125, op0=ALU.mult, op1=ALU.add), inc="sv")
        self.on("vector", lambda e: e.scalar_tensor_tensor(
            self.qa[:], self.qm4[:], -0.125, self.qa[:],
            op0=ALU.mult, op1=ALU.add), inc="sv")
        self.on("vector", lambda e: e.tensor_tensor(
            self.qy[:], self.qzi[:], self.qa[:], op=ALU.mult), inc="sv")
        q2 = self.on("vector", lambda e: e.tensor_tensor(
            self.qout[:].rearrange("p (blk c) -> p blk c", c=32),
            self.qy[:].rearrange("p (blk c) -> p blk c", c=32),
            self.qsc[:].unsqueeze(2).broadcast_to((128, 32, 32)),
            op=ALU.mult), inc="sv")
        return q2


def kernel(**inputs):
    from concourse.bass_utils import run_bass_kernel_spmd
    x = np.ascontiguousarray(np.asarray(inputs["L0_attn_x"], dtype=np.float32))
    mk = MK(nbatch=2, layers=(0, 1), sublayers=("attn", "ffn"))
    nc = bass.Bass("TRN2", target_bir_lowering=False)
    es = ExitStack()
    mk.build(nc, es)
    in_maps = [mk.in_map(x[2 * c:2 * c + 2], inputs) for c in range(8)]
    res = run_bass_kernel_spmd(nc, in_maps, core_ids=list(range(8)))
    out = np.concatenate([np.asarray(r["x_out"]) for r in res.results], axis=0)
    return out.astype(np.float32)


# version 11
# speedup vs baseline: 1.0414x; 1.0414x over previous
import numpy as np
from contextlib import ExitStack
import concourse.bass as bass
import concourse.mybir as mybir

F32 = mybir.dt.float32
BF16 = mybir.dt.bfloat16
F32R = mybir.dt.float32r
I32 = mybir.dt.int32
AF = mybir.ActivationFunctionType
ALU = mybir.AluOpType
AX = mybir.AxisListType

ENGINES = ("tensor", "vector", "scalar", "sync", "gpsimd")

D = 1024
T = 2048
NH = 16
DH = 64
NI = 8
NT = 16
NG = 4
NE = 8
HF = 4096
NFI = 32
NEG_BIG = -1.0e30
QS = 0.35355339059327373
LN_EPS = 1e-5


def make_consts():
    c = {}
    c["c_ident_f"] = np.eye(128, dtype=np.float32)
    c["c_ident_b"] = np.eye(128, dtype=np.float32)
    c["c_tri_b"] = np.triu(np.ones((128, 128), np.float32), 1) * NEG_BIG
    c["c_onesrow_f"] = np.ones((1, 128), dtype=np.float32)
    c["c_onesrow_r"] = np.ones((1, 128), dtype=np.float32)
    c["c_halfcol_b"] = np.full((128, 1), 0.5, dtype=np.float32)
    c["c_onescol_b"] = np.ones((128, 1), dtype=np.float32)
    c["c_iota_row_b"] = np.arange(T, dtype=np.float32)[None, :]
    c["c_half_row_b"] = np.full((1, 512), 0.5, dtype=np.float32)
    c["c_iotacol16"] = (np.arange(16)[None, :] * 128 + np.arange(128)[:, None]).astype(np.float32)
    c["c_epscol"] = np.full((128, 1), LN_EPS, dtype=np.float32)
    sel = np.zeros((8, 8 * 128), dtype=np.float32)
    for e in range(8):
        sel[e, e * 128:(e + 1) * 128] = 1.0
    c["c_sel"] = sel
    return c


CONST_DTYPES = {
    "c_ident_f": F32, "c_ident_b": BF16, "c_tri_b": BF16,
    "c_onesrow_f": F32, "c_onesrow_r": F32R, "c_halfcol_b": BF16, "c_onescol_b": BF16,
    "c_iota_row_b": BF16, "c_half_row_b": BF16,
    "c_iotacol16": F32, "c_sel": F32, "c_epscol": F32,
}


def np_for(dt):
    import ml_dtypes
    return {F32: np.float32, BF16: ml_dtypes.bfloat16, F32R: np.float32,
            I32: np.int32}[dt]


class MK:
    def __init__(self, nbatch=2, layers=(0, 1), sublayers=("attn", "ffn")):
        self.nbatch = nbatch
        self.layers = tuple(layers)
        self.sublayers = tuple(sublayers)

    def sem(self, name):
        if name not in self.sems:
            h = self.es.enter_context(self.nc.semaphore(name))
            self.sems[name] = [h, 0]
        return self.sems[name]

    def on(self, eng, fn, inc=None, n=1, waits=()):
        w = self.pending[eng] + list(waits)
        self.pending[eng] = []
        if eng in ("vector", "scalar"):
            own = self.OWN[eng]
            c = self.sem(own)[1]
            if c > 0:
                w.append((own, c))
            if inc is None:
                inc = own
        lw = self.lastwait[eng]
        best = {}
        for t in w:
            if t is None:
                continue
            s, v = t
            if v > best.get(s, -1):
                best[s] = v
        w2 = []
        for s, v in best.items():
            if v > lw.get(s, -1) and v > 0:
                w2.append((s, v))
                lw[s] = v
        tok = None
        if inc is not None:
            c = self.sem(inc)
            c[1] += n
            tok = (inc, c[1])
        self.ops[eng].append((tuple(w2), fn, inc, n))
        self.nops += 1
        return tok

    def tok(self, sem):
        return (sem, self.sems[sem][1])

    def dma(self, queue, stream, fn, self_wait=False):
        tok = self.on(queue, fn, inc="d_" + stream, n=16)
        if self_wait:
            self.on(queue, lambda e: e.nop(), waits=[tok])
        return tok

    def flush_ynorm(self):
        for it, rc, hh, G, yb in self._defer:
            ynorm = self.on("vector", lambda e, hh=hh, G=G, yb=yb: e.tensor_tensor(
                self.yT[(hh % 2) * 64:(hh % 2) * 64 + 64, hh // 2, G * 512:(G + 1) * 512],
                self.pbank(yb, hi=64), self.bsc[0:64, :], op=ALU.mult),
                inc="sv", waits=[t for t in (it, rc) if t is not None])
            self.guards["bscw"] = ynorm
            self.pb_set(yb, ynorm)
        self._defer = []

    def lp(self, fn):
        def wrapped(e):
            with self.nc.allow_low_precision(reason="f32r row for broadcast matmul"):
                return fn(e)
        return wrapped

    def wait(self, eng, tok):
        if tok is not None:
            self.pending[eng].append(tok)

    def waits(self, eng, toks):
        for t in toks:
            self.wait(eng, t)

    OWN = {"tensor": "spe", "vector": "sv", "scalar": "ss", "sync": "sio", "gpsimd": "sdw"}

    def bar(self, engines=ENGINES):
        snap = [(s, c[1]) for s, c in self.sems.items() if c[1] > 0]
        for e in engines:
            for s, v in snap:
                self.pending[e].append((s, v))

    def replay(self, ename, eng):
        sems = self.sems
        for w, fn, inc, n in self.ops[ename]:
            for s, v in w:
                eng.wait_ge(sems[s][0], v)
            instr = fn(eng)
            if inc is not None:
                instr.then_inc(sems[inc][0], n)

    def pbank(self, k, lo=0, hi=128, w0=0, w1=512):
        if k < 4:
            return self.pbA[lo:hi, k * 512 + w0: k * 512 + w1]
        if k < 8:
            return self.pbB[lo:hi, (k - 4) * 512 + w0: (k - 4) * 512 + w1]
        raise KeyError(k)

    def pb_wait(self, k):
        g = self.bank_guard.get(k)
        if g:
            if isinstance(g[0], str):
                self.pending["tensor"].append(g)
            else:
                for t in g:
                    self.pending["tensor"].append(t)

    def pb_set(self, k, tok):
        self.bank_guard[k] = tok

    def build(self, nc, es):
        self.nc = nc
        self.es = es
        self.ops = {e: [] for e in ENGINES}
        self.pending = {e: [] for e in ENGINES}
        self._defer = []
        self.lastwait = {e: {} for e in ENGINES}
        self.sems = {}
        self.bank_guard = {}
        self.guards = {}
        self.nops = 0
        for s in ("spe", "sv", "ss", "sio", "sdw"):
            self.sem(s)

        dram = nc.dram_tensor
        self.moe_cached = set()
        self.w1c = dram("w1cache", [NE, D, HF], BF16, kind="Internal").ap()
        self.w2c = dram("w2cache", [NE, HF, D], BF16, kind="Internal").ap()
        self.hx = dram("x_in", [self.nbatch, T, D], F32, kind="ExternalInput").ap()
        self.hout = dram("x_out", [self.nbatch, T, D], F32, kind="ExternalOutput").ap()
        self.hw = {}

        def win(name, shape):
            self.hw[name] = dram(name, shape, F32, kind="ExternalInput").ap()
        if 0 in self.layers:
            if "attn" in self.sublayers:
                win("L0_attn_w_attn", [D, 3 * D]); win("L0_attn_b_attn", [3 * D])
                win("L0_attn_w_proj", [D, D]); win("L0_attn_b_proj", [D])
            if "ffn" in self.sublayers:
                win("L0_ffn_Wr", [NE, D]); win("L0_ffn_W1", [NE, D, HF])
                win("L0_ffn_b1", [NE, HF]); win("L0_ffn_W2", [NE, HF, D])
                win("L0_ffn_b2", [NE, D])
        if 1 in self.layers:
            if "attn" in self.sublayers:
                win("L1_attn_W_kqv", [D, 3 * D]); win("L1_attn_b_kqv", [3 * D])
                win("L1_attn_W_proj", [D, D]); win("L1_attn_b_proj", [D])
                win("L1_attn_slopes", [NH])
            if "ffn" in self.sublayers:
                win("L1_ffn_weight", [D, D]); win("L1_ffn_bias", [D])
        self.hc = {}
        cvals = make_consts()
        for name, dt in CONST_DTYPES.items():
            self.hc[name] = dram(name, list(cvals[name].shape), dt, kind="ExternalInput").ap()

        sb = lambda name, shape, dt: es.enter_context(nc.sbuf_tensor(name, shape, dt))

        self.R = sb("R", [128, NI, T], F32)
        self.XB = sb("XB", [128, NI, T], BF16)
        self.yT = sb("yT", [128, NI, T], BF16)
        self.rows = sb("rows", [128, T], F32)
        self.rows2 = sb("rows2", [128, 512], F32R)
        self.lnb = sb("lnb", [128, 512], F32)
        self.bsc = self.lnb
        self.kcell = sb("kcell", [1, 2], BF16)
        self.nrcell = sb("nrcell", [128, 16], BF16)
        self.bcol = sb("bcol", [128, 24], F32)
        self.bqcol = sb("bqcol", [128, 16], F32)
        self.bkcol = sb("bkcol", [128, 16], F32)
        self.bpcol = sb("bpcol", [128, 8], F32)
        self.slope_bc = sb("slope_bc", [128, 16], F32)
        self.scol = sb("scol", [128, 32], F32)
        self.b24 = sb("b24", [32, 128], F32)
        self.bfcol = sb("bfcol", [128, 8], F32)
        self.pcol = sb("pcol", [128, 4], F32)
        self.qsc = sb("qsc", [128, D // 32], F32)
        self.qscr = sb("qscr", [128, D // 32], F32)
        self.consts = {}
        for name, dt in CONST_DTYPES.items():
            self.consts[name] = sb("s" + name, list(cvals[name].shape), dt)
        ARENA = 56320
        pad = (-nc.sbuf_base) % 32
        base0 = nc.sbuf_base + pad
        self.arena = sb("arena", [128, ARENA + pad], mybir.dt.uint8)
        at = lambda name, shape, dt, off: nc.alloc_sbuf_tensor_at(name, shape, dt, offset=base0 + off)
        self.vg = at("vg", [128, NT, 4 * 65], BF16, 0)
        self.Qh = at("Qh", [128, 2, T], BF16, 8320)
        self.Kh = at("Kh", [128, 2, T], BF16, 16512)
        self.asb = at("asb", [128, 4, 512], BF16, 24704)
        self.lnsq = at("lnsq", [128, T], BF16, 24704)
        self.wqk = at("wqk", [128, 2, 2, NI, 128], BF16, 28800)
        self.wv = at("wv", [128, NI, 256], BF16, 36992)
        self.vbias = at("vbias", [128, D], F32, 41088)
        self.wpj = at("wpj", [128, 2, NI, 128], BF16, 28800)
        self.tload = at("tload", [128, 2, D], F32, 0)
        self.w1s = at("w1s", [128, 2, NI, 256], BF16, 0)
        self.w2s = at("w2s", [128, 2, 16, 128], BF16, 8192)
        self.cbe = at("cbe", [128, 2, 512], F32, 16384)
        self.wrt = at("wrt", [128, NI, 8], F32, 20480)
        self.probs = at("probs", [128, NT, NE], F32, 20736)
        self.ptmp = at("ptmp", [128, NT, NE], F32, 21248)
        self.comb = at("comb", [128, NT, NE], F32, 21760)
        self.b1col = at("b1col", [128, NE, NFI], F32, 22272)
        self.b2col = at("b2col", [128, NE, NI], F32, 23296)
        self.h2 = at("h2", [128, NI, T], BF16, 23552)
        self.wdT = at("wdT", [128, NI, D], BF16, 0)
        self.wdc = nc.dram_tensor("wdcache", [128, NI * D], BF16, kind="Internal").ap()
        self.dscr = nc.dram_tensor("dbcast", [2, 1, 512], F32, kind="Internal").ap()
        self.qnat = at("qnat", [128, 2, D], F32, 16384)
        self.qy = at("qy", [128, D], F32, 24576)
        self.qa = at("qa", [128, D], F32, 28672)
        self.qd = at("qd", [128, D], F32, 32768)
        self.qm2 = at("qm2", [128, D], F32, 36864)
        self.qm4 = at("qm4", [128, D], F32, 40960)
        self.qzi = at("qzi", [128, D], F32, 45056)
        self.qout = at("qout", [128, D], BF16, 32768)
        self.pbA = es.enter_context(nc.psum_tensor("pbA", [128, 2048], F32))
        self.pbB = es.enter_context(nc.psum_tensor("pbB", [128, 2048], F32))
        self.ptb = self.pbB[:, 1536:2048].bitcast(BF16)

        self.plan_consts_load()
        for b in range(self.nbatch):
            self.plan_batch(b)
        fin = [(s, c[1]) for s, c in self.sems.items() if s.startswith("d_") and c[1] > 0]
        self.on("sync", lambda e: e.nop(), waits=fin)

        block = es.enter_context(nc.Block())
        for ename in ENGINES:
            getattr(block, ename)(lambda eng, _e=ename: self.replay(_e, eng))
        return nc

    def in_map(self, x_shard, weights):
        m = {"x_in": np.ascontiguousarray(x_shard, dtype=np.float32)}
        for k in self.hw:
            m[k] = np.ascontiguousarray(np.asarray(weights[k]), dtype=np.float32)
        cv = make_consts()
        for k, dt in CONST_DTYPES.items():
            m[k] = np.ascontiguousarray(cv[k].astype(np_for(dt)))
        return m

    def plan_consts_load(self):
        for name in CONST_DTYPES:
            self.dma("sync", "consts", lambda e, d=self.consts[name], s=self.hc[name]:
                     e.dma_start(d[:], s))
        self.bar()

    def plan_batch(self, b):
        self.plan_prologue(b)
        for L in self.layers:
            if "attn" in self.sublayers:
                self.plan_attn(b, L)
                self.plan_ln(b)
            if "ffn" in self.sublayers:
                if L == 0:
                    self.plan_moe(b)
                else:
                    self.plan_qffn(b)
                self.plan_ln(b, final=(L == self.layers[-1]))
        self.plan_epilogue(b)

    def transpose_f32(self, src_ap_fn, dst_ap_fn, bank, eng):
        idf = self.consts["c_ident_f"]
        self.pb_wait(bank)
        pe = self.on("tensor", lambda e: e.matmul(
            self.pbank(bank, w1=128), src_ap_fn(), idf[:], is_transpose=True,
            start=True, stop=True), inc="spe")
        self.wait(eng, pe)
        if eng == "vector":
            ev = self.on("vector", lambda e: e.tensor_copy(dst_ap_fn(), self.pbank(bank, w1=128)), inc="sv")
        else:
            ev = self.on("scalar", lambda e: e.copy(dst_ap_fn(), self.pbank(bank, w1=128)), inc="ss")
        self.pb_set(bank, ev)
        self._last_tp = pe
        return ev

    def plan_prologue(self, b):
        import os as _os
        _nt = int(_os.environ.get("MK_DEBUG_NT", str(NT)))
        self.wait("sync", self.guards.pop("wdc_store", None))
        guards = {}
        for tau in range(_nt):
            par = tau % 2
            g = guards.get(par)
            if g:
                self.pending["sync"].append(g)
            ld = self.dma("sync", f"x{par}", lambda e, p=par, t=tau: e.dma_start(
                self.tload[:, p, :], self.hx[b, t * 128:(t + 1) * 128, :]))
            self.wait("tensor", ld)
            for i in range(NI):
                bank = i % 2
                eng = "vector" if (i % 4) < 2 else "scalar"
                ev = self.transpose_f32(
                    lambda p=par, i=i: self.tload[:, p, i * 128:(i + 1) * 128],
                    lambda i=i, t=tau: self.R[:, i, t * 128:(t + 1) * 128],
                    bank, eng)
                eng2 = "scalar" if eng == "vector" else "vector"
                self.wait(eng2, ev)
                if eng2 == "vector":
                    self.on("vector", lambda e, i=i, t=tau: e.tensor_copy(
                        self.XB[:, i, t * 128:(t + 1) * 128],
                        self.R[:, i, t * 128:(t + 1) * 128]), inc="sv")
                else:
                    self.on("scalar", lambda e, i=i, t=tau: e.copy(
                        self.XB[:, i, t * 128:(t + 1) * 128],
                        self.R[:, i, t * 128:(t + 1) * 128]), inc="ss")
            guards[par] = self.tok("spe")
        self.bar()

    def plan_epilogue(self, b):
        st_guard = {}
        for tau in range(NT):
            par = tau % 2
            evs = []
            for i in range(NI):
                bank = i % 2
                eng = "vector" if (i % 4) < 2 else "scalar"
                g = st_guard.get(par)
                if g:
                    self.pending[eng].append(g)
                ev = self.transpose_f32(
                    lambda i=i, t=tau: self.R[:, i, t * 128:(t + 1) * 128],
                    lambda p=par, i=i: self.qnat[:, p, i * 128:(i + 1) * 128],
                    bank, eng)
                evs.append(ev)
            self.waits("sync", evs)
            st = self.dma("sync", f"out{par}", lambda e, p=par, t=tau: e.dma_start(
                self.hout[b, t * 128:(t + 1) * 128, :], self.qnat[:, p, :]))
            st_guard[par] = st
        self.bar()

    def bias_cols(self, hvec, n, dst, bank=0):
        nr = n // 128
        assert nr <= 32
        g = self.guards.get("b24")
        if g:
            self.pending["sync"].append(g)
        ld = self.dma("sync", "misc", lambda e, nr=nr: e.dma_start(
            self.b24[0:nr, :], hvec.rearrange("(j p) -> j p", p=128)), self_wait=True)
        self.wait("tensor", ld)
        idf = self.consts["c_ident_f"]
        self.pb_wait(bank)
        pe = self.on("tensor", lambda e, nr=nr: e.matmul(
            self.pbank(bank, w1=nr), self.b24[0:nr, :], idf[0:nr, 0:nr],
            is_transpose=True, start=True, stop=True), inc="spe")
        self.guards["b24"] = pe
        self.wait("vector", pe)
        ev = self.on("vector", lambda e, nr=nr: e.tensor_copy(dst[:, 0:nr], self.pbank(bank, w1=nr)), inc="sv")
        self.pb_set(bank, ev)
        return ev

    def plan_attn(self, b, L):
        alibi = (L == 1)
        W = self.hw["L1_attn_W_kqv" if alibi else "L0_attn_w_attn"]
        Bv = self.hw["L1_attn_b_kqv" if alibi else "L0_attn_b_attn"]
        WP = self.hw["L1_attn_W_proj" if alibi else "L0_attn_w_proj"]
        BP = self.hw["L1_attn_b_proj" if alibi else "L0_attn_b_proj"]
        qoff = D if alibi else 0
        koff = 0 if alibi else D
        voff = 2 * D
        onesr = self.consts["c_onesrow_f"]

        ev = self.bias_cols(Bv, 3 * D, self.bcol, bank=0)
        self.wait("vector", ev)
        for h in range(NH):
            jq, rq = divmod(qoff + 64 * h, 128)
            jk, rk = divmod(koff + 64 * h, 128)
            self.on("vector", lambda e, h=h, jq=jq, rq=rq: e.tensor_scalar(
                self.bqcol[0:64, h:h + 1], self.bcol[rq:rq + 64, jq:jq + 1], QS, None,
                op0=ALU.mult), inc="sv")
            self.on("vector", lambda e, h=h, jk=jk, rk=rk: e.tensor_scalar(
                self.bkcol[0:64, h:h + 1], self.bcol[rk:rk + 64, jk:jk + 1], QS, None,
                op0=ALU.mult), inc="sv")
        self.bias_cols(BP, D, self.bpcol, bank=1)
        ldv = self.dma("sync", "misc", lambda e: e.dma_start(
            self.rows[0:1, 0:D], Bv[voff:voff + D].rearrange("(o f) -> o f", o=1)), self_wait=True)
        self.wait("tensor", ldv)
        for jj in range(8):
            bank = 2 + jj % 2
            self.pb_wait(bank)
            pe = self.on("tensor", lambda e, jj=jj, bank=bank: e.matmul(
                self.pbank(bank, w1=128), onesr[:],
                self.rows[0:1, jj * 128:(jj + 1) * 128],
                start=True, stop=True), inc="spe")
            self.wait("vector", pe)
            evv = self.on("vector", lambda e, jj=jj, bank=bank: e.tensor_copy(
                self.vbias[:, jj * 128:(jj + 1) * 128], self.pbank(bank, w1=128)), inc="sv")
            self.pb_set(bank, evv)
        if alibi:
            self.pending["sync"].append(self.tok("spe"))
            lds = self.dma("sync", "misc", lambda e: e.dma_start(
                self.rows[0:1, 0:16], self.hw["L1_attn_slopes"].rearrange("(o s) -> o s", o=1)),
                self_wait=True)
            self.wait("tensor", lds)
            self.pb_wait(4)
            pe = self.on("tensor", lambda e: e.matmul(
                self.pbank(4, w1=16), onesr[:], self.rows[0:1, 0:16],
                start=True, stop=True), inc="spe")
            self.wait("vector", pe)
            self.on("vector", lambda e: e.tensor_copy(self.slope_bc[:], self.pbank(4, w1=16)), inc="sv")
            e2 = self.on("vector", lambda e: e.tensor_scalar(
                self.nrcell[:], self.pbank(4, w1=16), 1.0, 0.0, op0=ALU.mult, op1=ALU.max), inc="sv")
            self.pb_set(4, e2)
        self.on("vector", lambda e: e.memset(self.Kh[64:65, :], 1.0), inc="sv")
        self.bar()

        _nh = NH
        halfc = self.consts["c_halfcol_b"]
        hctx = {}
        for h in range(_nh + 1):
            if h < _nh:
                p = h % 2
                par = (h // 2) % 2
                hp = h % 2
                if hp == 0:
                    g = self.guards.get(("wqk", par))
                    if g:
                        self.pending["gpsimd"].append(g)
                    self.dma("gpsimd", f"wqk{par}", lambda e, h=h, pq=par: e.dma_start(
                        self.wqk[:, pq, 0, :, :],
                        W[:, qoff + 64 * h: qoff + 64 * (h + 2)].rearrange("(i p) f -> p i f", p=128)))
                    lk = self.dma("gpsimd", f"wqk{par}", lambda e, h=h, pq=par: e.dma_start(
                        self.wqk[:, pq, 1, :, :],
                        W[:, koff + 64 * h: koff + 64 * (h + 2)].rearrange("(i p) f -> p i f", p=128)))
                    self.guards[("wqk_tok", par)] = lk
                lk = self.guards[("wqk_tok", par)]
                if h % 4 == 0:
                    g = self.guards.get("wv")
                    if g:
                        self.pending["gpsimd"].append(g)
                    lv = self.dma("gpsimd", "wv", lambda e, h=h: e.dma_start(
                        self.wv[:],
                        W[:, voff + 64 * h: voff + 64 * (h + 4)].rearrange("(i p) f -> p i f", p=128)))
                sc = None
                if alibi:
                    sc = self.on("vector", lambda e, h=h, p=p: e.tensor_scalar(
                        self.scol[:, p * 16:(p + 1) * 16], self.consts["c_iotacol16"][:],
                        self.slope_bc[:, h:h + 1], None, op0=ALU.mult),
                        inc="sv", waits=[self.tok("ss")])
                hctx[h] = {"p": p, "rowtok": None, "sc": sc}
                qk_evacs = []
                for qk in (0, 1):
                    col = self.bqcol if qk == 0 else self.bkcol
                    dst = self.Qh if qk == 0 else self.Kh
                    for g4 in range(NG):
                        self.pb_wait(g4)
                    self.wait("tensor", lk)
                    for i in range(NI):
                        for g4 in range(NG):
                            self.on("tensor", lambda e, pq=par, qk=qk, i=i, g4=g4, hp=hp: e.matmul(
                                self.pbank(g4, hi=64), self.wqk[:, pq, qk, i, hp * 64:(hp + 1) * 64],
                                self.XB[:, i, g4 * 512:(g4 + 1) * 512],
                                start=(i == 0), stop=(i == NI - 1)),
                                inc="spe" if (i == NI - 1 and g4 == NG - 1) else None)
                    mm = self.tok("spe")
                    self.guards[("wqk", par)] = mm
                    self.wait("vector", mm)
                    for g4 in range(NG):
                        evq = self.on("vector", lambda e, h=h, p=p, g4=g4, dst=dst, col=col: e.tensor_scalar(
                            dst[0:64, p, g4 * 512:(g4 + 1) * 512], self.pbank(g4, hi=64),
                            QS, col[0:64, h:h + 1], op0=ALU.mult, op1=ALU.add), inc="sv")
                        self.pb_set(g4, evq)
                        qk_evacs.append(evq)
            if h > 0:
                hh = h - 1
                c = hctx[hh]
                p1 = c["p"]
                hloc = hh % 4
                self.wait("tensor", self.guards.get("vg_ready"))
                for G in range(NG):
                    yb = 6 + G % 2
                    self.pb_wait(yb)
                    ilist = list(range(4 * G + 3, -1, -1))
                    if alibi:
                        sched = [(idx,) for idx in range(len(ilist))]
                    elif G == 0:
                        sched = [(0,), (1,), (2,), (3,)]
                    else:
                        sched = [(idx,) for idx in range(4)] + [(a, a + 1) for a in range(4, len(ilist), 2)]
                    for grp in sched:
                        abanks = []
                        for gidx, idx in enumerate(grp):
                            i = ilist[idx]
                            diag = i - 4 * G
                            woff = 128 * diag if diag >= 0 else 0
                            wlen = 512 - woff
                            ab = 4 + (idx % 2)
                            abanks.append((idx, i, diag, woff, wlen, ab))
                            self.pb_wait(ab)
                            self.wait("tensor", c["rowtok"])
                            if alibi:
                                self.wait("tensor", c["sc"])
                            self.on("tensor", lambda e, i=i, G=G, ab=ab, woff=woff, wlen=wlen, diag=diag, p1=p1: e.matmul(
                                self.pbank(ab, w0=woff, w1=woff + wlen),
                                self.Kh[0:65, p1, i * 128:(i + 1) * 128],
                                self.Qh[0:65, p1, G * 512 + woff: G * 512 + woff + wlen],
                                start=True, stop=(diag < 0)), inc=None if diag >= 0 else "spe")
                            if diag >= 0:
                                self.on("tensor", lambda e, ab=ab, woff=woff: e.matmul(
                                    self.pbank(ab, w0=woff, w1=woff + 128),
                                    self.consts["c_tri_b"][:], self.consts["c_ident_b"][:],
                                    start=False, stop=True), inc="spe")
                        qkmm = self.tok("spe")
                        if len(grp) == 1:
                            idx, i, diag, woff, wlen, ab = abanks[0]
                            aslot = idx % 4
                            slot_g = self.guards.get(("asb", aslot))
                            if slot_g:
                                self.pending["scalar"].append(slot_g)
                            mstok = None
                            if idx == 0 and woff > 0:
                                if slot_g:
                                    self.pending["vector"].append(slot_g)
                                mstok = self.on("vector", lambda e, aslot=aslot, woff=woff: e.memset(
                                    self.asb[:, aslot, 0:woff], 0.0), inc="sv")
                            self.wait("scalar", qkmm)
                            bias = (lambda ee, i=i, p1=p1: None)
                            if alibi:
                                ex = self.on("scalar", lambda e, i=i, ab=ab, aslot=aslot, woff=woff, wlen=wlen, p1=p1:
                                             e.activation(self.asb[:, aslot, woff:woff + wlen],
                                                          self.pbank(ab, w0=woff, w1=woff + wlen),
                                                          AF.Exp, bias=self.scol[:, p1 * 16 + i:p1 * 16 + i + 1],
                                                          scale=1.0), inc="ss")
                            else:
                                ex = self.on("scalar", lambda e, i=i, ab=ab, aslot=aslot, woff=woff, wlen=wlen:
                                             e.activation(self.asb[:, aslot, woff:woff + wlen],
                                                          self.pbank(ab, w0=woff, w1=woff + wlen),
                                                          AF.Exp, bias=0.0, scale=1.0), inc="ss")
                            self.pb_set(ab, ex)
                            pairs = [(idx, i, woff, wlen, ab, aslot, ex, mstok)]
                        else:
                            (idxA, iA, _, _, _, abA), (idxB, iB, _, _, _, abB) = abanks
                            aslot = idxA % 4
                            assert abA == 4 and abB == 5 and aslot in (0, 2)
                            for s_ in (aslot, aslot + 1):
                                slot_g = self.guards.get(("asb", s_))
                                if slot_g:
                                    self.pending["scalar"].append(slot_g)
                            self.wait("scalar", qkmm)
                            ex = self.on("scalar", lambda e, aslot=aslot:
                                         e.activation(self.asb[:, aslot, :].rearrange("p a b -> p (a b)") if False else self.asb[:, aslot:aslot + 2, :].rearrange("p a b -> p (a b)"),
                                                      self.pbB[:, 0:1024],
                                                      AF.Exp, bias=0.0, scale=1.0), inc="ss")
                            self.pb_set(4, ex)
                            self.pb_set(5, ex)
                            pairs = [(idxA, iA, 0, 512, 4, aslot, ex, None),
                                     (idxB, iB, 0, 512, 5, aslot + 1, ex, None)]
                        for idx, i, woff, wlen, ab, aslot, ex, mstok in pairs:
                            self.wait("tensor", ex)
                            self.wait("tensor", mstok)
                            pw = 0 if idx == 0 else woff
                            pv = self.on("tensor", lambda e, i=i, idx=idx, aslot=aslot, hloc=hloc, pw=pw, yb=yb: e.matmul(
                                self.pbank(yb, hi=65, w0=pw, w1=512),
                                self.vg[:, i, hloc * 65:(hloc + 1) * 65],
                                self.asb[:, aslot, pw:512],
                                start=(idx == 0), stop=(i == 0)), inc="spe")
                            self.guards[("asb", aslot)] = pv
                    pvmm = self.tok("spe")
                    lw = self.guards.get("lnbw")
                    if G == NG - 1 or G == (1 if hh % 2 else 0):
                        self.wait("scalar", pvmm)
                        self.wait("scalar", lw)
                        rc = self.on("scalar", lambda e, yb=yb: e.activation(
                            self.lnb[64:65, 0:512], self.pbank(yb, lo=64, hi=65),
                            AF.Ln, bias=0.0, scale=1.0), inc="ss")
                        self.on("scalar", lambda e: e.activation(
                            self.lnb[64:65, 0:512], self.lnb[64:65, 0:512],
                            AF.Exp, bias=0.0, scale=-1.0), inc="ss")
                        rce = self.tok("ss")
                    else:
                        self.wait("vector", pvmm)
                        self.wait("vector", lw)
                        rc = self.on("vector", lambda e, yb=yb: e.reciprocal(
                            self.lnb[64:65, 0:512], self.pbank(yb, lo=64, hi=65)), inc="sv")
                        rce = rc
                    self.wait("sync", rce)
                    ot = self.dma("sync", "bcast", lambda e, G=G: e.dma_start(
                        self.dscr[G % 2], self.lnb[64:65, 0:512]), self_wait=True)
                    self.guards["lnbw"] = ot
                    self.wait("sync", self.guards.get("bscw"))
                    it = self.dma("sync", "bcast", lambda e, G=G: e.dma_start(
                        self.bsc[0:64, :], self.dscr[G % 2].broadcast_to((64, 512))))
                    if G == NG - 1:
                        self._defer.append((it, rc, hh, G, yb))
                    else:
                        self.wait("vector", it)
                        self.wait("vector", rc)
                        ynorm = self.on("vector", lambda e, hh=hh, G=G, yb=yb: e.tensor_tensor(
                            self.yT[(hh % 2) * 64:(hh % 2) * 64 + 64, hh // 2, G * 512:(G + 1) * 512],
                            self.pbank(yb, hi=64), self.bsc[0:64, :], op=ALU.mult), inc="sv")
                        self.guards["bscw"] = ynorm
                        self.pb_set(yb, ynorm)
            if h < _nh and h % 4 == 0:
                self.plan_vgen(b, lv, first=(h == 0), h0=h)
            if h < _nh:
                self.waits("gpsimd", qk_evacs)
                sk = self.on("gpsimd", lambda e, p=p: e.tensor_tensor(
                    self.Qh[64:128, p, :], self.Kh[0:64, p, :], self.Kh[0:64, p, :], op=ALU.mult), inc="sdw")
                sqq = self.on("gpsimd", lambda e, p=p, h=h: e.tensor_tensor(
                    self.yT[(h % 2) * 64:(h % 2) * 64 + 64, h // 2, :],
                    self.Qh[0:64, p, :], self.Qh[0:64, p, :], op=ALU.mult), inc="sdw")
                self.wait("tensor", sk)
                for g4 in range(NG):
                    self.pb_wait(g4)
                    self.on("tensor", lambda e, g4=g4, p=p: e.matmul(
                        self.pbank(g4, lo=0, hi=1), halfc[64:128, :],
                        self.Qh[64:128, p, g4 * 512:(g4 + 1) * 512], start=True, stop=True),
                        inc="spe" if g4 == NG - 1 else None)
                mmk = self.tok("spe")
                self.wait("vector", mmk)
                rmx = self.on("vector", lambda e, p=p: e.tensor_reduce(
                    self.kcell[0:1, p:p + 1], self.pbA[0:1, :], axis=AX.X, op=ALU.max), inc="sv")
                self.wait("tensor", rmx)
                self.wait("tensor", sqq)
                qb = (h % 2) * 64
                for g4 in range(NG):
                    self.on("tensor", lambda e, g4=g4, h=h, qb=qb: e.matmul(
                        self.pbank(g4, lo=0, hi=1), halfc[qb:qb + 64, :],
                        self.yT[qb:qb + 64, h // 2, g4 * 512:(g4 + 1) * 512], start=True, stop=False))
                    self.on("tensor", lambda e, g4=g4, p=p: e.matmul(
                        self.pbank(g4, lo=0, hi=1), self.kcell[0:1, p:p + 1],
                        self.consts["c_half_row_b"][0:1, 0:512],
                        start=False, stop=(not alibi)),
                        inc=None if alibi else ("spe" if g4 == NG - 1 else None))
                    if alibi:
                        self.on("tensor", lambda e, g4=g4, h=h: e.matmul(
                            self.pbank(g4, lo=0, hi=1), self.nrcell[0:1, h:h + 1],
                            self.consts["c_iota_row_b"][0:1, g4 * 512:(g4 + 1) * 512],
                            start=False, stop=True),
                            inc="spe" if g4 == NG - 1 else None)
                mmq = self.tok("spe")
                self.wait("scalar", mmq)
                rowtok = self.on("scalar", lambda e, p=p: e.activation(
                    self.Qh[64:65, p, :], self.pbA[0:1, :], AF.Identity,
                    bias=0.0, scale=-1.0), inc="ss")
                for g4 in range(NG):
                    self.pb_set(g4, rowtok)
                hctx[h]["rowtok"] = rowtok
            self.flush_ynorm()
        self.bar()
        for ft in range(NI):
            par = ft % 2
            g = self.guards.get(("wpj", par))
            if g:
                self.pending["gpsimd"].append(g)
            lp = self.dma("gpsimd", f"wpj{par}", lambda e, ft=ft, p=par: e.dma_start(
                self.wpj[:, p, :, :],
                WP[:, ft * 128:(ft + 1) * 128].rearrange("(i p) f -> p i f", p=128)))
            for tg in range(NG):
                self.pb_wait(tg)
            self.wait("tensor", lp)
            for ic in range(NI):
                for tg in range(NG):
                    self.on("tensor", lambda e, ft=ft, p=par, ic=ic, tg=tg: e.matmul(
                        self.pbank(tg), self.wpj[:, p, ic, :],
                        self.yT[:, ic, tg * 512:(tg + 1) * 512],
                        start=(ic == 0), stop=(ic == NI - 1)),
                        inc="spe" if (ic == NI - 1 and tg == NG - 1) else None)
            mm = self.tok("spe")
            self.guards[("wpj", par)] = mm
            self.wait("vector", mm)
            for tg in range(NG):
                evp = self.on("vector", lambda e, ft=ft, tg=tg: e.scalar_tensor_tensor(
                    self.R[:, ft, tg * 512:(tg + 1) * 512], self.pbank(tg),
                    self.bpcol[:, ft:ft + 1], self.R[:, ft, tg * 512:(tg + 1) * 512],
                    op0=ALU.add, op1=ALU.add), inc="sv")
                self.pb_set(tg, evp)
        self.bar()

    def plan_vgen(self, b, lv_tok, first, h0=0):
        if first:
            ms = self.on("vector", lambda e: e.memset(
                self.vg.rearrange("p t (hh c) -> p t hh c", c=65)[:, :, :, 64:65], 1.0), inc="sv")
            self.wait("tensor", ms)
        ev = None
        for tau in range(NT):
            bank = tau % 2
            self.pb_wait(bank)
            self.wait("tensor", lv_tok)
            for i in range(NI):
                self.on("tensor", lambda e, tau=tau, i=i, bank=bank: e.matmul(
                    self.pbank(bank, w1=256), self.XB[:, i, tau * 128:(tau + 1) * 128],
                    self.wv[:, i, 0:256], start=(i == 0), stop=(i == NI - 1)),
                    inc="spe" if i == NI - 1 else None)
            mm = self.tok("spe")
            self.wait("vector", mm)
            ev = self.on("vector", lambda e, tau=tau, bank=bank, h0=h0: e.tensor_tensor(
                self.vg[:, tau, :].rearrange("p (hh c) -> p hh c", c=65)[:, :, 0:64],
                self.pbank(bank, w1=256).rearrange("p (hh c) -> p hh c", c=64),
                self.vbias[:, h0 * 64:(h0 + 4) * 64].rearrange("p (hh c) -> p hh c", c=64),
                op=ALU.add), inc="sv")
            self.pb_set(bank, ev)
        self.guards["vg_ready"] = ev
        self.guards["wv"] = self.tok("spe")

    def plan_ln(self, b, final=False):
        onescb = self.consts["c_onescol_b"]
        onesr = self.consts["c_onesrow_f"]
        self.bar()
        prev_sq = None
        for i in range(NI):
            srcx = self.on("vector", lambda e, i=i: e.tensor_copy(
                self.XB[:, i, :], self.R[:, i, :]), inc="sv")
            if prev_sq:
                self.pending["scalar"].append(prev_sq)
            srcs = self.on("scalar", lambda e, i=i: e.activation(
                self.lnsq[:, :], self.R[:, i, :], AF.Square, bias=0.0, scale=1.0), inc="ss")
            self.wait("tensor", srcx)
            self.wait("tensor", srcs)
            for g4 in range(NG):
                if i == 0:
                    self.pb_wait(g4)
                    self.pb_wait(4 + g4)
                self.on("tensor", lambda e, g4=g4, i=i: e.matmul(
                    self.pbank(g4, lo=0, hi=1), onescb[:],
                    self.XB[:, i, g4 * 512:(g4 + 1) * 512],
                    start=(i == 0), stop=(i == NI - 1)))
                self.on("tensor", lambda e, g4=g4, i=i: e.matmul(
                    self.pbank(4 + g4, lo=0, hi=1), onescb[:],
                    self.lnsq[:, g4 * 512:(g4 + 1) * 512],
                    start=(i == 0), stop=(i == NI - 1)),
                    inc="spe" if g4 == NG - 1 else None)
            prev_sq = self.tok("spe")
        mm = self.tok("spe")
        self.wait("scalar", mm)
        evt0 = self.on("scalar", lambda e: e.activation(
            self.rows[0:1, :], self.pbA[0:1, :], AF.Identity,
            bias=0.0, scale=1.0 / D), inc="ss")
        for g4 in range(NG):
            self.pb_set(g4, evt0)
        self.wait("vector", evt0)
        v1 = self.on("vector", lambda e: e.tensor_tensor(
            self.rows[64:65, :], self.rows[0:1, :], self.rows[0:1, :], op=ALU.mult), inc="sv")
        self.wait("vector", mm)
        v2 = self.on("vector", lambda e: e.scalar_tensor_tensor(
            self.rows[32:33, :], self.pbB[0:1, :], 1.0 / D, self.rows[64:65, :],
            op0=ALU.mult, op1=ALU.subtract), inc="sv")
        for g4 in range(NG):
            self.pb_set(4 + g4, v2)
        mmb = None
        for g4 in range(NG):
            self.wait("scalar", v2)
            self.wait("scalar", mmb)
            self.on("scalar", self.lp(lambda e, g4=g4: e.activation(
                self.rows2[0:1, 0:512], self.rows[32:33, g4 * 512:(g4 + 1) * 512], AF.Ln,
                bias=self.consts["c_epscol"][0:1, 0:1], scale=1.0)), inc="ss")
            s2 = self.on("scalar", self.lp(lambda e: e.activation(
                self.rows2[0:1, 0:512], self.rows2[0:1, 0:512], AF.Exp, bias=0.0, scale=-0.5)), inc="ss")
            self.wait("tensor", s2)
            self.pb_wait(4)
            self.on("tensor", lambda e, g4=g4: e.matmul(
                self.pbank(4), onesr[:],
                self.rows[0:1, g4 * 512:(g4 + 1) * 512],
                start=True, stop=True))
            self.pb_wait(5)
            self.on("tensor", lambda e: e.matmul(
                self.pbank(5), self.consts["c_onesrow_r"][:],
                self.rows2[0:1, 0:512],
                start=True, stop=True), inc="spe")
            mmb = self.tok("spe")
            self.wait("vector", mmb)
            t2 = None
            for i in range(NI):
                self.on("vector", lambda e, i=i, g4=g4: e.tensor_tensor(
                    self.lnb[:], self.R[:, i, g4 * 512:(g4 + 1) * 512], self.pbank(4),
                    op=ALU.subtract), inc="sv")
                t2 = self.on("vector", lambda e, i=i, g4=g4: e.tensor_tensor(
                    self.R[:, i, g4 * 512:(g4 + 1) * 512], self.lnb[:], self.pbank(5),
                    op=ALU.mult), inc="sv")
                if not final:
                    self.wait("scalar", t2)
                    self.on("scalar", lambda e, i=i, g4=g4: e.copy(
                        self.XB[:, i, g4 * 512:(g4 + 1) * 512],
                        self.R[:, i, g4 * 512:(g4 + 1) * 512]), inc="ss")
            self.pb_set(4, t2)
            self.pb_set(5, t2)
        self.bar()

    def plan_moe(self, b):
        Wr = self.hw["L0_ffn_Wr"]; W1 = self.hw["L0_ffn_W1"]; W2 = self.hw["L0_ffn_W2"]
        B1 = self.hw["L0_ffn_b1"]; B2 = self.hw["L0_ffn_b2"]
        idf = self.consts["c_ident_f"]
        selc = self.consts["c_sel"]
        for i in range(NI):
            g = self.guards.get("b24")
            if g:
                self.pending["sync"].append(g)
            ldi = self.dma("sync", "misc", lambda e, i=i: e.dma_start(
                self.b24[0:8, :], Wr[:, i * 128:(i + 1) * 128]), self_wait=True)
            self.wait("tensor", ldi)
            self.pb_wait(0)
            pe = self.on("tensor", lambda e: e.matmul(
                self.pbank(0, w1=8), self.b24[0:8, :], idf[0:8, 0:8],
                is_transpose=True, start=True, stop=True), inc="spe")
            self.guards["b24"] = pe
            self.wait("vector", pe)
            evw = self.on("vector", lambda e, i=i: e.tensor_copy(
                self.wrt[:, i, :], self.pbank(0, lo=0, hi=128, w1=8)), inc="sv")
            self.pb_set(0, evw)
        for e_ in range(NE):
            self.bias_cols(B1[e_, :], HF, self.b1col[:, e_, :], bank=1)
            self.bias_cols(B2[e_, :], D, self.b2col[:, e_, :], bank=2)
        self.bar()
        for tau in range(NT):
            self.pb_wait(3)
            for i in range(NI):
                self.on("tensor", lambda e, tau=tau, i=i: e.matmul(
                    self.pbank(3, w1=8), self.R[:, i, tau * 128:(tau + 1) * 128],
                    self.wrt[:, i, :], start=(i == 0), stop=(i == NI - 1)),
                    inc="spe" if i == NI - 1 else None)
            mm = self.tok("spe")
            self.wait("vector", mm)
            self.on("vector", lambda e, tau=tau: e.tensor_reduce(
                self.pcol[:, 0:1], self.pbank(3, w1=8), axis=AX.X, op=ALU.max), inc="sv")
            v1b = self.on("vector", lambda e: e.tensor_scalar(
                self.pcol[:, 1:2], self.pcol[:, 0:1], -1.0, None, op0=ALU.mult), inc="sv")
            self.wait("scalar", v1b)
            s1 = self.on("scalar", lambda e, tau=tau: e.activation(
                self.ptmp[:, tau, :], self.pbank(3, w1=8), AF.Exp,
                bias=self.pcol[:, 1:2], scale=1.0), inc="ss")
            self.pb_set(3, s1)
            self.wait("vector", s1)
            self.on("vector", lambda e, tau=tau: e.tensor_reduce(
                self.pcol[:, 2:3], self.ptmp[:, tau, :], axis=AX.X, op=ALU.add), inc="sv")
            self.on("vector", lambda e: e.reciprocal(
                self.pcol[:, 3:4], self.pcol[:, 2:3]), inc="sv")
            self.on("vector", lambda e, tau=tau: e.tensor_scalar(
                self.probs[:, tau, :], self.ptmp[:, tau, :], self.pcol[:, 3:4], None,
                op0=ALU.mult), inc="sv")
            self.on("vector", lambda e, tau=tau: e.tensor_reduce(
                self.pcol[:, 0:1], self.probs[:, tau, :], axis=AX.X, op=ALU.max), inc="sv")
            self.on("vector", lambda e, tau=tau: e.tensor_scalar(
                self.ptmp[:, tau, :], self.probs[:, tau, :], self.pcol[:, 0:1], None,
                op0=ALU.subtract), inc="sv")
            self.on("vector", lambda e, tau=tau: e.tensor_scalar(
                self.ptmp[:, tau, :], self.ptmp[:, tau, :], 0.0, None, op0=ALU.is_ge), inc="sv")
            self.on("vector", lambda e, tau=tau: e.scalar_tensor_tensor(
                self.ptmp[:, tau, :], self.ptmp[:, tau, :], -2.0, self.probs[:, tau, :],
                op0=ALU.mult, op1=ALU.add), inc="sv")
            self.on("vector", lambda e, tau=tau: e.tensor_reduce(
                self.pcol[:, 1:2], self.ptmp[:, tau, :], axis=AX.X, op=ALU.max), inc="sv")
            self.on("vector", lambda e, tau=tau: e.tensor_scalar(
                self.ptmp[:, tau, :], self.probs[:, tau, :], self.pcol[:, 1:2], None,
                op0=ALU.subtract), inc="sv")
            self.on("vector", lambda e, tau=tau: e.tensor_scalar(
                self.ptmp[:, tau, :], self.ptmp[:, tau, :], 0.0, None, op0=ALU.is_ge), inc="sv")
            v10 = self.on("vector", lambda e, tau=tau: e.tensor_tensor(
                self.comb[:, tau, :], self.probs[:, tau, :], self.ptmp[:, tau, :],
                op=ALU.mult), inc="sv")
            self.wait("tensor", v10)
            self.pb_wait(3)
            pt = self.on("tensor", lambda e, tau=tau: e.matmul(
                self.pbank(3, lo=0, hi=8, w1=128), self.comb[:, tau, :], idf[:],
                is_transpose=True, start=True, stop=True), inc="spe")
            self.wait("vector", pt)
            evt = self.on("vector", lambda e, tau=tau: e.tensor_copy(
                self.rows[0:8, tau * 128:(tau + 1) * 128], self.pbank(3, lo=0, hi=8, w1=128)), inc="sv")
            self.pb_set(3, evt)
        self.bar()
        yflat = self.yT.rearrange("p a t -> p (a t)")
        h2flat = self.h2.rearrange("p a t -> p (a t)")

        def h_ap(fi, tg):
            base = fi * 1024 + tg * 512
            tens = yflat if base < 16384 else h2flat
            return tens[:, base % 16384: base % 16384 + 512]

        selc = self.consts["c_sel"]
        for th in range(2):
            if self.guards.get("wcache_done") is None and "d_wcache" in self.sems and th == 1:
                self.guards["wcache_done"] = self.tok("d_wcache")
            for e_ in range(NE):
                evc = []
                for tg, bkc in ((0, 3), (1, 6)):
                    self.pb_wait(bkc)
                    mmc = self.on("tensor", lambda e, e_=e_, th=th, tg=tg, bkc=bkc: e.matmul(
                        self.pbank(bkc), selc[:, e_ * 128:(e_ + 1) * 128],
                        self.rows[0:8, th * 1024 + tg * 512: th * 1024 + (tg + 1) * 512],
                        start=True, stop=True), inc="spe")
                    self.wait("vector", mmc)
                    ec = self.on("vector", lambda e, tg=tg, bkc=bkc: e.tensor_copy(
                        self.cbe[:, tg, :], self.pbank(bkc)), inc="sv")
                    self.pb_set(bkc, ec)
                    evc.append(ec)
                for slab in range(16):
                    spar = slab % 2
                    g = self.guards.get(("w1s", spar))
                    if g:
                        self.pending["gpsimd"].append(g)
                        self.pending["sync"].append(g)
                    gst = self.guards.get(("w1st", spar))
                    if gst:
                        self.pending["gpsimd"].append(gst)
                        self.pending["sync"].append(gst)
                    ck = ("w1", e_, slab)
                    if ck not in self.moe_cached:
                        self.moe_cached.add(ck)
                        l1 = self.dma("gpsimd", f"w1s{spar}", lambda e, e_=e_, slab=slab, spar=spar: e.dma_start(
                            self.w1s[:, spar, :, :],
                            W1[e_, :, slab * 256:(slab + 1) * 256].rearrange("(i p) f -> p i f", p=128)))
                        self.wait("sync", l1)
                        st = self.dma("sync", "wcache", lambda e, e_=e_, slab=slab, spar=spar: e.dma_start(
                            self.w1c[e_, :, slab * 256:(slab + 1) * 256].rearrange("(i p) f -> p i f", p=128),
                            self.w1s[:, spar, :, :]))
                        self.guards[("w1st", spar)] = st
                    else:
                        self.wait("sync", self.guards.get("wcache_done"))
                        l1 = self.dma("sync", f"w1r{spar}", lambda e, e_=e_, slab=slab, spar=spar: e.dma_start(
                            self.w1s[:, spar, :, :],
                            self.w1c[e_, :, slab * 256:(slab + 1) * 256].rearrange("(i p) f -> p i f", p=128)))
                        self.guards[("w1st", spar)] = None
                    for f2 in range(2):
                        ft = slab * 2 + f2
                        bks = ((ft % 2) * 2, (ft % 2) * 2 + 1)
                        for tg in range(2):
                            self.pb_wait(bks[tg])
                        self.wait("tensor", l1)
                        for i in range(NI):
                            for tg in range(2):
                                self.on("tensor", lambda e, spar=spar, i=i, f2=f2, th=th, tg=tg, bk=bks[tg]: e.matmul(
                                    self.pbank(bk), self.w1s[:, spar, i, f2 * 128:(f2 + 1) * 128],
                                    self.XB[:, i, th * 1024 + tg * 512: th * 1024 + (tg + 1) * 512],
                                    start=(i == 0), stop=(i == NI - 1)),
                                    inc="spe" if (i == NI - 1 and tg == 1) else None)
                        mm = self.tok("spe")
                        self.guards[("w1s", spar)] = mm
                        self.wait("scalar", mm)
                        for tg in range(2):
                            ge = self.on("scalar", lambda e, e_=e_, ft=ft, tg=tg, bk=bks[tg]: e.activation(
                                h_ap(ft, tg), self.pbank(bk), AF.Gelu,
                                bias=self.b1col[:, e_, ft:ft + 1], scale=1.0), inc="ss")
                            self.pb_set(bks[tg], ge)
                h_ready = self.tok("ss")
                for dt in range(NI):
                    bank0 = 4 if dt % 2 == 0 else 0
                    ltoks = []
                    for fh in range(2):
                        wpar = fh
                        g = self.guards.get(("w2s", wpar))
                        if g:
                            self.pending["gpsimd"].append(g)
                            self.pending["sync"].append(g)
                        gst = self.guards.get(("w2st", wpar))
                        if gst:
                            self.pending["gpsimd"].append(gst)
                            self.pending["sync"].append(gst)
                        ck = ("w2", e_, dt, fh)
                        if ck not in self.moe_cached:
                            self.moe_cached.add(ck)
                            l2 = self.dma("gpsimd", f"w2s{wpar}", lambda e, e_=e_, dt=dt, fh=fh, wpar=wpar: e.dma_start(
                                self.w2s[:, wpar, :, :],
                                W2[e_, fh * 2048:(fh + 1) * 2048, dt * 128:(dt + 1) * 128]
                                .rearrange("(fc p) d -> p fc d", p=128)))
                            self.wait("sync", l2)
                            st = self.dma("sync", "wcache", lambda e, e_=e_, dt=dt, fh=fh, wpar=wpar: e.dma_start(
                                self.w2c[e_, fh * 2048:(fh + 1) * 2048, dt * 128:(dt + 1) * 128]
                                .rearrange("(fc p) d -> p fc d", p=128),
                                self.w2s[:, wpar, :, :]))
                            self.guards[("w2st", wpar)] = st
                        else:
                            self.wait("sync", self.guards.get("wcache_done"))
                            l2 = self.dma("sync", f"w2r{wpar}", lambda e, e_=e_, dt=dt, fh=fh, wpar=wpar: e.dma_start(
                                self.w2s[:, wpar, :, :],
                                self.w2c[e_, fh * 2048:(fh + 1) * 2048, dt * 128:(dt + 1) * 128]
                                .rearrange("(fc p) d -> p fc d", p=128)))
                            self.guards[("w2st", wpar)] = None
                        ltoks.append((l2, wpar))
                    for tg in range(2):
                        self.pb_wait(bank0 + tg)
                    self.wait("tensor", h_ready)
                    for fh in range(2):
                        l2, wpar = ltoks[fh]
                        self.wait("tensor", l2)
                        for fc in range(16):
                            for tg in range(2):
                                self.on("tensor", lambda e, wpar=wpar, fh=fh, fc=fc, tg=tg, bk=bank0 + tg, th=th: e.matmul(
                                    self.pbank(bk), self.w2s[:, wpar, fc, :],
                                    h_ap(fh * 16 + fc, tg), start=(fh == 0 and fc == 0),
                                    stop=(fh == 1 and fc == 15)),
                                    inc="spe" if (fc == 15 and tg == 1) else None)
                        self.guards[("w2s", wpar)] = self.tok("spe")
                    mm = self.tok("spe")
                    self.wait("vector", mm)
                    self.waits("vector", evc)
                    for tg in range(2):
                        self.on("vector", lambda e, e_=e_, dt=dt, tg=tg, bk=bank0 + tg: e.scalar_tensor_tensor(
                            self.lnb[:], self.pbank(bk), self.b2col[:, e_, dt:dt + 1],
                            self.cbe[:, tg, :], op0=ALU.add, op1=ALU.mult), inc="sv")
                        ts2 = self.on("vector", lambda e, dt=dt, tg=tg, th=th: e.tensor_tensor(
                            self.R[:, dt, th * 1024 + tg * 512: th * 1024 + (tg + 1) * 512],
                            self.R[:, dt, th * 1024 + tg * 512: th * 1024 + (tg + 1) * 512],
                            self.lnb[:], op=ALU.add), inc="sv")
                        self.pb_set(bank0 + tg, ts2)
        self.bar()

    def plan_qffn(self, b):
        WQ = self.hw["L1_ffn_weight"]; BQ = self.hw["L1_ffn_bias"]
        self.bias_cols(BQ, D, self.bfcol, bank=0)
        self.bar()
        if b > 0:
            ld = self.dma("sync", "misc", lambda e: e.dma_start(
                self.wdT[:, :, :].rearrange("p a b -> p (a b)"), self.wdc), self_wait=True)
            self.guards["wdt_ld"] = ld
        for r in (range(NI) if b == 0 else ()):
            g = self.guards.get("qnat0")
            if g:
                self.pending["sync"].append(g)
            ld = self.dma("sync", "misc", lambda e, r=r: e.dma_start(
                self.qnat[:, 0, :], WQ[r * 128:(r + 1) * 128, :]), self_wait=True)
            qtok = self.plan_quant_tile(lambda: self.qnat[:, 0, :], ld)
            self.guards["qnat0"] = qtok
            for i in range(NI):
                self.pb_wait(7)
                self.wait("tensor", qtok)
                pe = self.on("tensor", lambda e, i=i: e.matmul(
                    self.ptb[:, 0:128], self.qout[:, i * 128:(i + 1) * 128],
                    self.consts["c_ident_b"][:],
                    is_transpose=True, start=True, stop=True), inc="spe")
                self.wait("vector", pe)
                ev = self.on("vector", lambda e, i=i, r=r: e.tensor_copy(
                    self.wdT[:, i, r * 128:(r + 1) * 128], self.ptb[:, 0:128]), inc="sv")
                self.pb_set(7, ev)
            self.pending["vector"].append(self.tok("spe"))
        if b == 0:
            self.pending["sync"].append(self.tok("sv"))
            st = self.dma("sync", "misc", lambda e: e.dma_start(
                self.wdc, self.wdT[:, :, :].rearrange("p a b -> p (a b)")), self_wait=True)
            self.guards["wdc_store"] = st
        self.bar()
        for tgc in range(NG):
            for tau in range(4 * tgc, 4 * tgc + 4):
                par = tau % 2
                evs = []
                for i in range(NI):
                    bank = 4 + i % 2
                    eng = "vector" if i % 2 == 0 else "scalar"
                    g = self.guards.get(("qnat", par))
                    if g:
                        self.pending[eng].append(g)
                    ev = self.transpose_f32(
                        lambda i=i, t=tau: self.R[:, i, t * 128:(t + 1) * 128],
                        lambda p=par, i=i: self.qnat[:, p, i * 128:(i + 1) * 128],
                        bank, eng)
                    evs.append(ev)
                qtok = self.plan_quant_tile(lambda p=par: self.qnat[:, p, :], None, pre=evs)
                self.guards[("qnat", par)] = qtok
                for i in range(NI):
                    self.pb_wait(7)
                    self.wait("tensor", qtok)
                    pe = self.on("tensor", lambda e, i=i: e.matmul(
                        self.ptb[:, 0:128], self.qout[:, i * 128:(i + 1) * 128],
                        self.consts["c_ident_b"][:],
                        is_transpose=True, start=True, stop=True), inc="spe")
                    self.wait("vector", pe)
                    ev = self.on("vector", lambda e, i=i, t=tau: e.tensor_copy(
                        self.XB[:, i, t * 128:(t + 1) * 128], self.ptb[:, 0:128]), inc="sv")
                    self.pb_set(7, ev)
                self.pending["vector"].append(self.tok("spe"))
            xbtok = self.tok("sv")
            for mt in range(NI):
                bank = mt % 2
                self.pb_wait(bank)
                self.wait("tensor", xbtok)
                self.wait("tensor", self.guards.pop("wdt_ld", None))
                for i in range(NI):
                    self.on("tensor", lambda e, mt=mt, i=i, tgc=tgc: e.matmul(
                        self.pbank(mt % 2),
                        self.wdT[:, i, mt * 128:(mt + 1) * 128],
                        self.XB[:, i, tgc * 512:(tgc + 1) * 512],
                        start=(i == 0), stop=(i == NI - 1)),
                        inc="spe" if i == NI - 1 else None)
                mm = self.tok("spe")
                self.wait("vector", mm)
                ev = self.on("vector", lambda e, mt=mt, tgc=tgc: e.scalar_tensor_tensor(
                    self.R[:, mt, tgc * 512:(tgc + 1) * 512], self.pbank(mt % 2),
                    self.bfcol[:, mt:mt + 1], self.R[:, mt, tgc * 512:(tgc + 1) * 512],
                    op0=ALU.add, op1=ALU.add), inc="sv")
                self.pb_set(mt % 2, ev)
        self.bar()

    def plan_quant_tile(self, src_fn, ld_tok, pre=()):
        if ld_tok is not None:
            self.pending["vector"].append(ld_tok)
        for t in pre:
            self.wait("vector", t)
        self.on("vector", lambda e: e.tensor_reduce(
            self.qsc[:], src_fn().rearrange("p (blk c) -> p blk c", c=32),
            axis=AX.X, op=ALU.max, apply_absolute_value=True), inc="sv")
        self.on("vector", lambda e: e.tensor_scalar(
            self.qsc[:], self.qsc[:], 1.0 / 7.5, 1e-30, op0=ALU.mult, op1=ALU.max), inc="sv")
        self.on("vector", lambda e: e.reciprocal(self.qscr[:], self.qsc[:]), inc="sv")
        self.on("vector", lambda e: e.tensor_tensor(
            self.qy[:].rearrange("p (blk c) -> p blk c", c=32),
            src_fn().rearrange("p (blk c) -> p blk c", c=32),
            self.qscr[:].unsqueeze(2).broadcast_to((128, 32, 32)),
            op=ALU.mult), inc="sv")
        ytok = self.tok("sv")
        self.wait("scalar", ytok)
        atok = self.on("scalar", lambda e: e.activation(
            self.qa[:], self.qy[:], AF.Abs, bias=0.0, scale=1.0), inc="ss")
        self.wait("vector", atok)
        self.on("vector", lambda e: e.tensor_scalar(
            self.qm2[:], self.qa[:], 2.0, -4.0, op0=ALU.is_ge, op1=ALU.mult), inc="sv")
        self.on("vector", lambda e: e.tensor_scalar(
            self.qm4[:], self.qa[:], 4.0, -2.0, op0=ALU.is_ge, op1=ALU.mult), inc="sv")
        self.on("vector", lambda e: e.scalar_tensor_tensor(
            self.qd[:], self.qm2[:], 8.0, self.qm4[:],
            op0=ALU.add, op1=ALU.add), inc="sv")
        self.on("vector", lambda e: e.tensor_tensor(
            self.qzi[:], self.qy[:], self.qd[:], op=ALU.mult), inc="sv")
        self.on("vector", lambda e: e.tensor_scalar(
            self.qzi[:], self.qzi[:], 12582912.0, -12582912.0, op0=ALU.add, op1=ALU.add), inc="sv")
        self.on("vector", lambda e: e.tensor_scalar(
            self.qa[:], self.qm2[:], -0.03125, 0.125, op0=ALU.mult, op1=ALU.add), inc="sv")
        self.on("vector", lambda e: e.scalar_tensor_tensor(
            self.qa[:], self.qm4[:], -0.125, self.qa[:],
            op0=ALU.mult, op1=ALU.add), inc="sv")
        self.on("vector", lambda e: e.tensor_tensor(
            self.qy[:], self.qzi[:], self.qa[:], op=ALU.mult), inc="sv")
        q2 = self.on("vector", lambda e: e.tensor_tensor(
            self.qout[:].rearrange("p (blk c) -> p blk c", c=32),
            self.qy[:].rearrange("p (blk c) -> p blk c", c=32),
            self.qsc[:].unsqueeze(2).broadcast_to((128, 32, 32)),
            op=ALU.mult), inc="sv")
        return q2


def kernel(**inputs):
    from concourse.bass_utils import run_bass_kernel_spmd
    x = np.ascontiguousarray(np.asarray(inputs["L0_attn_x"], dtype=np.float32))
    mk = MK(nbatch=2, layers=(0, 1), sublayers=("attn", "ffn"))
    nc = bass.Bass("TRN2", target_bir_lowering=False)
    es = ExitStack()
    mk.build(nc, es)
    in_maps = [mk.in_map(x[2 * c:2 * c + 2], inputs) for c in range(8)]
    res = run_bass_kernel_spmd(nc, in_maps, core_ids=list(range(8)))
    out = np.concatenate([np.asarray(r["x_out"]) for r in res.results], axis=0)
    return out.astype(np.float32)


# version 12
# speedup vs baseline: 1.0590x; 1.0169x over previous
import numpy as np
from contextlib import ExitStack
import concourse.bass as bass
import concourse.mybir as mybir

F32 = mybir.dt.float32
BF16 = mybir.dt.bfloat16
F32R = mybir.dt.float32r
I32 = mybir.dt.int32
AF = mybir.ActivationFunctionType
ALU = mybir.AluOpType
AX = mybir.AxisListType

ENGINES = ("tensor", "vector", "scalar", "sync", "gpsimd")

D = 1024
T = 2048
NH = 16
DH = 64
NI = 8
NT = 16
NG = 4
NE = 8
HF = 4096
NFI = 32
NEG_BIG = -1.0e30
QS = 0.35355339059327373
LN_EPS = 1e-5


def make_consts():
    c = {}
    c["c_ident_f"] = np.eye(128, dtype=np.float32)
    c["c_ident_b"] = np.eye(128, dtype=np.float32)
    c["c_tri_b"] = np.triu(np.ones((128, 128), np.float32), 1) * NEG_BIG
    c["c_onesrow_f"] = np.ones((1, 128), dtype=np.float32)
    c["c_onesrow_r"] = np.ones((1, 128), dtype=np.float32)
    c["c_halfcol_b"] = np.full((128, 1), 0.5, dtype=np.float32)
    c["c_onescol_b"] = np.ones((128, 1), dtype=np.float32)
    c["c_iota_row_b"] = np.arange(T, dtype=np.float32)[None, :]
    c["c_half_row_b"] = np.full((1, 512), 0.5, dtype=np.float32)
    c["c_iotacol16"] = (np.arange(16)[None, :] * 128 + np.arange(128)[:, None]).astype(np.float32)
    c["c_epscol"] = np.full((128, 1), LN_EPS, dtype=np.float32)
    sel = np.zeros((8, 8 * 128), dtype=np.float32)
    for e in range(8):
        sel[e, e * 128:(e + 1) * 128] = 1.0
    c["c_sel"] = sel
    return c


CONST_DTYPES = {
    "c_ident_f": F32, "c_ident_b": BF16, "c_tri_b": BF16,
    "c_onesrow_f": F32, "c_onesrow_r": F32R, "c_halfcol_b": BF16, "c_onescol_b": BF16,
    "c_iota_row_b": BF16, "c_half_row_b": BF16,
    "c_iotacol16": F32, "c_sel": F32, "c_epscol": F32,
}


def np_for(dt):
    import ml_dtypes
    return {F32: np.float32, BF16: ml_dtypes.bfloat16, F32R: np.float32,
            I32: np.int32}[dt]


class MK:
    def __init__(self, nbatch=2, layers=(0, 1), sublayers=("attn", "ffn")):
        self.nbatch = nbatch
        self.layers = tuple(layers)
        self.sublayers = tuple(sublayers)

    def sem(self, name):
        if name not in self.sems:
            h = self.es.enter_context(self.nc.semaphore(name))
            self.sems[name] = [h, 0]
        return self.sems[name]

    def on(self, eng, fn, inc=None, n=1, waits=()):
        w = self.pending[eng] + list(waits)
        self.pending[eng] = []
        if eng in ("vector", "scalar"):
            own = self.OWN[eng]
            c = self.sem(own)[1]
            if c > 0:
                w.append((own, c))
            if inc is None:
                inc = own
        lw = self.lastwait[eng]
        best = {}
        for t in w:
            if t is None:
                continue
            s, v = t
            if v > best.get(s, -1):
                best[s] = v
        w2 = []
        for s, v in best.items():
            if v > lw.get(s, -1) and v > 0:
                w2.append((s, v))
                lw[s] = v
        tok = None
        if inc is not None:
            c = self.sem(inc)
            c[1] += n
            tok = (inc, c[1])
        self.ops[eng].append((tuple(w2), fn, inc, n))
        self.nops += 1
        return tok

    def tok(self, sem):
        return (sem, self.sems[sem][1])

    def dma(self, queue, stream, fn, self_wait=False):
        tok = self.on(queue, fn, inc="d_" + stream, n=16)
        if self_wait:
            self.on(queue, lambda e: e.nop(), waits=[tok])
        return tok

    def flush_ynorm(self):
        for it, rc, hh, G, yb in self._defer:
            ynorm = self.on("vector", lambda e, hh=hh, G=G, yb=yb: e.tensor_tensor(
                self.yT[(hh % 2) * 64:(hh % 2) * 64 + 64, hh // 2, G * 512:(G + 1) * 512],
                self.pbank(yb, hi=64), self.bsc[0:64, :], op=ALU.mult),
                inc="sv", waits=[t for t in (it, rc) if t is not None])
            self.guards["bscw"] = ynorm
            self.pb_set(yb, ynorm)
        self._defer = []

    def lp(self, fn):
        def wrapped(e):
            with self.nc.allow_low_precision(reason="f32r row for broadcast matmul"):
                return fn(e)
        return wrapped

    def wait(self, eng, tok):
        if tok is not None:
            self.pending[eng].append(tok)

    def waits(self, eng, toks):
        for t in toks:
            self.wait(eng, t)

    OWN = {"tensor": "spe", "vector": "sv", "scalar": "ss", "sync": "sio", "gpsimd": "sdw"}

    def bar(self, engines=ENGINES):
        snap = [(s, c[1]) for s, c in self.sems.items() if c[1] > 0]
        for e in engines:
            for s, v in snap:
                self.pending[e].append((s, v))

    def replay(self, ename, eng):
        sems = self.sems
        for w, fn, inc, n in self.ops[ename]:
            for s, v in w:
                eng.wait_ge(sems[s][0], v)
            instr = fn(eng)
            if inc is not None:
                instr.then_inc(sems[inc][0], n)

    def pbank(self, k, lo=0, hi=128, w0=0, w1=512):
        if k < 4:
            return self.pbA[lo:hi, k * 512 + w0: k * 512 + w1]
        if k < 8:
            return self.pbB[lo:hi, (k - 4) * 512 + w0: (k - 4) * 512 + w1]
        raise KeyError(k)

    def pb_wait(self, k):
        g = self.bank_guard.get(k)
        if g:
            if isinstance(g[0], str):
                self.pending["tensor"].append(g)
            else:
                for t in g:
                    self.pending["tensor"].append(t)

    def pb_set(self, k, tok):
        self.bank_guard[k] = tok

    def build(self, nc, es):
        self.nc = nc
        self.es = es
        self.ops = {e: [] for e in ENGINES}
        self.pending = {e: [] for e in ENGINES}
        self._defer = []
        self.lastwait = {e: {} for e in ENGINES}
        self.sems = {}
        self.bank_guard = {}
        self.guards = {}
        self.nops = 0
        for s in ("spe", "sv", "ss", "sio", "sdw"):
            self.sem(s)

        dram = nc.dram_tensor
        self.moe_cached = set()
        self.w1c = dram("w1cache", [NE, D, HF], BF16, kind="Internal").ap()
        self.w2c = dram("w2cache", [NE, HF, D], BF16, kind="Internal").ap()
        self.hx = dram("x_in", [self.nbatch, T, D], F32, kind="ExternalInput").ap()
        self.hout = dram("x_out", [self.nbatch, T, D], F32, kind="ExternalOutput").ap()
        self.hw = {}

        def win(name, shape):
            self.hw[name] = dram(name, shape, F32, kind="ExternalInput").ap()
        if 0 in self.layers:
            if "attn" in self.sublayers:
                win("L0_attn_w_attn", [D, 3 * D]); win("L0_attn_b_attn", [3 * D])
                win("L0_attn_w_proj", [D, D]); win("L0_attn_b_proj", [D])
            if "ffn" in self.sublayers:
                win("L0_ffn_Wr", [NE, D]); win("L0_ffn_W1", [NE, D, HF])
                win("L0_ffn_b1", [NE, HF]); win("L0_ffn_W2", [NE, HF, D])
                win("L0_ffn_b2", [NE, D])
        if 1 in self.layers:
            if "attn" in self.sublayers:
                win("L1_attn_W_kqv", [D, 3 * D]); win("L1_attn_b_kqv", [3 * D])
                win("L1_attn_W_proj", [D, D]); win("L1_attn_b_proj", [D])
                win("L1_attn_slopes", [NH])
            if "ffn" in self.sublayers:
                win("L1_ffn_weight", [D, D]); win("L1_ffn_bias", [D])
        self.hc = {}
        cvals = make_consts()
        for name, dt in CONST_DTYPES.items():
            self.hc[name] = dram(name, list(cvals[name].shape), dt, kind="ExternalInput").ap()

        sb = lambda name, shape, dt: es.enter_context(nc.sbuf_tensor(name, shape, dt))

        self.R = sb("R", [128, NI, T], F32)
        self.XB = sb("XB", [128, NI, T], BF16)
        self.yT = sb("yT", [128, NI, T], BF16)
        self.rows = sb("rows", [128, T], F32)
        self.rows2 = sb("rows2", [128, 512], F32R)
        self.lnb = sb("lnb", [128, 512], F32)
        self.bsc = self.lnb
        self.kcell = sb("kcell", [1, 2], BF16)
        self.nrcell = sb("nrcell", [128, 16], BF16)
        self.bcol = sb("bcol", [128, 24], F32)
        self.bqcol = sb("bqcol", [128, 16], F32)
        self.bkcol = sb("bkcol", [128, 16], F32)
        self.bpcol = sb("bpcol", [128, 8], F32)
        self.slope_bc = sb("slope_bc", [128, 16], F32)
        self.scol = sb("scol", [128, 32], F32)
        self.b24 = sb("b24", [32, 128], F32)
        self.bfcol = sb("bfcol", [128, 8], F32)
        self.pcol = sb("pcol", [128, 4], F32)
        self.qsc = sb("qsc", [128, D // 32], F32)
        self.qscr = sb("qscr", [128, D // 32], F32)
        self.consts = {}
        for name, dt in CONST_DTYPES.items():
            self.consts[name] = sb("s" + name, list(cvals[name].shape), dt)
        ARENA = 56320
        pad = (-nc.sbuf_base) % 32
        base0 = nc.sbuf_base + pad
        self.arena = sb("arena", [128, ARENA + pad], mybir.dt.uint8)
        at = lambda name, shape, dt, off: nc.alloc_sbuf_tensor_at(name, shape, dt, offset=base0 + off)
        self.vg = at("vg", [128, NT, 4 * 65], BF16, 0)
        self.Qh = at("Qh", [128, 2, T], BF16, 8320)
        self.Kh = at("Kh", [128, 2, T], BF16, 16512)
        self.asb = at("asb", [128, 4, 512], BF16, 24704)
        self.lnsq = at("lnsq", [128, T], BF16, 24704)
        self.wqk = at("wqk", [128, 2, 2, NI, 128], BF16, 28800)
        self.wv = at("wv", [128, NI, 256], BF16, 36992)
        self.vbias = at("vbias", [128, D], F32, 41088)
        self.wpj = at("wpj", [128, 2, NI, 128], BF16, 28800)
        self.tload = at("tload", [128, 2, D], F32, 0)
        self.w1s = at("w1s", [128, 2, NI, 256], BF16, 0)
        self.w2s = at("w2s", [128, 2, 16, 128], BF16, 8192)
        self.cbe = at("cbe", [128, 2, 512], F32, 16384)
        self.wrt = at("wrt", [128, NI, 8], F32, 20480)
        self.probs = at("probs", [128, NT, NE], F32, 20736)
        self.ptmp = at("ptmp", [128, NT, NE], F32, 21248)
        self.comb = at("comb", [128, NT, NE], F32, 21760)
        self.b1col = at("b1col", [128, NE, NFI], F32, 22272)
        self.b2col = at("b2col", [128, NE, NI], F32, 23296)
        self.h2 = at("h2", [128, NI, T], BF16, 23552)
        self.wdT = at("wdT", [128, NI, D], BF16, 0)
        self.wdc = nc.dram_tensor("wdcache", [128, NI * D], BF16, kind="Internal").ap()
        self.dscr = nc.dram_tensor("dbcast", [2, 1, 512], F32, kind="Internal").ap()
        self.qnat = at("qnat", [128, 2, D], F32, 16384)
        self.qy = at("qy", [128, D], F32, 24576)
        self.qa = at("qa", [128, D], F32, 28672)
        self.qd = at("qd", [128, D], F32, 32768)
        self.qm2 = at("qm2", [128, D], F32, 36864)
        self.qm4 = at("qm4", [128, D], F32, 40960)
        self.qzi = at("qzi", [128, D], F32, 45056)
        self.qout = at("qout", [128, D], BF16, 32768)
        self.pbA = es.enter_context(nc.psum_tensor("pbA", [128, 2048], F32))
        self.pbB = es.enter_context(nc.psum_tensor("pbB", [128, 2048], F32))
        self.ptb = self.pbB[:, 1536:2048].bitcast(BF16)

        self.plan_consts_load()
        for b in range(self.nbatch):
            self.plan_batch(b)
        fin = [(s, c[1]) for s, c in self.sems.items() if s.startswith("d_") and c[1] > 0]
        self.on("sync", lambda e: e.nop(), waits=fin)

        block = es.enter_context(nc.Block())
        for ename in ENGINES:
            getattr(block, ename)(lambda eng, _e=ename: self.replay(_e, eng))
        return nc

    def in_map(self, x_shard, weights):
        m = {"x_in": np.ascontiguousarray(x_shard, dtype=np.float32)}
        for k in self.hw:
            m[k] = np.ascontiguousarray(np.asarray(weights[k]), dtype=np.float32)
        cv = make_consts()
        for k, dt in CONST_DTYPES.items():
            m[k] = np.ascontiguousarray(cv[k].astype(np_for(dt)))
        return m

    def plan_consts_load(self):
        for name in CONST_DTYPES:
            self.dma("sync", "consts", lambda e, d=self.consts[name], s=self.hc[name]:
                     e.dma_start(d[:], s))
        self.bar()

    def plan_batch(self, b):
        self.plan_prologue(b)
        for L in self.layers:
            if "attn" in self.sublayers:
                self.plan_attn(b, L)
                self.plan_ln(b)
            if "ffn" in self.sublayers:
                if L == 0:
                    self.plan_moe(b)
                else:
                    self.plan_qffn(b)
                self.plan_ln(b, final=(L == self.layers[-1]))
        self.plan_epilogue(b)

    def transpose_f32(self, src_ap_fn, dst_ap_fn, bank, eng):
        idf = self.consts["c_ident_f"]
        self.pb_wait(bank)
        pe = self.on("tensor", lambda e: e.matmul(
            self.pbank(bank, w1=128), src_ap_fn(), idf[:], is_transpose=True,
            start=True, stop=True), inc="spe")
        self.wait(eng, pe)
        if eng == "vector":
            ev = self.on("vector", lambda e: e.tensor_copy(dst_ap_fn(), self.pbank(bank, w1=128)), inc="sv")
        else:
            ev = self.on("scalar", lambda e: e.copy(dst_ap_fn(), self.pbank(bank, w1=128)), inc="ss")
        self.pb_set(bank, ev)
        self._last_tp = pe
        return ev

    def plan_prologue(self, b):
        import os as _os
        _nt = int(_os.environ.get("MK_DEBUG_NT", str(NT)))
        self.wait("sync", self.guards.pop("wdc_store", None))
        guards = {}
        for tau in range(_nt):
            par = tau % 2
            g = guards.get(par)
            if g:
                self.pending["sync"].append(g)
            ld = self.dma("sync", f"x{par}", lambda e, p=par, t=tau: e.dma_start(
                self.tload[:, p, :], self.hx[b, t * 128:(t + 1) * 128, :]))
            self.wait("tensor", ld)
            for i in range(NI):
                bank = i % 2
                eng = "vector" if (i % 4) < 2 else "scalar"
                ev = self.transpose_f32(
                    lambda p=par, i=i: self.tload[:, p, i * 128:(i + 1) * 128],
                    lambda i=i, t=tau: self.R[:, i, t * 128:(t + 1) * 128],
                    bank, eng)
                eng2 = "scalar" if eng == "vector" else "vector"
                self.wait(eng2, ev)
                if eng2 == "vector":
                    self.on("vector", lambda e, i=i, t=tau: e.tensor_copy(
                        self.XB[:, i, t * 128:(t + 1) * 128],
                        self.R[:, i, t * 128:(t + 1) * 128]), inc="sv")
                else:
                    self.on("scalar", lambda e, i=i, t=tau: e.copy(
                        self.XB[:, i, t * 128:(t + 1) * 128],
                        self.R[:, i, t * 128:(t + 1) * 128]), inc="ss")
            guards[par] = self.tok("spe")
        self.bar()

    def plan_epilogue(self, b):
        st_guard = {}
        for tau in range(NT):
            par = tau % 2
            evs = []
            for i in range(NI):
                bank = i % 2
                eng = "vector" if (i % 4) < 2 else "scalar"
                g = st_guard.get(par)
                if g:
                    self.pending[eng].append(g)
                ev = self.transpose_f32(
                    lambda i=i, t=tau: self.R[:, i, t * 128:(t + 1) * 128],
                    lambda p=par, i=i: self.qnat[:, p, i * 128:(i + 1) * 128],
                    bank, eng)
                evs.append(ev)
            self.waits("sync", evs)
            st = self.dma("sync", f"out{par}", lambda e, p=par, t=tau: e.dma_start(
                self.hout[b, t * 128:(t + 1) * 128, :], self.qnat[:, p, :]))
            st_guard[par] = st
        self.bar()

    def bias_cols(self, hvec, n, dst, bank=0):
        nr = n // 128
        assert nr <= 32
        g = self.guards.get("b24")
        if g:
            self.pending["sync"].append(g)
        ld = self.dma("sync", "misc", lambda e, nr=nr: e.dma_start(
            self.b24[0:nr, :], hvec.rearrange("(j p) -> j p", p=128)), self_wait=True)
        self.wait("tensor", ld)
        idf = self.consts["c_ident_f"]
        self.pb_wait(bank)
        pe = self.on("tensor", lambda e, nr=nr: e.matmul(
            self.pbank(bank, w1=nr), self.b24[0:nr, :], idf[0:nr, 0:nr],
            is_transpose=True, start=True, stop=True), inc="spe")
        self.guards["b24"] = pe
        self.wait("vector", pe)
        ev = self.on("vector", lambda e, nr=nr: e.tensor_copy(dst[:, 0:nr], self.pbank(bank, w1=nr)), inc="sv")
        self.pb_set(bank, ev)
        return ev

    def plan_attn(self, b, L):
        alibi = (L == 1)
        W = self.hw["L1_attn_W_kqv" if alibi else "L0_attn_w_attn"]
        Bv = self.hw["L1_attn_b_kqv" if alibi else "L0_attn_b_attn"]
        WP = self.hw["L1_attn_W_proj" if alibi else "L0_attn_w_proj"]
        BP = self.hw["L1_attn_b_proj" if alibi else "L0_attn_b_proj"]
        qoff = D if alibi else 0
        koff = 0 if alibi else D
        voff = 2 * D
        onesr = self.consts["c_onesrow_f"]

        ev = self.bias_cols(Bv, 3 * D, self.bcol, bank=0)
        self.wait("vector", ev)
        for h in range(NH):
            jq, rq = divmod(qoff + 64 * h, 128)
            jk, rk = divmod(koff + 64 * h, 128)
            self.on("vector", lambda e, h=h, jq=jq, rq=rq: e.tensor_scalar(
                self.bqcol[0:64, h:h + 1], self.bcol[rq:rq + 64, jq:jq + 1], QS, None,
                op0=ALU.mult), inc="sv")
            self.on("vector", lambda e, h=h, jk=jk, rk=rk: e.tensor_scalar(
                self.bkcol[0:64, h:h + 1], self.bcol[rk:rk + 64, jk:jk + 1], QS, None,
                op0=ALU.mult), inc="sv")
        self.bias_cols(BP, D, self.bpcol, bank=1)
        ldv = self.dma("sync", "misc", lambda e: e.dma_start(
            self.rows[0:1, 0:D], Bv[voff:voff + D].rearrange("(o f) -> o f", o=1)), self_wait=True)
        self.wait("tensor", ldv)
        for jj in range(8):
            bank = 2 + jj % 2
            self.pb_wait(bank)
            pe = self.on("tensor", lambda e, jj=jj, bank=bank: e.matmul(
                self.pbank(bank, w1=128), onesr[:],
                self.rows[0:1, jj * 128:(jj + 1) * 128],
                start=True, stop=True), inc="spe")
            self.wait("vector", pe)
            evv = self.on("vector", lambda e, jj=jj, bank=bank: e.tensor_copy(
                self.vbias[:, jj * 128:(jj + 1) * 128], self.pbank(bank, w1=128)), inc="sv")
            self.pb_set(bank, evv)
        if alibi:
            self.pending["sync"].append(self.tok("spe"))
            lds = self.dma("sync", "misc", lambda e: e.dma_start(
                self.rows[0:1, 0:16], self.hw["L1_attn_slopes"].rearrange("(o s) -> o s", o=1)),
                self_wait=True)
            self.wait("tensor", lds)
            self.pb_wait(4)
            pe = self.on("tensor", lambda e: e.matmul(
                self.pbank(4, w1=16), onesr[:], self.rows[0:1, 0:16],
                start=True, stop=True), inc="spe")
            self.wait("vector", pe)
            self.on("vector", lambda e: e.tensor_copy(self.slope_bc[:], self.pbank(4, w1=16)), inc="sv")
            e2 = self.on("vector", lambda e: e.tensor_scalar(
                self.nrcell[:], self.pbank(4, w1=16), 1.0, 0.0, op0=ALU.mult, op1=ALU.max), inc="sv")
            self.pb_set(4, e2)
        self.on("vector", lambda e: e.memset(self.Kh[64:65, :], 1.0), inc="sv")
        self.bar()

        _nh = NH
        halfc = self.consts["c_halfcol_b"]
        hctx = {}
        for h in range(_nh + 1):
            if h < _nh:
                p = h % 2
                par = (h // 2) % 2
                hp = h % 2
                if hp == 0:
                    g = self.guards.get(("wqk", par))
                    if g:
                        self.pending["gpsimd"].append(g)
                    self.dma("gpsimd", f"wqk{par}", lambda e, h=h, pq=par: e.dma_start(
                        self.wqk[:, pq, 0, :, :],
                        W[:, qoff + 64 * h: qoff + 64 * (h + 2)].rearrange("(i p) f -> p i f", p=128)))
                    lk = self.dma("gpsimd", f"wqk{par}", lambda e, h=h, pq=par: e.dma_start(
                        self.wqk[:, pq, 1, :, :],
                        W[:, koff + 64 * h: koff + 64 * (h + 2)].rearrange("(i p) f -> p i f", p=128)))
                    self.guards[("wqk_tok", par)] = lk
                lk = self.guards[("wqk_tok", par)]
                if h % 4 == 0:
                    g = self.guards.get("wv")
                    if g:
                        self.pending["gpsimd"].append(g)
                    lv = self.dma("gpsimd", "wv", lambda e, h=h: e.dma_start(
                        self.wv[:],
                        W[:, voff + 64 * h: voff + 64 * (h + 4)].rearrange("(i p) f -> p i f", p=128)))
                sc = None
                if alibi:
                    sc = self.on("vector", lambda e, h=h, p=p: e.tensor_scalar(
                        self.scol[:, p * 16:(p + 1) * 16], self.consts["c_iotacol16"][:],
                        self.slope_bc[:, h:h + 1], None, op0=ALU.mult),
                        inc="sv", waits=[self.tok("ss")])
                hctx[h] = {"p": p, "rowtok": None, "sc": sc}
                qk_evacs = []
                for qk in (0, 1):
                    col = self.bqcol if qk == 0 else self.bkcol
                    dst = self.Qh if qk == 0 else self.Kh
                    for g4 in range(NG):
                        self.pb_wait(g4)
                    self.wait("tensor", lk)
                    for i in range(NI):
                        for g4 in range(NG):
                            self.on("tensor", lambda e, pq=par, qk=qk, i=i, g4=g4, hp=hp: e.matmul(
                                self.pbank(g4, hi=64), self.wqk[:, pq, qk, i, hp * 64:(hp + 1) * 64],
                                self.XB[:, i, g4 * 512:(g4 + 1) * 512],
                                start=(i == 0), stop=(i == NI - 1)),
                                inc="spe" if (i == NI - 1 and g4 == NG - 1) else None)
                    mm = self.tok("spe")
                    self.guards[("wqk", par)] = mm
                    self.wait("vector", mm)
                    for g4 in range(NG):
                        evq = self.on("vector", lambda e, h=h, p=p, g4=g4, dst=dst, col=col: e.tensor_scalar(
                            dst[0:64, p, g4 * 512:(g4 + 1) * 512], self.pbank(g4, hi=64),
                            QS, col[0:64, h:h + 1], op0=ALU.mult, op1=ALU.add), inc="sv")
                        self.pb_set(g4, evq)
                        qk_evacs.append(evq)
            if h > 0:
                hh = h - 1
                c = hctx[hh]
                p1 = c["p"]
                hloc = hh % 4
                self.wait("tensor", self.guards.get("vg_ready"))
                for G in range(NG):
                    yb = 6 + G % 2
                    self.pb_wait(yb)
                    ilist = list(range(4 * G + 3, -1, -1))
                    if alibi:
                        sched = [(idx,) for idx in range(len(ilist))]
                    elif G == 0:
                        sched = [(0,), (1,), (2,), (3,)]
                    else:
                        sched = [(idx,) for idx in range(4)] + [(a, a + 1) for a in range(4, len(ilist), 2)]
                    for grp in sched:
                        abanks = []
                        for gidx, idx in enumerate(grp):
                            i = ilist[idx]
                            diag = i - 4 * G
                            woff = 128 * diag if diag >= 0 else 0
                            wlen = 512 - woff
                            ab = 4 + (idx % 2)
                            abanks.append((idx, i, diag, woff, wlen, ab))
                            self.pb_wait(ab)
                            self.wait("tensor", c["rowtok"])
                            if alibi:
                                self.wait("tensor", c["sc"])
                            self.on("tensor", lambda e, i=i, G=G, ab=ab, woff=woff, wlen=wlen, diag=diag, p1=p1: e.matmul(
                                self.pbank(ab, w0=woff, w1=woff + wlen),
                                self.Kh[0:65, p1, i * 128:(i + 1) * 128],
                                self.Qh[0:65, p1, G * 512 + woff: G * 512 + woff + wlen],
                                start=True, stop=(diag < 0)), inc=None if diag >= 0 else "spe")
                            if diag >= 0:
                                self.on("tensor", lambda e, ab=ab, woff=woff: e.matmul(
                                    self.pbank(ab, w0=woff, w1=woff + 128),
                                    self.consts["c_tri_b"][:], self.consts["c_ident_b"][:],
                                    start=False, stop=True), inc="spe")
                        qkmm = self.tok("spe")
                        if len(grp) == 1:
                            idx, i, diag, woff, wlen, ab = abanks[0]
                            aslot = idx % 4
                            slot_g = self.guards.get(("asb", aslot))
                            if slot_g:
                                self.pending["scalar"].append(slot_g)
                            mstok = None
                            if idx == 0 and woff > 0:
                                if slot_g:
                                    self.pending["vector"].append(slot_g)
                                mstok = self.on("vector", lambda e, aslot=aslot, woff=woff: e.memset(
                                    self.asb[:, aslot, 0:woff], 0.0), inc="sv")
                            self.wait("scalar", qkmm)
                            bias = (lambda ee, i=i, p1=p1: None)
                            if alibi:
                                ex = self.on("scalar", lambda e, i=i, ab=ab, aslot=aslot, woff=woff, wlen=wlen, p1=p1:
                                             e.activation(self.asb[:, aslot, woff:woff + wlen],
                                                          self.pbank(ab, w0=woff, w1=woff + wlen),
                                                          AF.Exp, bias=self.scol[:, p1 * 16 + i:p1 * 16 + i + 1],
                                                          scale=1.0), inc="ss")
                            else:
                                ex = self.on("scalar", lambda e, i=i, ab=ab, aslot=aslot, woff=woff, wlen=wlen:
                                             e.activation(self.asb[:, aslot, woff:woff + wlen],
                                                          self.pbank(ab, w0=woff, w1=woff + wlen),
                                                          AF.Exp, bias=0.0, scale=1.0), inc="ss")
                            self.pb_set(ab, ex)
                            pairs = [(idx, i, woff, wlen, ab, aslot, ex, mstok)]
                        else:
                            (idxA, iA, _, _, _, abA), (idxB, iB, _, _, _, abB) = abanks
                            aslot = idxA % 4
                            assert abA == 4 and abB == 5 and aslot in (0, 2)
                            for s_ in (aslot, aslot + 1):
                                slot_g = self.guards.get(("asb", s_))
                                if slot_g:
                                    self.pending["scalar"].append(slot_g)
                            self.wait("scalar", qkmm)
                            ex = self.on("scalar", lambda e, aslot=aslot:
                                         e.activation(self.asb[:, aslot, :].rearrange("p a b -> p (a b)") if False else self.asb[:, aslot:aslot + 2, :].rearrange("p a b -> p (a b)"),
                                                      self.pbB[:, 0:1024],
                                                      AF.Exp, bias=0.0, scale=1.0), inc="ss")
                            self.pb_set(4, ex)
                            self.pb_set(5, ex)
                            pairs = [(idxA, iA, 0, 512, 4, aslot, ex, None),
                                     (idxB, iB, 0, 512, 5, aslot + 1, ex, None)]
                        for idx, i, woff, wlen, ab, aslot, ex, mstok in pairs:
                            self.wait("tensor", ex)
                            self.wait("tensor", mstok)
                            pw = 0 if idx == 0 else woff
                            pv = self.on("tensor", lambda e, i=i, idx=idx, aslot=aslot, hloc=hloc, pw=pw, yb=yb: e.matmul(
                                self.pbank(yb, hi=65, w0=pw, w1=512),
                                self.vg[:, i, hloc * 65:(hloc + 1) * 65],
                                self.asb[:, aslot, pw:512],
                                start=(idx == 0), stop=(i == 0)), inc="spe")
                            self.guards[("asb", aslot)] = pv
                    pvmm = self.tok("spe")
                    if G == NG - 1 or G == (1 if hh % 2 else 0):
                        self.wait("scalar", pvmm)
                        rc = self.on("scalar", self.lp(lambda e, yb=yb: e.activation(
                            self.rows2[0:1, 0:512], self.pbank(yb, lo=64, hi=65),
                            AF.Ln, bias=0.0, scale=1.0)), inc="ss")
                        self.on("scalar", self.lp(lambda e: e.activation(
                            self.rows2[0:1, 0:512], self.rows2[0:1, 0:512],
                            AF.Exp, bias=0.0, scale=-1.0)), inc="ss")
                        rce = self.tok("ss")
                    else:
                        self.wait("vector", pvmm)
                        rc = self.on("vector", self.lp(lambda e, yb=yb: e.reciprocal(
                            self.rows2[0:1, 0:512], self.pbank(yb, lo=64, hi=65))), inc="sv")
                        rce = rc
                    ab_bc = 4 + G % 2
                    self.wait("tensor", rce)
                    self.pb_wait(ab_bc)
                    bc = self.on("tensor", lambda e, ab_bc=ab_bc: e.matmul(
                        self.pbank(ab_bc, hi=64), self.consts["c_onesrow_r"][0:1, 0:64],
                        self.rows2[0:1, 0:512], start=True, stop=True), inc="spe")
                    self.wait("vector", bc)
                    sc2 = self.on("vector", lambda e, ab_bc=ab_bc: e.tensor_copy(
                        self.bsc[0:64, :], self.pbank(ab_bc, hi=64)), inc="sv")
                    self.pb_set(ab_bc, sc2)
                    if G == NG - 1:
                        self._defer.append((sc2, rc, hh, G, yb))
                    else:
                        self.wait("vector", rc)
                        ynorm = self.on("vector", lambda e, hh=hh, G=G, yb=yb: e.tensor_tensor(
                            self.yT[(hh % 2) * 64:(hh % 2) * 64 + 64, hh // 2, G * 512:(G + 1) * 512],
                            self.pbank(yb, hi=64), self.bsc[0:64, :], op=ALU.mult), inc="sv")
                        self.guards["bscw"] = ynorm
                        self.pb_set(yb, ynorm)
            if h < _nh and h % 4 == 0:
                self.plan_vgen(b, lv, first=(h == 0), h0=h)
            if h < _nh:
                self.waits("gpsimd", qk_evacs)
                sk = self.on("gpsimd", lambda e, p=p: e.tensor_tensor(
                    self.Qh[64:128, p, :], self.Kh[0:64, p, :], self.Kh[0:64, p, :], op=ALU.mult), inc="sdw")
                sqq = self.on("gpsimd", lambda e, p=p, h=h: e.tensor_tensor(
                    self.yT[(h % 2) * 64:(h % 2) * 64 + 64, h // 2, :],
                    self.Qh[0:64, p, :], self.Qh[0:64, p, :], op=ALU.mult), inc="sdw")
                self.wait("tensor", sk)
                for g4 in range(NG):
                    self.pb_wait(g4)
                    self.on("tensor", lambda e, g4=g4, p=p: e.matmul(
                        self.pbank(g4, lo=0, hi=1), halfc[64:128, :],
                        self.Qh[64:128, p, g4 * 512:(g4 + 1) * 512], start=True, stop=True),
                        inc="spe" if g4 == NG - 1 else None)
                mmk = self.tok("spe")
                self.wait("vector", mmk)
                rmx = self.on("vector", lambda e, p=p: e.tensor_reduce(
                    self.kcell[0:1, p:p + 1], self.pbA[0:1, :], axis=AX.X, op=ALU.max), inc="sv")
                self.wait("tensor", rmx)
                self.wait("tensor", sqq)
                qb = (h % 2) * 64
                for g4 in range(NG):
                    self.on("tensor", lambda e, g4=g4, h=h, qb=qb: e.matmul(
                        self.pbank(g4, lo=0, hi=1), halfc[qb:qb + 64, :],
                        self.yT[qb:qb + 64, h // 2, g4 * 512:(g4 + 1) * 512], start=True, stop=False))
                    self.on("tensor", lambda e, g4=g4, p=p: e.matmul(
                        self.pbank(g4, lo=0, hi=1), self.kcell[0:1, p:p + 1],
                        self.consts["c_half_row_b"][0:1, 0:512],
                        start=False, stop=(not alibi)),
                        inc=None if alibi else ("spe" if g4 == NG - 1 else None))
                    if alibi:
                        self.on("tensor", lambda e, g4=g4, h=h: e.matmul(
                            self.pbank(g4, lo=0, hi=1), self.nrcell[0:1, h:h + 1],
                            self.consts["c_iota_row_b"][0:1, g4 * 512:(g4 + 1) * 512],
                            start=False, stop=True),
                            inc="spe" if g4 == NG - 1 else None)
                mmq = self.tok("spe")
                self.wait("scalar", mmq)
                rowtok = self.on("scalar", lambda e, p=p: e.activation(
                    self.Qh[64:65, p, :], self.pbA[0:1, :], AF.Identity,
                    bias=0.0, scale=-1.0), inc="ss")
                for g4 in range(NG):
                    self.pb_set(g4, rowtok)
                hctx[h]["rowtok"] = rowtok
            self.flush_ynorm()
        self.bar()
        for ft in range(NI):
            par = ft % 2
            g = self.guards.get(("wpj", par))
            if g:
                self.pending["gpsimd"].append(g)
            lp = self.dma("gpsimd", f"wpj{par}", lambda e, ft=ft, p=par: e.dma_start(
                self.wpj[:, p, :, :],
                WP[:, ft * 128:(ft + 1) * 128].rearrange("(i p) f -> p i f", p=128)))
            for tg in range(NG):
                self.pb_wait(tg)
            self.wait("tensor", lp)
            for ic in range(NI):
                for tg in range(NG):
                    self.on("tensor", lambda e, ft=ft, p=par, ic=ic, tg=tg: e.matmul(
                        self.pbank(tg), self.wpj[:, p, ic, :],
                        self.yT[:, ic, tg * 512:(tg + 1) * 512],
                        start=(ic == 0), stop=(ic == NI - 1)),
                        inc="spe" if (ic == NI - 1 and tg == NG - 1) else None)
            mm = self.tok("spe")
            self.guards[("wpj", par)] = mm
            self.wait("vector", mm)
            for tg in range(NG):
                evp = self.on("vector", lambda e, ft=ft, tg=tg: e.scalar_tensor_tensor(
                    self.R[:, ft, tg * 512:(tg + 1) * 512], self.pbank(tg),
                    self.bpcol[:, ft:ft + 1], self.R[:, ft, tg * 512:(tg + 1) * 512],
                    op0=ALU.add, op1=ALU.add), inc="sv")
                self.pb_set(tg, evp)
        self.bar()

    def plan_vgen(self, b, lv_tok, first, h0=0):
        if first:
            ms = self.on("vector", lambda e: e.memset(
                self.vg.rearrange("p t (hh c) -> p t hh c", c=65)[:, :, :, 64:65], 1.0), inc="sv")
            self.wait("tensor", ms)
        ev = None
        for tau in range(NT):
            bank = tau % 2
            self.pb_wait(bank)
            self.wait("tensor", lv_tok)
            for i in range(NI):
                self.on("tensor", lambda e, tau=tau, i=i, bank=bank: e.matmul(
                    self.pbank(bank, w1=256), self.XB[:, i, tau * 128:(tau + 1) * 128],
                    self.wv[:, i, 0:256], start=(i == 0), stop=(i == NI - 1)),
                    inc="spe" if i == NI - 1 else None)
            mm = self.tok("spe")
            self.wait("vector", mm)
            ev = self.on("vector", lambda e, tau=tau, bank=bank, h0=h0: e.tensor_tensor(
                self.vg[:, tau, :].rearrange("p (hh c) -> p hh c", c=65)[:, :, 0:64],
                self.pbank(bank, w1=256).rearrange("p (hh c) -> p hh c", c=64),
                self.vbias[:, h0 * 64:(h0 + 4) * 64].rearrange("p (hh c) -> p hh c", c=64),
                op=ALU.add), inc="sv")
            self.pb_set(bank, ev)
        self.guards["vg_ready"] = ev
        self.guards["wv"] = self.tok("spe")

    def plan_ln(self, b, final=False):
        onescb = self.consts["c_onescol_b"]
        onesr = self.consts["c_onesrow_f"]
        self.bar()
        prev_sq = None
        for i in range(NI):
            srcx = self.on("vector", lambda e, i=i: e.tensor_copy(
                self.XB[:, i, :], self.R[:, i, :]), inc="sv")
            if prev_sq:
                self.pending["scalar"].append(prev_sq)
            srcs = self.on("scalar", lambda e, i=i: e.activation(
                self.lnsq[:, :], self.R[:, i, :], AF.Square, bias=0.0, scale=1.0), inc="ss")
            self.wait("tensor", srcx)
            self.wait("tensor", srcs)
            for g4 in range(NG):
                if i == 0:
                    self.pb_wait(g4)
                    self.pb_wait(4 + g4)
                self.on("tensor", lambda e, g4=g4, i=i: e.matmul(
                    self.pbank(g4, lo=0, hi=1), onescb[:],
                    self.XB[:, i, g4 * 512:(g4 + 1) * 512],
                    start=(i == 0), stop=(i == NI - 1)))
                self.on("tensor", lambda e, g4=g4, i=i: e.matmul(
                    self.pbank(4 + g4, lo=0, hi=1), onescb[:],
                    self.lnsq[:, g4 * 512:(g4 + 1) * 512],
                    start=(i == 0), stop=(i == NI - 1)),
                    inc="spe" if g4 == NG - 1 else None)
            prev_sq = self.tok("spe")
        mm = self.tok("spe")
        self.wait("scalar", mm)
        evt0 = self.on("scalar", lambda e: e.activation(
            self.rows[0:1, :], self.pbA[0:1, :], AF.Identity,
            bias=0.0, scale=1.0 / D), inc="ss")
        for g4 in range(NG):
            self.pb_set(g4, evt0)
        self.wait("vector", evt0)
        v1 = self.on("vector", lambda e: e.tensor_tensor(
            self.rows[64:65, :], self.rows[0:1, :], self.rows[0:1, :], op=ALU.mult), inc="sv")
        self.wait("vector", mm)
        v2 = self.on("vector", lambda e: e.scalar_tensor_tensor(
            self.rows[32:33, :], self.pbB[0:1, :], 1.0 / D, self.rows[64:65, :],
            op0=ALU.mult, op1=ALU.subtract), inc="sv")
        for g4 in range(NG):
            self.pb_set(4 + g4, v2)
        mmb = None
        for g4 in range(NG):
            self.wait("scalar", v2)
            self.wait("scalar", mmb)
            self.on("scalar", self.lp(lambda e, g4=g4: e.activation(
                self.rows2[0:1, 0:512], self.rows[32:33, g4 * 512:(g4 + 1) * 512], AF.Ln,
                bias=self.consts["c_epscol"][0:1, 0:1], scale=1.0)), inc="ss")
            s2 = self.on("scalar", self.lp(lambda e: e.activation(
                self.rows2[0:1, 0:512], self.rows2[0:1, 0:512], AF.Exp, bias=0.0, scale=-0.5)), inc="ss")
            self.wait("tensor", s2)
            self.pb_wait(4)
            self.on("tensor", lambda e, g4=g4: e.matmul(
                self.pbank(4), onesr[:],
                self.rows[0:1, g4 * 512:(g4 + 1) * 512],
                start=True, stop=True))
            self.pb_wait(5)
            self.on("tensor", lambda e: e.matmul(
                self.pbank(5), self.consts["c_onesrow_r"][:],
                self.rows2[0:1, 0:512],
                start=True, stop=True), inc="spe")
            mmb = self.tok("spe")
            self.wait("vector", mmb)
            t2 = None
            for i in range(NI):
                self.on("vector", lambda e, i=i, g4=g4: e.tensor_tensor(
                    self.lnb[:], self.R[:, i, g4 * 512:(g4 + 1) * 512], self.pbank(4),
                    op=ALU.subtract), inc="sv")
                t2 = self.on("vector", lambda e, i=i, g4=g4: e.tensor_tensor(
                    self.R[:, i, g4 * 512:(g4 + 1) * 512], self.lnb[:], self.pbank(5),
                    op=ALU.mult), inc="sv")
                if not final:
                    self.wait("scalar", t2)
                    self.on("scalar", lambda e, i=i, g4=g4: e.copy(
                        self.XB[:, i, g4 * 512:(g4 + 1) * 512],
                        self.R[:, i, g4 * 512:(g4 + 1) * 512]), inc="ss")
            self.pb_set(4, t2)
            self.pb_set(5, t2)
        self.bar()

    def plan_moe(self, b):
        Wr = self.hw["L0_ffn_Wr"]; W1 = self.hw["L0_ffn_W1"]; W2 = self.hw["L0_ffn_W2"]
        B1 = self.hw["L0_ffn_b1"]; B2 = self.hw["L0_ffn_b2"]
        idf = self.consts["c_ident_f"]
        selc = self.consts["c_sel"]
        for i in range(NI):
            g = self.guards.get("b24")
            if g:
                self.pending["sync"].append(g)
            ldi = self.dma("sync", "misc", lambda e, i=i: e.dma_start(
                self.b24[0:8, :], Wr[:, i * 128:(i + 1) * 128]), self_wait=True)
            self.wait("tensor", ldi)
            self.pb_wait(0)
            pe = self.on("tensor", lambda e: e.matmul(
                self.pbank(0, w1=8), self.b24[0:8, :], idf[0:8, 0:8],
                is_transpose=True, start=True, stop=True), inc="spe")
            self.guards["b24"] = pe
            self.wait("vector", pe)
            evw = self.on("vector", lambda e, i=i: e.tensor_copy(
                self.wrt[:, i, :], self.pbank(0, lo=0, hi=128, w1=8)), inc="sv")
            self.pb_set(0, evw)
        for e_ in range(NE):
            self.bias_cols(B1[e_, :], HF, self.b1col[:, e_, :], bank=1)
            self.bias_cols(B2[e_, :], D, self.b2col[:, e_, :], bank=2)
        self.bar()
        for tau in range(NT):
            self.pb_wait(3)
            for i in range(NI):
                self.on("tensor", lambda e, tau=tau, i=i: e.matmul(
                    self.pbank(3, w1=8), self.R[:, i, tau * 128:(tau + 1) * 128],
                    self.wrt[:, i, :], start=(i == 0), stop=(i == NI - 1)),
                    inc="spe" if i == NI - 1 else None)
            mm = self.tok("spe")
            self.wait("vector", mm)
            self.on("vector", lambda e, tau=tau: e.tensor_reduce(
                self.pcol[:, 0:1], self.pbank(3, w1=8), axis=AX.X, op=ALU.max), inc="sv")
            v1b = self.on("vector", lambda e: e.tensor_scalar(
                self.pcol[:, 1:2], self.pcol[:, 0:1], -1.0, None, op0=ALU.mult), inc="sv")
            self.wait("scalar", v1b)
            s1 = self.on("scalar", lambda e, tau=tau: e.activation(
                self.ptmp[:, tau, :], self.pbank(3, w1=8), AF.Exp,
                bias=self.pcol[:, 1:2], scale=1.0), inc="ss")
            self.pb_set(3, s1)
            self.wait("vector", s1)
            self.on("vector", lambda e, tau=tau: e.tensor_reduce(
                self.pcol[:, 2:3], self.ptmp[:, tau, :], axis=AX.X, op=ALU.add), inc="sv")
            self.on("vector", lambda e: e.reciprocal(
                self.pcol[:, 3:4], self.pcol[:, 2:3]), inc="sv")
            self.on("vector", lambda e, tau=tau: e.tensor_scalar(
                self.probs[:, tau, :], self.ptmp[:, tau, :], self.pcol[:, 3:4], None,
                op0=ALU.mult), inc="sv")
            self.on("vector", lambda e, tau=tau: e.tensor_reduce(
                self.pcol[:, 0:1], self.probs[:, tau, :], axis=AX.X, op=ALU.max), inc="sv")
            self.on("vector", lambda e, tau=tau: e.tensor_scalar(
                self.ptmp[:, tau, :], self.probs[:, tau, :], self.pcol[:, 0:1], None,
                op0=ALU.subtract), inc="sv")
            self.on("vector", lambda e, tau=tau: e.tensor_scalar(
                self.ptmp[:, tau, :], self.ptmp[:, tau, :], 0.0, None, op0=ALU.is_ge), inc="sv")
            self.on("vector", lambda e, tau=tau: e.scalar_tensor_tensor(
                self.ptmp[:, tau, :], self.ptmp[:, tau, :], -2.0, self.probs[:, tau, :],
                op0=ALU.mult, op1=ALU.add), inc="sv")
            self.on("vector", lambda e, tau=tau: e.tensor_reduce(
                self.pcol[:, 1:2], self.ptmp[:, tau, :], axis=AX.X, op=ALU.max), inc="sv")
            self.on("vector", lambda e, tau=tau: e.tensor_scalar(
                self.ptmp[:, tau, :], self.probs[:, tau, :], self.pcol[:, 1:2], None,
                op0=ALU.subtract), inc="sv")
            self.on("vector", lambda e, tau=tau: e.tensor_scalar(
                self.ptmp[:, tau, :], self.ptmp[:, tau, :], 0.0, None, op0=ALU.is_ge), inc="sv")
            v10 = self.on("vector", lambda e, tau=tau: e.tensor_tensor(
                self.comb[:, tau, :], self.probs[:, tau, :], self.ptmp[:, tau, :],
                op=ALU.mult), inc="sv")
            self.wait("tensor", v10)
            self.pb_wait(3)
            pt = self.on("tensor", lambda e, tau=tau: e.matmul(
                self.pbank(3, lo=0, hi=8, w1=128), self.comb[:, tau, :], idf[:],
                is_transpose=True, start=True, stop=True), inc="spe")
            self.wait("vector", pt)
            evt = self.on("vector", lambda e, tau=tau: e.tensor_copy(
                self.rows[0:8, tau * 128:(tau + 1) * 128], self.pbank(3, lo=0, hi=8, w1=128)), inc="sv")
            self.pb_set(3, evt)
        self.bar()
        yflat = self.yT.rearrange("p a t -> p (a t)")
        h2flat = self.h2.rearrange("p a t -> p (a t)")

        def h_ap(fi, tg):
            base = fi * 1024 + tg * 512
            tens = yflat if base < 16384 else h2flat
            return tens[:, base % 16384: base % 16384 + 512]

        selc = self.consts["c_sel"]
        for th in range(2):
            if self.guards.get("wcache_done") is None and "d_wcache" in self.sems and th == 1:
                self.guards["wcache_done"] = self.tok("d_wcache")
            for e_ in range(NE):
                evc = []
                for tg, bkc in ((0, 3), (1, 6)):
                    self.pb_wait(bkc)
                    mmc = self.on("tensor", lambda e, e_=e_, th=th, tg=tg, bkc=bkc: e.matmul(
                        self.pbank(bkc), selc[:, e_ * 128:(e_ + 1) * 128],
                        self.rows[0:8, th * 1024 + tg * 512: th * 1024 + (tg + 1) * 512],
                        start=True, stop=True), inc="spe")
                    self.wait("vector", mmc)
                    ec = self.on("vector", lambda e, tg=tg, bkc=bkc: e.tensor_copy(
                        self.cbe[:, tg, :], self.pbank(bkc)), inc="sv")
                    self.pb_set(bkc, ec)
                    evc.append(ec)
                for slab in range(16):
                    spar = slab % 2
                    g = self.guards.get(("w1s", spar))
                    if g:
                        self.pending["gpsimd"].append(g)
                        self.pending["sync"].append(g)
                    gst = self.guards.get(("w1st", spar))
                    if gst:
                        self.pending["gpsimd"].append(gst)
                        self.pending["sync"].append(gst)
                    ck = ("w1", e_, slab)
                    if ck not in self.moe_cached:
                        self.moe_cached.add(ck)
                        l1 = self.dma("gpsimd", f"w1s{spar}", lambda e, e_=e_, slab=slab, spar=spar: e.dma_start(
                            self.w1s[:, spar, :, :],
                            W1[e_, :, slab * 256:(slab + 1) * 256].rearrange("(i p) f -> p i f", p=128)))
                        self.wait("sync", l1)
                        st = self.dma("sync", "wcache", lambda e, e_=e_, slab=slab, spar=spar: e.dma_start(
                            self.w1c[e_, :, slab * 256:(slab + 1) * 256].rearrange("(i p) f -> p i f", p=128),
                            self.w1s[:, spar, :, :]))
                        self.guards[("w1st", spar)] = st
                    else:
                        self.wait("sync", self.guards.get("wcache_done"))
                        l1 = self.dma("sync", f"w1r{spar}", lambda e, e_=e_, slab=slab, spar=spar: e.dma_start(
                            self.w1s[:, spar, :, :],
                            self.w1c[e_, :, slab * 256:(slab + 1) * 256].rearrange("(i p) f -> p i f", p=128)))
                        self.guards[("w1st", spar)] = None
                    for f2 in range(2):
                        ft = slab * 2 + f2
                        bks = ((ft % 2) * 2, (ft % 2) * 2 + 1)
                        for tg in range(2):
                            self.pb_wait(bks[tg])
                        self.wait("tensor", l1)
                        for i in range(NI):
                            for tg in range(2):
                                self.on("tensor", lambda e, spar=spar, i=i, f2=f2, th=th, tg=tg, bk=bks[tg]: e.matmul(
                                    self.pbank(bk), self.w1s[:, spar, i, f2 * 128:(f2 + 1) * 128],
                                    self.XB[:, i, th * 1024 + tg * 512: th * 1024 + (tg + 1) * 512],
                                    start=(i == 0), stop=(i == NI - 1)),
                                    inc="spe" if (i == NI - 1 and tg == 1) else None)
                        mm = self.tok("spe")
                        self.guards[("w1s", spar)] = mm
                        self.wait("scalar", mm)
                        for tg in range(2):
                            ge = self.on("scalar", lambda e, e_=e_, ft=ft, tg=tg, bk=bks[tg]: e.activation(
                                h_ap(ft, tg), self.pbank(bk), AF.Gelu,
                                bias=self.b1col[:, e_, ft:ft + 1], scale=1.0), inc="ss")
                            self.pb_set(bks[tg], ge)
                h_ready = self.tok("ss")
                for dt in range(NI):
                    bank0 = 4 if dt % 2 == 0 else 0
                    ltoks = []
                    for fh in range(2):
                        wpar = fh
                        g = self.guards.get(("w2s", wpar))
                        if g:
                            self.pending["gpsimd"].append(g)
                            self.pending["sync"].append(g)
                        gst = self.guards.get(("w2st", wpar))
                        if gst:
                            self.pending["gpsimd"].append(gst)
                            self.pending["sync"].append(gst)
                        ck = ("w2", e_, dt, fh)
                        if ck not in self.moe_cached:
                            self.moe_cached.add(ck)
                            l2 = self.dma("gpsimd", f"w2s{wpar}", lambda e, e_=e_, dt=dt, fh=fh, wpar=wpar: e.dma_start(
                                self.w2s[:, wpar, :, :],
                                W2[e_, fh * 2048:(fh + 1) * 2048, dt * 128:(dt + 1) * 128]
                                .rearrange("(fc p) d -> p fc d", p=128)))
                            self.wait("sync", l2)
                            st = self.dma("sync", "wcache", lambda e, e_=e_, dt=dt, fh=fh, wpar=wpar: e.dma_start(
                                self.w2c[e_, fh * 2048:(fh + 1) * 2048, dt * 128:(dt + 1) * 128]
                                .rearrange("(fc p) d -> p fc d", p=128),
                                self.w2s[:, wpar, :, :]))
                            self.guards[("w2st", wpar)] = st
                        else:
                            self.wait("sync", self.guards.get("wcache_done"))
                            l2 = self.dma("sync", f"w2r{wpar}", lambda e, e_=e_, dt=dt, fh=fh, wpar=wpar: e.dma_start(
                                self.w2s[:, wpar, :, :],
                                self.w2c[e_, fh * 2048:(fh + 1) * 2048, dt * 128:(dt + 1) * 128]
                                .rearrange("(fc p) d -> p fc d", p=128)))
                            self.guards[("w2st", wpar)] = None
                        ltoks.append((l2, wpar))
                    for tg in range(2):
                        self.pb_wait(bank0 + tg)
                    self.wait("tensor", h_ready)
                    for fh in range(2):
                        l2, wpar = ltoks[fh]
                        self.wait("tensor", l2)
                        for fc in range(16):
                            for tg in range(2):
                                self.on("tensor", lambda e, wpar=wpar, fh=fh, fc=fc, tg=tg, bk=bank0 + tg, th=th: e.matmul(
                                    self.pbank(bk), self.w2s[:, wpar, fc, :],
                                    h_ap(fh * 16 + fc, tg), start=(fh == 0 and fc == 0),
                                    stop=(fh == 1 and fc == 15)),
                                    inc="spe" if (fc == 15 and tg == 1) else None)
                        self.guards[("w2s", wpar)] = self.tok("spe")
                    mm = self.tok("spe")
                    self.wait("vector", mm)
                    self.waits("vector", evc)
                    for tg in range(2):
                        self.on("vector", lambda e, e_=e_, dt=dt, tg=tg, bk=bank0 + tg: e.scalar_tensor_tensor(
                            self.lnb[:], self.pbank(bk), self.b2col[:, e_, dt:dt + 1],
                            self.cbe[:, tg, :], op0=ALU.add, op1=ALU.mult), inc="sv")
                        ts2 = self.on("vector", lambda e, dt=dt, tg=tg, th=th: e.tensor_tensor(
                            self.R[:, dt, th * 1024 + tg * 512: th * 1024 + (tg + 1) * 512],
                            self.R[:, dt, th * 1024 + tg * 512: th * 1024 + (tg + 1) * 512],
                            self.lnb[:], op=ALU.add), inc="sv")
                        self.pb_set(bank0 + tg, ts2)
        self.bar()

    def plan_qffn(self, b):
        WQ = self.hw["L1_ffn_weight"]; BQ = self.hw["L1_ffn_bias"]
        self.bias_cols(BQ, D, self.bfcol, bank=0)
        self.bar()
        if b > 0:
            ld = self.dma("sync", "misc", lambda e: e.dma_start(
                self.wdT[:, :, :].rearrange("p a b -> p (a b)"), self.wdc), self_wait=True)
            self.guards["wdt_ld"] = ld
        for r in (range(NI) if b == 0 else ()):
            g = self.guards.get("qnat0")
            if g:
                self.pending["sync"].append(g)
            ld = self.dma("sync", "misc", lambda e, r=r: e.dma_start(
                self.qnat[:, 0, :], WQ[r * 128:(r + 1) * 128, :]), self_wait=True)
            qtok = self.plan_quant_tile(lambda: self.qnat[:, 0, :], ld)
            self.guards["qnat0"] = qtok
            for i in range(NI):
                self.pb_wait(7)
                self.wait("tensor", qtok)
                pe = self.on("tensor", lambda e, i=i: e.matmul(
                    self.ptb[:, 0:128], self.qout[:, i * 128:(i + 1) * 128],
                    self.consts["c_ident_b"][:],
                    is_transpose=True, start=True, stop=True), inc="spe")
                self.wait("vector", pe)
                ev = self.on("vector", lambda e, i=i, r=r: e.tensor_copy(
                    self.wdT[:, i, r * 128:(r + 1) * 128], self.ptb[:, 0:128]), inc="sv")
                self.pb_set(7, ev)
            self.pending["vector"].append(self.tok("spe"))
        if b == 0:
            self.pending["sync"].append(self.tok("sv"))
            st = self.dma("sync", "misc", lambda e: e.dma_start(
                self.wdc, self.wdT[:, :, :].rearrange("p a b -> p (a b)")), self_wait=True)
            self.guards["wdc_store"] = st
        self.bar()
        for tgc in range(NG):
            for tau in range(4 * tgc, 4 * tgc + 4):
                par = tau % 2
                evs = []
                for i in range(NI):
                    bank = 4 + i % 2
                    eng = "vector" if i % 2 == 0 else "scalar"
                    g = self.guards.get(("qnat", par))
                    if g:
                        self.pending[eng].append(g)
                    ev = self.transpose_f32(
                        lambda i=i, t=tau: self.R[:, i, t * 128:(t + 1) * 128],
                        lambda p=par, i=i: self.qnat[:, p, i * 128:(i + 1) * 128],
                        bank, eng)
                    evs.append(ev)
                qtok = self.plan_quant_tile(lambda p=par: self.qnat[:, p, :], None, pre=evs)
                self.guards[("qnat", par)] = qtok
                for i in range(NI):
                    self.pb_wait(7)
                    self.wait("tensor", qtok)
                    pe = self.on("tensor", lambda e, i=i: e.matmul(
                        self.ptb[:, 0:128], self.qout[:, i * 128:(i + 1) * 128],
                        self.consts["c_ident_b"][:],
                        is_transpose=True, start=True, stop=True), inc="spe")
                    self.wait("vector", pe)
                    ev = self.on("vector", lambda e, i=i, t=tau: e.tensor_copy(
                        self.XB[:, i, t * 128:(t + 1) * 128], self.ptb[:, 0:128]), inc="sv")
                    self.pb_set(7, ev)
                self.pending["vector"].append(self.tok("spe"))
            xbtok = self.tok("sv")
            for mt in range(NI):
                bank = mt % 2
                self.pb_wait(bank)
                self.wait("tensor", xbtok)
                self.wait("tensor", self.guards.pop("wdt_ld", None))
                for i in range(NI):
                    self.on("tensor", lambda e, mt=mt, i=i, tgc=tgc: e.matmul(
                        self.pbank(mt % 2),
                        self.wdT[:, i, mt * 128:(mt + 1) * 128],
                        self.XB[:, i, tgc * 512:(tgc + 1) * 512],
                        start=(i == 0), stop=(i == NI - 1)),
                        inc="spe" if i == NI - 1 else None)
                mm = self.tok("spe")
                self.wait("vector", mm)
                ev = self.on("vector", lambda e, mt=mt, tgc=tgc: e.scalar_tensor_tensor(
                    self.R[:, mt, tgc * 512:(tgc + 1) * 512], self.pbank(mt % 2),
                    self.bfcol[:, mt:mt + 1], self.R[:, mt, tgc * 512:(tgc + 1) * 512],
                    op0=ALU.add, op1=ALU.add), inc="sv")
                self.pb_set(mt % 2, ev)
        self.bar()

    def plan_quant_tile(self, src_fn, ld_tok, pre=()):
        if ld_tok is not None:
            self.pending["vector"].append(ld_tok)
        for t in pre:
            self.wait("vector", t)
        self.on("vector", lambda e: e.tensor_reduce(
            self.qsc[:], src_fn().rearrange("p (blk c) -> p blk c", c=32),
            axis=AX.X, op=ALU.max, apply_absolute_value=True), inc="sv")
        self.on("vector", lambda e: e.tensor_scalar(
            self.qsc[:], self.qsc[:], 1.0 / 7.5, 1e-30, op0=ALU.mult, op1=ALU.max), inc="sv")
        self.on("vector", lambda e: e.reciprocal(self.qscr[:], self.qsc[:]), inc="sv")
        self.on("vector", lambda e: e.tensor_tensor(
            self.qy[:].rearrange("p (blk c) -> p blk c", c=32),
            src_fn().rearrange("p (blk c) -> p blk c", c=32),
            self.qscr[:].unsqueeze(2).broadcast_to((128, 32, 32)),
            op=ALU.mult), inc="sv")
        ytok = self.tok("sv")
        self.wait("scalar", ytok)
        atok = self.on("scalar", lambda e: e.activation(
            self.qa[:], self.qy[:], AF.Abs, bias=0.0, scale=1.0), inc="ss")
        self.wait("vector", atok)
        self.on("vector", lambda e: e.tensor_scalar(
            self.qm2[:], self.qa[:], 2.0, -4.0, op0=ALU.is_ge, op1=ALU.mult), inc="sv")
        self.on("vector", lambda e: e.tensor_scalar(
            self.qm4[:], self.qa[:], 4.0, -2.0, op0=ALU.is_ge, op1=ALU.mult), inc="sv")
        self.on("vector", lambda e: e.scalar_tensor_tensor(
            self.qd[:], self.qm2[:], 8.0, self.qm4[:],
            op0=ALU.add, op1=ALU.add), inc="sv")
        self.on("vector", lambda e: e.tensor_tensor(
            self.qzi[:], self.qy[:], self.qd[:], op=ALU.mult), inc="sv")
        self.on("vector", lambda e: e.tensor_scalar(
            self.qzi[:], self.qzi[:], 12582912.0, -12582912.0, op0=ALU.add, op1=ALU.add), inc="sv")
        self.on("vector", lambda e: e.tensor_scalar(
            self.qa[:], self.qm2[:], -0.03125, 0.125, op0=ALU.mult, op1=ALU.add), inc="sv")
        self.on("vector", lambda e: e.scalar_tensor_tensor(
            self.qa[:], self.qm4[:], -0.125, self.qa[:],
            op0=ALU.mult, op1=ALU.add), inc="sv")
        self.on("vector", lambda e: e.tensor_tensor(
            self.qy[:], self.qzi[:], self.qa[:], op=ALU.mult), inc="sv")
        q2 = self.on("vector", lambda e: e.tensor_tensor(
            self.qout[:].rearrange("p (blk c) -> p blk c", c=32),
            self.qy[:].rearrange("p (blk c) -> p blk c", c=32),
            self.qsc[:].unsqueeze(2).broadcast_to((128, 32, 32)),
            op=ALU.mult), inc="sv")
        return q2


def kernel(**inputs):
    from concourse.bass_utils import run_bass_kernel_spmd
    x = np.ascontiguousarray(np.asarray(inputs["L0_attn_x"], dtype=np.float32))
    mk = MK(nbatch=2, layers=(0, 1), sublayers=("attn", "ffn"))
    nc = bass.Bass("TRN2", target_bir_lowering=False)
    es = ExitStack()
    mk.build(nc, es)
    in_maps = [mk.in_map(x[2 * c:2 * c + 2], inputs) for c in range(8)]
    res = run_bass_kernel_spmd(nc, in_maps, core_ids=list(range(8)))
    out = np.concatenate([np.asarray(r["x_out"]) for r in res.results], axis=0)
    return out.astype(np.float32)


# version 13
# speedup vs baseline: 1.1922x; 1.1258x over previous
import numpy as np
from contextlib import ExitStack
import concourse.bass as bass
import concourse.mybir as mybir

F32 = mybir.dt.float32
BF16 = mybir.dt.bfloat16
F32R = mybir.dt.float32r
I32 = mybir.dt.int32
AF = mybir.ActivationFunctionType
ALU = mybir.AluOpType
AX = mybir.AxisListType

ENGINES = ("tensor", "vector", "scalar", "sync", "gpsimd")

D = 1024
T = 2048
NH = 16
DH = 64
NI = 8
NT = 16
NG = 4
NE = 8
HF = 4096
NFI = 32
NEG_BIG = -1.0e30
QS = 0.35355339059327373
LN_EPS = 1e-5


def make_consts():
    c = {}
    c["c_ident_f"] = np.eye(128, dtype=np.float32)
    c["c_ident_b"] = np.eye(128, dtype=np.float32)
    c["c_tri_b"] = np.triu(np.ones((128, 128), np.float32), 1) * NEG_BIG
    c["c_onesrow_f"] = np.ones((1, 128), dtype=np.float32)
    c["c_onesrow_r"] = np.ones((1, 128), dtype=np.float32)
    c["c_halfcol_b"] = np.full((128, 1), 0.5, dtype=np.float32)
    c["c_onescol_b"] = np.ones((128, 1), dtype=np.float32)
    c["c_iota_row_b"] = np.arange(T, dtype=np.float32)[None, :]
    c["c_half_row_b"] = np.full((1, 512), 0.5, dtype=np.float32)
    c["c_iotacol16"] = (np.arange(16)[None, :] * 128 + np.arange(128)[:, None]).astype(np.float32)
    c["c_epscol"] = np.full((128, 1), LN_EPS, dtype=np.float32)
    sel = np.zeros((8, 8 * 128), dtype=np.float32)
    for e in range(8):
        sel[e, e * 128:(e + 1) * 128] = 1.0
    c["c_sel"] = sel
    return c


CONST_DTYPES = {
    "c_ident_f": F32, "c_ident_b": BF16, "c_tri_b": BF16,
    "c_onesrow_f": F32, "c_onesrow_r": F32R, "c_halfcol_b": BF16, "c_onescol_b": BF16,
    "c_iota_row_b": BF16, "c_half_row_b": BF16,
    "c_iotacol16": F32, "c_sel": F32, "c_epscol": F32,
}


def np_for(dt):
    import ml_dtypes
    return {F32: np.float32, BF16: ml_dtypes.bfloat16, F32R: np.float32,
            I32: np.int32}[dt]


class MK:
    def __init__(self, nbatch=2, layers=(0, 1), sublayers=("attn", "ffn")):
        self.nbatch = nbatch
        self.layers = tuple(layers)
        self.sublayers = tuple(sublayers)

    def sem(self, name):
        if name not in self.sems:
            h = self.es.enter_context(self.nc.semaphore(name))
            self.sems[name] = [h, 0]
        return self.sems[name]

    def on(self, eng, fn, inc=None, n=1, waits=()):
        w = self.pending[eng] + list(waits)
        self.pending[eng] = []
        if eng in ("vector", "scalar"):
            own = self.OWN[eng]
            c = self.sem(own)[1]
            if c > 0:
                w.append((own, c))
            if inc is None:
                inc = own
        lw = self.lastwait[eng]
        best = {}
        for t in w:
            if t is None:
                continue
            s, v = t
            if v > best.get(s, -1):
                best[s] = v
        w2 = []
        for s, v in best.items():
            if v > lw.get(s, -1) and v > 0:
                w2.append((s, v))
                lw[s] = v
        tok = None
        if inc is not None:
            c = self.sem(inc)
            c[1] += n
            tok = (inc, c[1])
        self.ops[eng].append((tuple(w2), fn, inc, n))
        self.nops += 1
        return tok

    def tok(self, sem):
        return (sem, self.sems[sem][1])

    def dma(self, queue, stream, fn, self_wait=False):
        tok = self.on(queue, fn, inc="d_" + stream, n=16)
        if self_wait:
            self.on(queue, lambda e: e.nop(), waits=[tok])
        return tok

    def flush_ynorm(self):
        for it, rc, hh, G, yb in self._defer:
            ynorm = self.on("vector", lambda e, hh=hh, G=G, yb=yb: e.tensor_tensor(
                self.yT[(hh % 2) * 64:(hh % 2) * 64 + 64, hh // 2, G * 512:(G + 1) * 512],
                self.pbank(yb, hi=64), self.bsc[0:64, :], op=ALU.mult),
                inc="sv", waits=[t for t in (it, rc) if t is not None])
            self.guards["bscw"] = ynorm
            self.pb_set(yb, ynorm)
        self._defer = []

    def lp(self, fn):
        def wrapped(e):
            with self.nc.allow_low_precision(reason="f32r row for broadcast matmul"):
                return fn(e)
        return wrapped

    def wait(self, eng, tok):
        if tok is not None:
            self.pending[eng].append(tok)

    def waits(self, eng, toks):
        for t in toks:
            self.wait(eng, t)

    OWN = {"tensor": "spe", "vector": "sv", "scalar": "ss", "sync": "sio", "gpsimd": "sdw"}

    def bar(self, engines=ENGINES):
        snap = [(s, c[1]) for s, c in self.sems.items() if c[1] > 0]
        for e in engines:
            for s, v in snap:
                self.pending[e].append((s, v))

    def replay(self, ename, eng):
        sems = self.sems
        for w, fn, inc, n in self.ops[ename]:
            for s, v in w:
                eng.wait_ge(sems[s][0], v)
            instr = fn(eng)
            if inc is not None:
                instr.then_inc(sems[inc][0], n)

    def pbank(self, k, lo=0, hi=128, w0=0, w1=512):
        if k < 4:
            return self.pbA[lo:hi, k * 512 + w0: k * 512 + w1]
        if k < 8:
            return self.pbB[lo:hi, (k - 4) * 512 + w0: (k - 4) * 512 + w1]
        raise KeyError(k)

    def pb_wait(self, k):
        g = self.bank_guard.get(k)
        if g:
            if isinstance(g[0], str):
                self.pending["tensor"].append(g)
            else:
                for t in g:
                    self.pending["tensor"].append(t)

    def pb_set(self, k, tok):
        self.bank_guard[k] = tok

    def build(self, nc, es):
        self.nc = nc
        self.es = es
        self.ops = {e: [] for e in ENGINES}
        self.pending = {e: [] for e in ENGINES}
        self._defer = []
        self.lastwait = {e: {} for e in ENGINES}
        self.sems = {}
        self.bank_guard = {}
        self.guards = {}
        self.nops = 0
        for s in ("spe", "sv", "ss", "sio", "sdw"):
            self.sem(s)

        dram = nc.dram_tensor
        self.moe_cached = set()
        self.w1c = dram("w1cache", [NE, D, HF], BF16, kind="Internal").ap()
        self.w2c = dram("w2cache", [NE, HF, D], BF16, kind="Internal").ap()
        self.hx = dram("x_in", [self.nbatch, T, D], F32, kind="ExternalInput").ap()
        self.hout = dram("x_out", [self.nbatch, T, D], F32, kind="ExternalOutput").ap()
        self.hw = {}

        def win(name, shape):
            self.hw[name] = dram(name, shape, F32, kind="ExternalInput").ap()
        if 0 in self.layers:
            if "attn" in self.sublayers:
                win("L0_attn_w_attn", [D, 3 * D]); win("L0_attn_b_attn", [3 * D])
                win("L0_attn_w_proj", [D, D]); win("L0_attn_b_proj", [D])
            if "ffn" in self.sublayers:
                win("L0_ffn_Wr", [NE, D]); win("L0_ffn_W1", [NE, D, HF])
                win("L0_ffn_b1", [NE, HF]); win("L0_ffn_W2", [NE, HF, D])
                win("L0_ffn_b2", [NE, D])
        if 1 in self.layers:
            if "attn" in self.sublayers:
                win("L1_attn_W_kqv", [D, 3 * D]); win("L1_attn_b_kqv", [3 * D])
                win("L1_attn_W_proj", [D, D]); win("L1_attn_b_proj", [D])
                win("L1_attn_slopes", [NH])
            if "ffn" in self.sublayers:
                win("L1_ffn_weight", [D, D]); win("L1_ffn_bias", [D])
        self.hc = {}
        cvals = make_consts()
        for name, dt in CONST_DTYPES.items():
            self.hc[name] = dram(name, list(cvals[name].shape), dt, kind="ExternalInput").ap()

        sb = lambda name, shape, dt: es.enter_context(nc.sbuf_tensor(name, shape, dt))

        self.R = sb("R", [128, NI, T], F32)
        self.XB = sb("XB", [128, NI, T], BF16)
        self.yT = sb("yT", [128, NI, T], BF16)
        self.rows = sb("rows", [128, T], F32)
        self.rows2 = sb("rows2", [128, 512], F32R)
        self.lnb = sb("lnb", [128, 512], F32)
        self.bsc = self.lnb
        self.kcell = sb("kcell", [1, 2], BF16)
        self.nrcell = sb("nrcell", [128, 16], BF16)
        self.bcol = sb("bcol", [128, 24], F32)
        self.bqcol = sb("bqcol", [128, 16], F32)
        self.bkcol = sb("bkcol", [128, 16], F32)
        self.bpcol = sb("bpcol", [128, 8], F32)
        self.slope_bc = sb("slope_bc", [128, 16], F32)
        self.scol = sb("scol", [128, 32], F32)
        self.b24 = sb("b24", [32, 128], F32)
        self.bfcol = sb("bfcol", [128, 8], F32)
        self.pcol = sb("pcol", [128, 4], F32)
        self.qsc = sb("qsc", [128, D // 32], F32)
        self.qscr = sb("qscr", [128, D // 32], F32)
        self.consts = {}
        for name, dt in CONST_DTYPES.items():
            self.consts[name] = sb("s" + name, list(cvals[name].shape), dt)
        ARENA = 56320
        pad = (-nc.sbuf_base) % 32
        base0 = nc.sbuf_base + pad
        self.arena = sb("arena", [128, ARENA + pad], mybir.dt.uint8)
        at = lambda name, shape, dt, off: nc.alloc_sbuf_tensor_at(name, shape, dt, offset=base0 + off)
        self.vg = at("vg", [128, NT, 4 * 65], BF16, 0)
        self.Qh = at("Qh", [128, 2, T], BF16, 8320)
        self.Kh = at("Kh", [128, 2, T], BF16, 16512)
        self.asb = at("asb", [128, 4, 512], BF16, 24704)
        self.lnsq = at("lnsq", [128, T], BF16, 24704)
        self.wqk = at("wqk", [128, 2, 2, NI, 128], BF16, 28800)
        self.wv = at("wv", [128, NI, 256], BF16, 36992)
        self.vbias = at("vbias", [128, D], F32, 41088)
        self.wpj = at("wpj", [128, 2, NI, 128], BF16, 28800)
        self.tload = at("tload", [128, 2, D], F32, 0)
        self.w1s = at("w1s", [128, 2, NI, 256], BF16, 0)
        self.w2s = at("w2s", [128, 2, 16, 128], BF16, 8192)
        self.cbe = at("cbe", [128, 2, 512], F32, 16384)
        self.wrt = at("wrt", [128, NI, 8], F32, 20480)
        self.probs = at("probs", [128, NT, NE], F32, 20736)
        self.ptmp = at("ptmp", [128, NT, NE], F32, 21248)
        self.comb = at("comb", [128, NT, NE], F32, 21760)
        self.b1col = at("b1col", [128, NE, NFI], F32, 22272)
        self.b2col = at("b2col", [128, NE, NI], F32, 23296)
        self.h2 = at("h2", [128, NI, T], BF16, 23552)
        self.wdT = at("wdT", [128, NI, D], BF16, 0)
        self.wdc = nc.dram_tensor("wdcache", [128, NI * D], BF16, kind="Internal").ap()
        self.dscr = nc.dram_tensor("dbcast", [2, 1, 512], F32, kind="Internal").ap()
        self.qnat = at("qnat", [128, 2, D], F32, 16384)
        self.qy = at("qy", [128, D], F32, 24576)
        self.qa = at("qa", [128, D], F32, 28672)
        self.qd = at("qd", [128, D], F32, 32768)
        self.qm2 = at("qm2", [128, D], F32, 36864)
        self.qm4 = at("qm4", [128, D], F32, 40960)
        self.qzi = at("qzi", [128, D], F32, 45056)
        self.qout = at("qout", [128, D], BF16, 32768)
        self.pbA = es.enter_context(nc.psum_tensor("pbA", [128, 2048], F32))
        self.pbB = es.enter_context(nc.psum_tensor("pbB", [128, 2048], F32))
        self.ptb = self.pbB[:, 1536:2048].bitcast(BF16)

        self.plan_consts_load()
        for b in range(self.nbatch):
            self.plan_batch(b)
        fin = [(s, c[1]) for s, c in self.sems.items() if s.startswith("d_") and c[1] > 0]
        self.on("sync", lambda e: e.nop(), waits=fin)

        block = es.enter_context(nc.Block())
        for ename in ENGINES:
            getattr(block, ename)(lambda eng, _e=ename: self.replay(_e, eng))
        return nc

    def in_map(self, x_shard, weights):
        m = {"x_in": np.ascontiguousarray(x_shard, dtype=np.float32)}
        for k in self.hw:
            m[k] = np.ascontiguousarray(np.asarray(weights[k]), dtype=np.float32)
        cv = make_consts()
        for k, dt in CONST_DTYPES.items():
            m[k] = np.ascontiguousarray(cv[k].astype(np_for(dt)))
        return m

    def plan_consts_load(self):
        for name in CONST_DTYPES:
            self.dma("sync", "consts", lambda e, d=self.consts[name], s=self.hc[name]:
                     e.dma_start(d[:], s))
        self.bar()

    def plan_batch(self, b):
        self.plan_prologue(b)
        for L in self.layers:
            if "attn" in self.sublayers:
                self.plan_attn(b, L)
                self.plan_ln(b)
            if "ffn" in self.sublayers:
                if L == 0:
                    self.plan_moe(b)
                else:
                    self.plan_qffn(b)
                self.plan_ln(b, final=(L == self.layers[-1]))
        self.plan_epilogue(b)

    def transpose_f32(self, src_ap_fn, dst_ap_fn, bank, eng):
        idf = self.consts["c_ident_f"]
        self.pb_wait(bank)
        pe = self.on("tensor", lambda e: e.matmul(
            self.pbank(bank, w1=128), src_ap_fn(), idf[:], is_transpose=True,
            start=True, stop=True), inc="spe")
        self.wait(eng, pe)
        if eng == "vector":
            ev = self.on("vector", lambda e: e.tensor_copy(dst_ap_fn(), self.pbank(bank, w1=128)), inc="sv")
        else:
            ev = self.on("scalar", lambda e: e.copy(dst_ap_fn(), self.pbank(bank, w1=128)), inc="ss")
        self.pb_set(bank, ev)
        self._last_tp = pe
        return ev

    def plan_prologue(self, b):
        import os as _os
        _nt = int(_os.environ.get("MK_DEBUG_NT", str(NT)))
        self.wait("sync", self.guards.pop("wdc_store", None))
        guards = {}
        for tau in range(_nt):
            par = tau % 2
            g = guards.get(par)
            if g:
                self.pending["sync"].append(g)
            ld = self.dma("sync", f"x{par}", lambda e, p=par, t=tau: e.dma_start(
                self.tload[:, p, :], self.hx[b, t * 128:(t + 1) * 128, :]))
            self.wait("tensor", ld)
            for i in range(NI):
                bank = i % 2
                eng = "vector" if (i % 4) < 2 else "scalar"
                ev = self.transpose_f32(
                    lambda p=par, i=i: self.tload[:, p, i * 128:(i + 1) * 128],
                    lambda i=i, t=tau: self.R[:, i, t * 128:(t + 1) * 128],
                    bank, eng)
                eng2 = "scalar" if eng == "vector" else "vector"
                self.wait(eng2, ev)
                if eng2 == "vector":
                    self.on("vector", lambda e, i=i, t=tau: e.tensor_copy(
                        self.XB[:, i, t * 128:(t + 1) * 128],
                        self.R[:, i, t * 128:(t + 1) * 128]), inc="sv")
                else:
                    self.on("scalar", lambda e, i=i, t=tau: e.copy(
                        self.XB[:, i, t * 128:(t + 1) * 128],
                        self.R[:, i, t * 128:(t + 1) * 128]), inc="ss")
            guards[par] = self.tok("spe")
        self.bar()

    def plan_epilogue(self, b):
        st_guard = {}
        for tau in range(NT):
            par = tau % 2
            evs = []
            for i in range(NI):
                bank = i % 2
                eng = "vector" if (i % 4) < 2 else "scalar"
                g = st_guard.get(par)
                if g:
                    self.pending[eng].append(g)
                ev = self.transpose_f32(
                    lambda i=i, t=tau: self.R[:, i, t * 128:(t + 1) * 128],
                    lambda p=par, i=i: self.qnat[:, p, i * 128:(i + 1) * 128],
                    bank, eng)
                evs.append(ev)
            self.waits("sync", evs)
            st = self.dma("sync", f"out{par}", lambda e, p=par, t=tau: e.dma_start(
                self.hout[b, t * 128:(t + 1) * 128, :], self.qnat[:, p, :]))
            st_guard[par] = st
        self.bar()

    def bias_cols(self, hvec, n, dst, bank=0):
        nr = n // 128
        assert nr <= 32
        g = self.guards.get("b24")
        if g:
            self.pending["sync"].append(g)
        ld = self.dma("sync", "misc", lambda e, nr=nr: e.dma_start(
            self.b24[0:nr, :], hvec.rearrange("(j p) -> j p", p=128)), self_wait=True)
        self.wait("tensor", ld)
        idf = self.consts["c_ident_f"]
        self.pb_wait(bank)
        pe = self.on("tensor", lambda e, nr=nr: e.matmul(
            self.pbank(bank, w1=nr), self.b24[0:nr, :], idf[0:nr, 0:nr],
            is_transpose=True, start=True, stop=True), inc="spe")
        self.guards["b24"] = pe
        self.wait("vector", pe)
        ev = self.on("vector", lambda e, nr=nr: e.tensor_copy(dst[:, 0:nr], self.pbank(bank, w1=nr)), inc="sv")
        self.pb_set(bank, ev)
        return ev

    def plan_attn(self, b, L):
        alibi = (L == 1)
        W = self.hw["L1_attn_W_kqv" if alibi else "L0_attn_w_attn"]
        Bv = self.hw["L1_attn_b_kqv" if alibi else "L0_attn_b_attn"]
        WP = self.hw["L1_attn_W_proj" if alibi else "L0_attn_w_proj"]
        BP = self.hw["L1_attn_b_proj" if alibi else "L0_attn_b_proj"]
        qoff = D if alibi else 0
        koff = 0 if alibi else D
        voff = 2 * D
        onesr = self.consts["c_onesrow_f"]

        ev = self.bias_cols(Bv, 3 * D, self.bcol, bank=0)
        self.wait("vector", ev)
        for h in range(NH):
            jq, rq = divmod(qoff + 64 * h, 128)
            jk, rk = divmod(koff + 64 * h, 128)
            self.on("vector", lambda e, h=h, jq=jq, rq=rq: e.tensor_scalar(
                self.bqcol[0:64, h:h + 1], self.bcol[rq:rq + 64, jq:jq + 1], QS, None,
                op0=ALU.mult), inc="sv")
            self.on("vector", lambda e, h=h, jk=jk, rk=rk: e.tensor_scalar(
                self.bkcol[0:64, h:h + 1], self.bcol[rk:rk + 64, jk:jk + 1], QS, None,
                op0=ALU.mult), inc="sv")
        self.bias_cols(BP, D, self.bpcol, bank=1)
        ldv = self.dma("sync", "misc", lambda e: e.dma_start(
            self.rows[0:1, 0:D], Bv[voff:voff + D].rearrange("(o f) -> o f", o=1)), self_wait=True)
        self.wait("tensor", ldv)
        for jj in range(8):
            bank = 2 + jj % 2
            self.pb_wait(bank)
            pe = self.on("tensor", lambda e, jj=jj, bank=bank: e.matmul(
                self.pbank(bank, w1=128), onesr[:],
                self.rows[0:1, jj * 128:(jj + 1) * 128],
                start=True, stop=True), inc="spe")
            self.wait("vector", pe)
            evv = self.on("vector", lambda e, jj=jj, bank=bank: e.tensor_copy(
                self.vbias[:, jj * 128:(jj + 1) * 128], self.pbank(bank, w1=128)), inc="sv")
            self.pb_set(bank, evv)
        if alibi:
            self.pending["sync"].append(self.tok("spe"))
            lds = self.dma("sync", "misc", lambda e: e.dma_start(
                self.rows[0:1, 0:16], self.hw["L1_attn_slopes"].rearrange("(o s) -> o s", o=1)),
                self_wait=True)
            self.wait("tensor", lds)
            self.pb_wait(4)
            pe = self.on("tensor", lambda e: e.matmul(
                self.pbank(4, w1=16), onesr[:], self.rows[0:1, 0:16],
                start=True, stop=True), inc="spe")
            self.wait("vector", pe)
            self.on("vector", lambda e: e.tensor_copy(self.slope_bc[:], self.pbank(4, w1=16)), inc="sv")
            e2 = self.on("vector", lambda e: e.tensor_scalar(
                self.nrcell[:], self.pbank(4, w1=16), 1.0, 0.0, op0=ALU.mult, op1=ALU.max), inc="sv")
            self.pb_set(4, e2)
        self.on("vector", lambda e: e.memset(self.Kh[64:65, :], 1.0), inc="sv")
        self.bar()

        _nh = NH
        halfc = self.consts["c_halfcol_b"]
        hctx = {}
        for h in range(_nh + 1):
            if h < _nh:
                p = h % 2
                par = (h // 2) % 2
                hp = h % 2
                if hp == 0:
                    g = self.guards.get(("wqk", par))
                    if g:
                        self.pending["gpsimd"].append(g)
                    self.dma("gpsimd", f"wqk{par}", lambda e, h=h, pq=par: e.dma_start(
                        self.wqk[:, pq, 0, :, :],
                        W[:, qoff + 64 * h: qoff + 64 * (h + 2)].rearrange("(i p) f -> p i f", p=128)))
                    lk = self.dma("gpsimd", f"wqk{par}", lambda e, h=h, pq=par: e.dma_start(
                        self.wqk[:, pq, 1, :, :],
                        W[:, koff + 64 * h: koff + 64 * (h + 2)].rearrange("(i p) f -> p i f", p=128)))
                    self.guards[("wqk_tok", par)] = lk
                lk = self.guards[("wqk_tok", par)]
                if h % 4 == 0:
                    g = self.guards.get("wv")
                    if g:
                        self.pending["gpsimd"].append(g)
                    lv = self.dma("gpsimd", "wv", lambda e, h=h: e.dma_start(
                        self.wv[:],
                        W[:, voff + 64 * h: voff + 64 * (h + 4)].rearrange("(i p) f -> p i f", p=128)))
                sc = None
                if alibi:
                    sc = self.on("vector", lambda e, h=h, p=p: e.tensor_scalar(
                        self.scol[:, p * 16:(p + 1) * 16], self.consts["c_iotacol16"][:],
                        self.slope_bc[:, h:h + 1], None, op0=ALU.mult),
                        inc="sv", waits=[self.tok("ss")])
                hctx[h] = {"p": p, "rowtok": None, "sc": sc}
                qk_evacs = []
                for qk in (0, 1):
                    col = self.bqcol if qk == 0 else self.bkcol
                    dst = self.Qh if qk == 0 else self.Kh
                    self.wait("tensor", lk)
                    for g4 in range(NG):
                        self.pb_wait(g4)
                        for i in range(NI):
                            self.on("tensor", lambda e, pq=par, qk=qk, i=i, g4=g4, hp=hp: e.matmul(
                                self.pbank(g4, hi=64), self.wqk[:, pq, qk, i, hp * 64:(hp + 1) * 64],
                                self.XB[:, i, g4 * 512:(g4 + 1) * 512],
                                start=(i == 0), stop=(i == NI - 1)),
                                inc="spe" if i == NI - 1 else None)
                        mm = self.tok("spe")
                        self.wait("vector", mm)
                        evq = self.on("vector", lambda e, h=h, p=p, g4=g4, dst=dst, col=col: e.tensor_scalar(
                            dst[0:64, p, g4 * 512:(g4 + 1) * 512], self.pbank(g4, hi=64),
                            QS, col[0:64, h:h + 1], op0=ALU.mult, op1=ALU.add), inc="sv")
                        self.pb_set(g4, evq)
                        qk_evacs.append(evq)
                    self.guards[("wqk", par)] = self.tok("spe")
            if h > 0:
                hh = h - 1
                c = hctx[hh]
                p1 = c["p"]
                hloc = hh % 4
                self.wait("tensor", self.guards.get("vg_ready"))
                for G in range(NG):
                    yb = 6 + G % 2
                    self.pb_wait(yb)
                    ilist = list(range(4 * G + 3, -1, -1))
                    if alibi:
                        sched = [(idx,) for idx in range(len(ilist))]
                    elif G == 0:
                        sched = [(0,), (1,), (2,), (3,)]
                    else:
                        sched = [(idx,) for idx in range(4)] + [(a, a + 1) for a in range(4, len(ilist), 2)]
                    pending_pv = []
                    for grp in sched:
                        abanks = []
                        for gidx, idx in enumerate(grp):
                            i = ilist[idx]
                            diag = i - 4 * G
                            woff = 128 * diag if diag >= 0 else 0
                            wlen = 512 - woff
                            ab = 4 + (idx % 2)
                            abanks.append((idx, i, diag, woff, wlen, ab))
                            self.pb_wait(ab)
                            self.wait("tensor", c["rowtok"])
                            if alibi:
                                self.wait("tensor", c["sc"])
                            self.on("tensor", lambda e, i=i, G=G, ab=ab, woff=woff, wlen=wlen, diag=diag, p1=p1: e.matmul(
                                self.pbank(ab, w0=woff, w1=woff + wlen),
                                self.Kh[0:65, p1, i * 128:(i + 1) * 128],
                                self.Qh[0:65, p1, G * 512 + woff: G * 512 + woff + wlen],
                                start=True, stop=(diag < 0)), inc=None if diag >= 0 else "spe")
                            if diag >= 0:
                                self.on("tensor", lambda e, ab=ab, woff=woff: e.matmul(
                                    self.pbank(ab, w0=woff, w1=woff + 128),
                                    self.consts["c_tri_b"][:], self.consts["c_ident_b"][:],
                                    start=False, stop=True), inc="spe")
                        qkmm = self.tok("spe")
                        if len(grp) == 1:
                            idx, i, diag, woff, wlen, ab = abanks[0]
                            aslot = idx % 4
                            slot_g = self.guards.get(("asb", aslot))
                            if slot_g:
                                self.pending["scalar"].append(slot_g)
                            mstok = None
                            if idx == 0 and woff > 0:
                                if slot_g:
                                    self.pending["vector"].append(slot_g)
                                mstok = self.on("vector", lambda e, aslot=aslot, woff=woff: e.memset(
                                    self.asb[:, aslot, 0:woff], 0.0), inc="sv")
                            self.wait("scalar", qkmm)
                            bias = (lambda ee, i=i, p1=p1: None)
                            if alibi:
                                ex = self.on("scalar", lambda e, i=i, ab=ab, aslot=aslot, woff=woff, wlen=wlen, p1=p1:
                                             e.activation(self.asb[:, aslot, woff:woff + wlen],
                                                          self.pbank(ab, w0=woff, w1=woff + wlen),
                                                          AF.Exp, bias=self.scol[:, p1 * 16 + i:p1 * 16 + i + 1],
                                                          scale=1.0), inc="ss")
                            else:
                                ex = self.on("scalar", lambda e, i=i, ab=ab, aslot=aslot, woff=woff, wlen=wlen:
                                             e.activation(self.asb[:, aslot, woff:woff + wlen],
                                                          self.pbank(ab, w0=woff, w1=woff + wlen),
                                                          AF.Exp, bias=0.0, scale=1.0), inc="ss")
                            self.pb_set(ab, ex)
                            pairs = [(idx, i, woff, wlen, ab, aslot, ex, mstok)]
                        else:
                            (idxA, iA, _, _, _, abA), (idxB, iB, _, _, _, abB) = abanks
                            aslot = idxA % 4
                            assert abA == 4 and abB == 5 and aslot in (0, 2)
                            for s_ in (aslot, aslot + 1):
                                slot_g = self.guards.get(("asb", s_))
                                if slot_g:
                                    self.pending["scalar"].append(slot_g)
                            self.wait("scalar", qkmm)
                            ex = self.on("scalar", lambda e, aslot=aslot:
                                         e.activation(self.asb[:, aslot, :].rearrange("p a b -> p (a b)") if False else self.asb[:, aslot:aslot + 2, :].rearrange("p a b -> p (a b)"),
                                                      self.pbB[:, 0:1024],
                                                      AF.Exp, bias=0.0, scale=1.0), inc="ss")
                            self.pb_set(4, ex)
                            self.pb_set(5, ex)
                            pairs = [(idxA, iA, 0, 512, 4, aslot, ex, None),
                                     (idxB, iB, 0, 512, 5, aslot + 1, ex, None)]
                        for fn in pending_pv:
                            fn()
                        pending_pv = []
                        for idx, i, woff, wlen, ab, aslot, ex, mstok in pairs:
                            pw = 0 if idx == 0 else woff

                            def emit_pv(i=i, idx=idx, aslot=aslot, hloc=hloc, pw=pw, yb=yb, ex=ex, mstok=mstok):
                                self.wait("tensor", ex)
                                self.wait("tensor", mstok)
                                pv = self.on("tensor", lambda e: e.matmul(
                                    self.pbank(yb, hi=65, w0=pw, w1=512),
                                    self.vg[:, i, hloc * 65:(hloc + 1) * 65],
                                    self.asb[:, aslot, pw:512],
                                    start=(idx == 0), stop=(i == 0)), inc="spe")
                                self.guards[("asb", aslot)] = pv
                            pending_pv.append(emit_pv)
                    for fn in pending_pv:
                        fn()
                    pvmm = self.tok("spe")
                    if G == NG - 1 or G == (1 if hh % 2 else 0):
                        self.wait("scalar", pvmm)
                        rc = self.on("scalar", self.lp(lambda e, yb=yb: e.activation(
                            self.rows2[0:1, 0:512], self.pbank(yb, lo=64, hi=65),
                            AF.Ln, bias=0.0, scale=1.0)), inc="ss")
                        self.on("scalar", self.lp(lambda e: e.activation(
                            self.rows2[0:1, 0:512], self.rows2[0:1, 0:512],
                            AF.Exp, bias=0.0, scale=-1.0)), inc="ss")
                        rce = self.tok("ss")
                    else:
                        self.wait("vector", pvmm)
                        rc = self.on("vector", self.lp(lambda e, yb=yb: e.reciprocal(
                            self.rows2[0:1, 0:512], self.pbank(yb, lo=64, hi=65))), inc="sv")
                        rce = rc
                    ab_bc = 4 + G % 2
                    self.wait("tensor", rce)
                    self.pb_wait(ab_bc)
                    bc = self.on("tensor", lambda e, ab_bc=ab_bc: e.matmul(
                        self.pbank(ab_bc, hi=64), self.consts["c_onesrow_r"][0:1, 0:64],
                        self.rows2[0:1, 0:512], start=True, stop=True), inc="spe")
                    self.wait("vector", bc)
                    sc2 = self.on("vector", lambda e, ab_bc=ab_bc: e.tensor_copy(
                        self.bsc[0:64, :], self.pbank(ab_bc, hi=64)), inc="sv")
                    self.pb_set(ab_bc, sc2)
                    if G == NG - 1:
                        self._defer.append((sc2, rc, hh, G, yb))
                    else:
                        self.wait("vector", rc)
                        ynorm = self.on("vector", lambda e, hh=hh, G=G, yb=yb: e.tensor_tensor(
                            self.yT[(hh % 2) * 64:(hh % 2) * 64 + 64, hh // 2, G * 512:(G + 1) * 512],
                            self.pbank(yb, hi=64), self.bsc[0:64, :], op=ALU.mult), inc="sv")
                        self.guards["bscw"] = ynorm
                        self.pb_set(yb, ynorm)
            if h < _nh and h % 4 == 0:
                self.plan_vgen(b, lv, first=(h == 0), h0=h)
            if h < _nh:
                self.waits("gpsimd", qk_evacs)
                sk = self.on("gpsimd", lambda e, p=p: e.tensor_tensor(
                    self.Qh[64:128, p, :], self.Kh[0:64, p, :], self.Kh[0:64, p, :], op=ALU.mult), inc="sdw")
                sqq = self.on("gpsimd", lambda e, p=p, h=h: e.tensor_tensor(
                    self.yT[(h % 2) * 64:(h % 2) * 64 + 64, h // 2, :],
                    self.Qh[0:64, p, :], self.Qh[0:64, p, :], op=ALU.mult), inc="sdw")
                self.wait("tensor", sk)
                for g4 in range(NG):
                    self.pb_wait(g4)
                    self.on("tensor", lambda e, g4=g4, p=p: e.matmul(
                        self.pbank(g4, lo=0, hi=1), halfc[64:128, :],
                        self.Qh[64:128, p, g4 * 512:(g4 + 1) * 512], start=True, stop=True),
                        inc="spe" if g4 == NG - 1 else None)
                mmk = self.tok("spe")
                self.wait("vector", mmk)
                rmx = self.on("vector", lambda e, p=p: e.tensor_reduce(
                    self.kcell[0:1, p:p + 1], self.pbA[0:1, :], axis=AX.X, op=ALU.max), inc="sv")
                self.wait("tensor", rmx)
                self.wait("tensor", sqq)
                qb = (h % 2) * 64
                for g4 in range(NG):
                    self.on("tensor", lambda e, g4=g4, h=h, qb=qb: e.matmul(
                        self.pbank(g4, lo=0, hi=1), halfc[qb:qb + 64, :],
                        self.yT[qb:qb + 64, h // 2, g4 * 512:(g4 + 1) * 512], start=True, stop=False))
                    self.on("tensor", lambda e, g4=g4, p=p: e.matmul(
                        self.pbank(g4, lo=0, hi=1), self.kcell[0:1, p:p + 1],
                        self.consts["c_half_row_b"][0:1, 0:512],
                        start=False, stop=(not alibi)),
                        inc=None if alibi else ("spe" if g4 == NG - 1 else None))
                    if alibi:
                        self.on("tensor", lambda e, g4=g4, h=h: e.matmul(
                            self.pbank(g4, lo=0, hi=1), self.nrcell[0:1, h:h + 1],
                            self.consts["c_iota_row_b"][0:1, g4 * 512:(g4 + 1) * 512],
                            start=False, stop=True),
                            inc="spe" if g4 == NG - 1 else None)
                mmq = self.tok("spe")
                self.wait("scalar", mmq)
                rowtok = self.on("scalar", lambda e, p=p: e.activation(
                    self.Qh[64:65, p, :], self.pbA[0:1, :], AF.Identity,
                    bias=0.0, scale=-1.0), inc="ss")
                for g4 in range(NG):
                    self.pb_set(g4, rowtok)
                hctx[h]["rowtok"] = rowtok
            self.flush_ynorm()
        self.bar()
        for ft in range(NI):
            par = ft % 2
            g = self.guards.get(("wpj", par))
            if g:
                self.pending["gpsimd"].append(g)
            lp = self.dma("gpsimd", f"wpj{par}", lambda e, ft=ft, p=par: e.dma_start(
                self.wpj[:, p, :, :],
                WP[:, ft * 128:(ft + 1) * 128].rearrange("(i p) f -> p i f", p=128)))
            for tg in range(NG):
                self.pb_wait(tg)
            self.wait("tensor", lp)
            for ic in range(NI):
                for tg in range(NG):
                    self.on("tensor", lambda e, ft=ft, p=par, ic=ic, tg=tg: e.matmul(
                        self.pbank(tg), self.wpj[:, p, ic, :],
                        self.yT[:, ic, tg * 512:(tg + 1) * 512],
                        start=(ic == 0), stop=(ic == NI - 1)),
                        inc="spe" if (ic == NI - 1 and tg == NG - 1) else None)
            mm = self.tok("spe")
            self.guards[("wpj", par)] = mm
            self.wait("vector", mm)
            for tg in range(NG):
                evp = self.on("vector", lambda e, ft=ft, tg=tg: e.scalar_tensor_tensor(
                    self.R[:, ft, tg * 512:(tg + 1) * 512], self.pbank(tg),
                    self.bpcol[:, ft:ft + 1], self.R[:, ft, tg * 512:(tg + 1) * 512],
                    op0=ALU.add, op1=ALU.add), inc="sv")
                self.pb_set(tg, evp)
        self.bar()

    def plan_vgen(self, b, lv_tok, first, h0=0):
        if first:
            ms = self.on("vector", lambda e: e.memset(
                self.vg.rearrange("p t (hh c) -> p t hh c", c=65)[:, :, :, 64:65], 1.0), inc="sv")
            self.wait("tensor", ms)
        ev = None
        for tau in range(NT):
            bank = tau % 2
            self.pb_wait(bank)
            self.wait("tensor", lv_tok)
            for i in range(NI):
                self.on("tensor", lambda e, tau=tau, i=i, bank=bank: e.matmul(
                    self.pbank(bank, w1=256), self.XB[:, i, tau * 128:(tau + 1) * 128],
                    self.wv[:, i, 0:256], start=(i == 0), stop=(i == NI - 1)),
                    inc="spe" if i == NI - 1 else None)
            mm = self.tok("spe")
            self.wait("vector", mm)
            ev = self.on("vector", lambda e, tau=tau, bank=bank, h0=h0: e.tensor_tensor(
                self.vg[:, tau, :].rearrange("p (hh c) -> p hh c", c=65)[:, :, 0:64],
                self.pbank(bank, w1=256).rearrange("p (hh c) -> p hh c", c=64),
                self.vbias[:, h0 * 64:(h0 + 4) * 64].rearrange("p (hh c) -> p hh c", c=64),
                op=ALU.add), inc="sv")
            self.pb_set(bank, ev)
        self.guards["vg_ready"] = ev
        self.guards["wv"] = self.tok("spe")

    def plan_ln(self, b, final=False):
        onescb = self.consts["c_onescol_b"]
        onesr = self.consts["c_onesrow_f"]
        self.bar()
        prev_sq = None
        for i in range(NI):
            srcx = self.on("vector", lambda e, i=i: e.tensor_copy(
                self.XB[:, i, :], self.R[:, i, :]), inc="sv")
            if prev_sq:
                self.pending["scalar"].append(prev_sq)
            srcs = self.on("scalar", lambda e, i=i: e.activation(
                self.lnsq[:, :], self.R[:, i, :], AF.Square, bias=0.0, scale=1.0), inc="ss")
            self.wait("tensor", srcx)
            self.wait("tensor", srcs)
            for g4 in range(NG):
                if i == 0:
                    self.pb_wait(g4)
                    self.pb_wait(4 + g4)
                self.on("tensor", lambda e, g4=g4, i=i: e.matmul(
                    self.pbank(g4, lo=0, hi=1), onescb[:],
                    self.XB[:, i, g4 * 512:(g4 + 1) * 512],
                    start=(i == 0), stop=(i == NI - 1)))
                self.on("tensor", lambda e, g4=g4, i=i: e.matmul(
                    self.pbank(4 + g4, lo=0, hi=1), onescb[:],
                    self.lnsq[:, g4 * 512:(g4 + 1) * 512],
                    start=(i == 0), stop=(i == NI - 1)),
                    inc="spe" if g4 == NG - 1 else None)
            prev_sq = self.tok("spe")
        mm = self.tok("spe")
        self.wait("scalar", mm)
        evt0 = self.on("scalar", lambda e: e.activation(
            self.rows[0:1, :], self.pbA[0:1, :], AF.Identity,
            bias=0.0, scale=1.0 / D), inc="ss")
        for g4 in range(NG):
            self.pb_set(g4, evt0)
        self.wait("vector", evt0)
        v1 = self.on("vector", lambda e: e.tensor_tensor(
            self.rows[64:65, :], self.rows[0:1, :], self.rows[0:1, :], op=ALU.mult), inc="sv")
        self.wait("vector", mm)
        v2 = self.on("vector", lambda e: e.scalar_tensor_tensor(
            self.rows[32:33, :], self.pbB[0:1, :], 1.0 / D, self.rows[64:65, :],
            op0=ALU.mult, op1=ALU.subtract), inc="sv")
        for g4 in range(NG):
            self.pb_set(4 + g4, v2)
        mmb = None
        for g4 in range(NG):
            self.wait("scalar", v2)
            self.wait("scalar", mmb)
            self.on("scalar", self.lp(lambda e, g4=g4: e.activation(
                self.rows2[0:1, 0:512], self.rows[32:33, g4 * 512:(g4 + 1) * 512], AF.Ln,
                bias=self.consts["c_epscol"][0:1, 0:1], scale=1.0)), inc="ss")
            s2 = self.on("scalar", self.lp(lambda e: e.activation(
                self.rows2[0:1, 0:512], self.rows2[0:1, 0:512], AF.Exp, bias=0.0, scale=-0.5)), inc="ss")
            self.wait("tensor", s2)
            self.pb_wait(4)
            self.on("tensor", lambda e, g4=g4: e.matmul(
                self.pbank(4), onesr[:],
                self.rows[0:1, g4 * 512:(g4 + 1) * 512],
                start=True, stop=True))
            self.pb_wait(5)
            self.on("tensor", lambda e: e.matmul(
                self.pbank(5), self.consts["c_onesrow_r"][:],
                self.rows2[0:1, 0:512],
                start=True, stop=True), inc="spe")
            mmb = self.tok("spe")
            self.wait("vector", mmb)
            t2 = None
            for i in range(NI):
                self.on("vector", lambda e, i=i, g4=g4: e.tensor_tensor(
                    self.lnb[:], self.R[:, i, g4 * 512:(g4 + 1) * 512], self.pbank(4),
                    op=ALU.subtract), inc="sv")
                t2 = self.on("vector", lambda e, i=i, g4=g4: e.tensor_tensor(
                    self.R[:, i, g4 * 512:(g4 + 1) * 512], self.lnb[:], self.pbank(5),
                    op=ALU.mult), inc="sv")
                if not final:
                    self.wait("scalar", t2)
                    self.on("scalar", lambda e, i=i, g4=g4: e.copy(
                        self.XB[:, i, g4 * 512:(g4 + 1) * 512],
                        self.R[:, i, g4 * 512:(g4 + 1) * 512]), inc="ss")
            self.pb_set(4, t2)
            self.pb_set(5, t2)
        self.bar()

    def plan_moe(self, b):
        Wr = self.hw["L0_ffn_Wr"]; W1 = self.hw["L0_ffn_W1"]; W2 = self.hw["L0_ffn_W2"]
        B1 = self.hw["L0_ffn_b1"]; B2 = self.hw["L0_ffn_b2"]
        idf = self.consts["c_ident_f"]
        selc = self.consts["c_sel"]
        for i in range(NI):
            g = self.guards.get("b24")
            if g:
                self.pending["sync"].append(g)
            ldi = self.dma("sync", "misc", lambda e, i=i: e.dma_start(
                self.b24[0:8, :], Wr[:, i * 128:(i + 1) * 128]), self_wait=True)
            self.wait("tensor", ldi)
            self.pb_wait(0)
            pe = self.on("tensor", lambda e: e.matmul(
                self.pbank(0, w1=8), self.b24[0:8, :], idf[0:8, 0:8],
                is_transpose=True, start=True, stop=True), inc="spe")
            self.guards["b24"] = pe
            self.wait("vector", pe)
            evw = self.on("vector", lambda e, i=i: e.tensor_copy(
                self.wrt[:, i, :], self.pbank(0, lo=0, hi=128, w1=8)), inc="sv")
            self.pb_set(0, evw)
        for e_ in range(NE):
            self.bias_cols(B1[e_, :], HF, self.b1col[:, e_, :], bank=1)
            self.bias_cols(B2[e_, :], D, self.b2col[:, e_, :], bank=2)
        self.bar()
        for tau in range(NT):
            self.pb_wait(3)
            for i in range(NI):
                self.on("tensor", lambda e, tau=tau, i=i: e.matmul(
                    self.pbank(3, w1=8), self.R[:, i, tau * 128:(tau + 1) * 128],
                    self.wrt[:, i, :], start=(i == 0), stop=(i == NI - 1)),
                    inc="spe" if i == NI - 1 else None)
            mm = self.tok("spe")
            self.wait("vector", mm)
            self.on("vector", lambda e, tau=tau: e.tensor_reduce(
                self.pcol[:, 0:1], self.pbank(3, w1=8), axis=AX.X, op=ALU.max), inc="sv")
            v1b = self.on("vector", lambda e: e.tensor_scalar(
                self.pcol[:, 1:2], self.pcol[:, 0:1], -1.0, None, op0=ALU.mult), inc="sv")
            self.wait("scalar", v1b)
            s1 = self.on("scalar", lambda e, tau=tau: e.activation(
                self.ptmp[:, tau, :], self.pbank(3, w1=8), AF.Exp,
                bias=self.pcol[:, 1:2], scale=1.0), inc="ss")
            self.pb_set(3, s1)
            self.wait("vector", s1)
            self.on("vector", lambda e, tau=tau: e.tensor_reduce(
                self.pcol[:, 2:3], self.ptmp[:, tau, :], axis=AX.X, op=ALU.add), inc="sv")
            self.on("vector", lambda e: e.reciprocal(
                self.pcol[:, 3:4], self.pcol[:, 2:3]), inc="sv")
            self.on("vector", lambda e, tau=tau: e.tensor_scalar(
                self.probs[:, tau, :], self.ptmp[:, tau, :], self.pcol[:, 3:4], None,
                op0=ALU.mult), inc="sv")
            self.on("vector", lambda e, tau=tau: e.tensor_reduce(
                self.pcol[:, 0:1], self.probs[:, tau, :], axis=AX.X, op=ALU.max), inc="sv")
            self.on("vector", lambda e, tau=tau: e.tensor_scalar(
                self.ptmp[:, tau, :], self.probs[:, tau, :], self.pcol[:, 0:1], None,
                op0=ALU.subtract), inc="sv")
            self.on("vector", lambda e, tau=tau: e.tensor_scalar(
                self.ptmp[:, tau, :], self.ptmp[:, tau, :], 0.0, None, op0=ALU.is_ge), inc="sv")
            self.on("vector", lambda e, tau=tau: e.scalar_tensor_tensor(
                self.ptmp[:, tau, :], self.ptmp[:, tau, :], -2.0, self.probs[:, tau, :],
                op0=ALU.mult, op1=ALU.add), inc="sv")
            self.on("vector", lambda e, tau=tau: e.tensor_reduce(
                self.pcol[:, 1:2], self.ptmp[:, tau, :], axis=AX.X, op=ALU.max), inc="sv")
            self.on("vector", lambda e, tau=tau: e.tensor_scalar(
                self.ptmp[:, tau, :], self.probs[:, tau, :], self.pcol[:, 1:2], None,
                op0=ALU.subtract), inc="sv")
            self.on("vector", lambda e, tau=tau: e.tensor_scalar(
                self.ptmp[:, tau, :], self.ptmp[:, tau, :], 0.0, None, op0=ALU.is_ge), inc="sv")
            v10 = self.on("vector", lambda e, tau=tau: e.tensor_tensor(
                self.comb[:, tau, :], self.probs[:, tau, :], self.ptmp[:, tau, :],
                op=ALU.mult), inc="sv")
            self.wait("tensor", v10)
            self.pb_wait(3)
            pt = self.on("tensor", lambda e, tau=tau: e.matmul(
                self.pbank(3, lo=0, hi=8, w1=128), self.comb[:, tau, :], idf[:],
                is_transpose=True, start=True, stop=True), inc="spe")
            self.wait("vector", pt)
            evt = self.on("vector", lambda e, tau=tau: e.tensor_copy(
                self.rows[0:8, tau * 128:(tau + 1) * 128], self.pbank(3, lo=0, hi=8, w1=128)), inc="sv")
            self.pb_set(3, evt)
        self.bar()
        yflat = self.yT.rearrange("p a t -> p (a t)")
        h2flat = self.h2.rearrange("p a t -> p (a t)")

        def h_ap(fi, tg):
            base = fi * 1024 + tg * 512
            tens = yflat if base < 16384 else h2flat
            return tens[:, base % 16384: base % 16384 + 512]

        selc = self.consts["c_sel"]
        for th in range(2):
            if self.guards.get("wcache_done") is None and "d_wcache" in self.sems and th == 1:
                self.guards["wcache_done"] = self.tok("d_wcache")
            for e_ in range(NE):
                evc = []
                for tg, bkc in ((0, 3), (1, 6)):
                    self.pb_wait(bkc)
                    mmc = self.on("tensor", lambda e, e_=e_, th=th, tg=tg, bkc=bkc: e.matmul(
                        self.pbank(bkc), selc[:, e_ * 128:(e_ + 1) * 128],
                        self.rows[0:8, th * 1024 + tg * 512: th * 1024 + (tg + 1) * 512],
                        start=True, stop=True), inc="spe")
                    self.wait("vector", mmc)
                    ec = self.on("vector", lambda e, tg=tg, bkc=bkc: e.tensor_copy(
                        self.cbe[:, tg, :], self.pbank(bkc)), inc="sv")
                    self.pb_set(bkc, ec)
                    evc.append(ec)
                for slab in range(16):
                    spar = slab % 2
                    g = self.guards.get(("w1s", spar))
                    if g:
                        self.pending["gpsimd"].append(g)
                        self.pending["sync"].append(g)
                    gst = self.guards.get(("w1st", spar))
                    if gst:
                        self.pending["gpsimd"].append(gst)
                        self.pending["sync"].append(gst)
                    ck = ("w1", e_, slab)
                    if ck not in self.moe_cached:
                        self.moe_cached.add(ck)
                        l1 = self.dma("gpsimd", f"w1s{spar}", lambda e, e_=e_, slab=slab, spar=spar: e.dma_start(
                            self.w1s[:, spar, :, :],
                            W1[e_, :, slab * 256:(slab + 1) * 256].rearrange("(i p) f -> p i f", p=128)))
                        self.wait("sync", l1)
                        st = self.dma("sync", "wcache", lambda e, e_=e_, slab=slab, spar=spar: e.dma_start(
                            self.w1c[e_, :, slab * 256:(slab + 1) * 256].rearrange("(i p) f -> p i f", p=128),
                            self.w1s[:, spar, :, :]))
                        self.guards[("w1st", spar)] = st
                    else:
                        self.wait("sync", self.guards.get("wcache_done"))
                        l1 = self.dma("sync", f"w1r{spar}", lambda e, e_=e_, slab=slab, spar=spar: e.dma_start(
                            self.w1s[:, spar, :, :],
                            self.w1c[e_, :, slab * 256:(slab + 1) * 256].rearrange("(i p) f -> p i f", p=128)))
                        self.guards[("w1st", spar)] = None
                    for f2 in range(2):
                        ft = slab * 2 + f2
                        bks = ((ft % 2) * 2, (ft % 2) * 2 + 1)
                        for tg in range(2):
                            self.pb_wait(bks[tg])
                        self.wait("tensor", l1)
                        for i in range(NI):
                            for tg in range(2):
                                self.on("tensor", lambda e, spar=spar, i=i, f2=f2, th=th, tg=tg, bk=bks[tg]: e.matmul(
                                    self.pbank(bk), self.w1s[:, spar, i, f2 * 128:(f2 + 1) * 128],
                                    self.XB[:, i, th * 1024 + tg * 512: th * 1024 + (tg + 1) * 512],
                                    start=(i == 0), stop=(i == NI - 1)),
                                    inc="spe" if (i == NI - 1 and tg == 1) else None)
                        mm = self.tok("spe")
                        self.guards[("w1s", spar)] = mm
                        self.wait("scalar", mm)
                        for tg in range(2):
                            ge = self.on("scalar", lambda e, e_=e_, ft=ft, tg=tg, bk=bks[tg]: e.activation(
                                h_ap(ft, tg), self.pbank(bk), AF.Gelu,
                                bias=self.b1col[:, e_, ft:ft + 1], scale=1.0), inc="ss")
                            self.pb_set(bks[tg], ge)
                h_ready = self.tok("ss")
                for dt in range(NI):
                    bank0 = 4 if dt % 2 == 0 else 0
                    ltoks = []
                    for fh in range(2):
                        wpar = fh
                        g = self.guards.get(("w2s", wpar))
                        if g:
                            self.pending["gpsimd"].append(g)
                            self.pending["sync"].append(g)
                        gst = self.guards.get(("w2st", wpar))
                        if gst:
                            self.pending["gpsimd"].append(gst)
                            self.pending["sync"].append(gst)
                        ck = ("w2", e_, dt, fh)
                        if ck not in self.moe_cached:
                            self.moe_cached.add(ck)
                            l2 = self.dma("gpsimd", f"w2s{wpar}", lambda e, e_=e_, dt=dt, fh=fh, wpar=wpar: e.dma_start(
                                self.w2s[:, wpar, :, :],
                                W2[e_, fh * 2048:(fh + 1) * 2048, dt * 128:(dt + 1) * 128]
                                .rearrange("(fc p) d -> p fc d", p=128)))
                            self.wait("sync", l2)
                            st = self.dma("sync", "wcache", lambda e, e_=e_, dt=dt, fh=fh, wpar=wpar: e.dma_start(
                                self.w2c[e_, fh * 2048:(fh + 1) * 2048, dt * 128:(dt + 1) * 128]
                                .rearrange("(fc p) d -> p fc d", p=128),
                                self.w2s[:, wpar, :, :]))
                            self.guards[("w2st", wpar)] = st
                        else:
                            self.wait("sync", self.guards.get("wcache_done"))
                            l2 = self.dma("sync", f"w2r{wpar}", lambda e, e_=e_, dt=dt, fh=fh, wpar=wpar: e.dma_start(
                                self.w2s[:, wpar, :, :],
                                self.w2c[e_, fh * 2048:(fh + 1) * 2048, dt * 128:(dt + 1) * 128]
                                .rearrange("(fc p) d -> p fc d", p=128)))
                            self.guards[("w2st", wpar)] = None
                        ltoks.append((l2, wpar))
                    for tg in range(2):
                        self.pb_wait(bank0 + tg)
                    self.wait("tensor", h_ready)
                    for fh in range(2):
                        l2, wpar = ltoks[fh]
                        self.wait("tensor", l2)
                        for fc in range(16):
                            for tg in range(2):
                                self.on("tensor", lambda e, wpar=wpar, fh=fh, fc=fc, tg=tg, bk=bank0 + tg, th=th: e.matmul(
                                    self.pbank(bk), self.w2s[:, wpar, fc, :],
                                    h_ap(fh * 16 + fc, tg), start=(fh == 0 and fc == 0),
                                    stop=(fh == 1 and fc == 15)),
                                    inc="spe" if (fc == 15 and tg == 1) else None)
                        self.guards[("w2s", wpar)] = self.tok("spe")
                    mm = self.tok("spe")
                    self.wait("vector", mm)
                    self.waits("vector", evc)
                    for tg in range(2):
                        self.on("vector", lambda e, e_=e_, dt=dt, tg=tg, bk=bank0 + tg: e.scalar_tensor_tensor(
                            self.lnb[:], self.pbank(bk), self.b2col[:, e_, dt:dt + 1],
                            self.cbe[:, tg, :], op0=ALU.add, op1=ALU.mult), inc="sv")
                        ts2 = self.on("vector", lambda e, dt=dt, tg=tg, th=th: e.tensor_tensor(
                            self.R[:, dt, th * 1024 + tg * 512: th * 1024 + (tg + 1) * 512],
                            self.R[:, dt, th * 1024 + tg * 512: th * 1024 + (tg + 1) * 512],
                            self.lnb[:], op=ALU.add), inc="sv")
                        self.pb_set(bank0 + tg, ts2)
        self.bar()

    def plan_qffn(self, b):
        WQ = self.hw["L1_ffn_weight"]; BQ = self.hw["L1_ffn_bias"]
        self.bias_cols(BQ, D, self.bfcol, bank=0)
        self.bar()
        if b > 0:
            ld = self.dma("sync", "misc", lambda e: e.dma_start(
                self.wdT[:, :, :].rearrange("p a b -> p (a b)"), self.wdc), self_wait=True)
            self.guards["wdt_ld"] = ld
        for r in (range(NI) if b == 0 else ()):
            g = self.guards.get("qnat0")
            if g:
                self.pending["sync"].append(g)
            ld = self.dma("sync", "misc", lambda e, r=r: e.dma_start(
                self.qnat[:, 0, :], WQ[r * 128:(r + 1) * 128, :]), self_wait=True)
            qtok = self.plan_quant_tile(lambda: self.qnat[:, 0, :], ld)
            self.guards["qnat0"] = qtok
            for i in range(NI):
                self.pb_wait(7)
                self.wait("tensor", qtok)
                pe = self.on("tensor", lambda e, i=i: e.matmul(
                    self.ptb[:, 0:128], self.qout[:, i * 128:(i + 1) * 128],
                    self.consts["c_ident_b"][:],
                    is_transpose=True, start=True, stop=True), inc="spe")
                self.wait("vector", pe)
                ev = self.on("vector", lambda e, i=i, r=r: e.tensor_copy(
                    self.wdT[:, i, r * 128:(r + 1) * 128], self.ptb[:, 0:128]), inc="sv")
                self.pb_set(7, ev)
            self.pending["vector"].append(self.tok("spe"))
        if b == 0:
            self.pending["sync"].append(self.tok("sv"))
            st = self.dma("sync", "misc", lambda e: e.dma_start(
                self.wdc, self.wdT[:, :, :].rearrange("p a b -> p (a b)")), self_wait=True)
            self.guards["wdc_store"] = st
        self.bar()
        for tgc in range(NG):
            for tau in range(4 * tgc, 4 * tgc + 4):
                par = tau % 2
                evs = []
                for i in range(NI):
                    bank = 4 + i % 2
                    eng = "vector" if i % 2 == 0 else "scalar"
                    g = self.guards.get(("qnat", par))
                    if g:
                        self.pending[eng].append(g)
                    ev = self.transpose_f32(
                        lambda i=i, t=tau: self.R[:, i, t * 128:(t + 1) * 128],
                        lambda p=par, i=i: self.qnat[:, p, i * 128:(i + 1) * 128],
                        bank, eng)
                    evs.append(ev)
                qtok = self.plan_quant_tile(lambda p=par: self.qnat[:, p, :], None, pre=evs)
                self.guards[("qnat", par)] = qtok
                for i in range(NI):
                    self.pb_wait(7)
                    self.wait("tensor", qtok)
                    pe = self.on("tensor", lambda e, i=i: e.matmul(
                        self.ptb[:, 0:128], self.qout[:, i * 128:(i + 1) * 128],
                        self.consts["c_ident_b"][:],
                        is_transpose=True, start=True, stop=True), inc="spe")
                    self.wait("vector", pe)
                    ev = self.on("vector", lambda e, i=i, t=tau: e.tensor_copy(
                        self.XB[:, i, t * 128:(t + 1) * 128], self.ptb[:, 0:128]), inc="sv")
                    self.pb_set(7, ev)
                self.pending["vector"].append(self.tok("spe"))
            xbtok = self.tok("sv")
            for mt in range(NI):
                bank = mt % 2
                self.pb_wait(bank)
                self.wait("tensor", xbtok)
                self.wait("tensor", self.guards.pop("wdt_ld", None))
                for i in range(NI):
                    self.on("tensor", lambda e, mt=mt, i=i, tgc=tgc: e.matmul(
                        self.pbank(mt % 2),
                        self.wdT[:, i, mt * 128:(mt + 1) * 128],
                        self.XB[:, i, tgc * 512:(tgc + 1) * 512],
                        start=(i == 0), stop=(i == NI - 1)),
                        inc="spe" if i == NI - 1 else None)
                mm = self.tok("spe")
                self.wait("vector", mm)
                ev = self.on("vector", lambda e, mt=mt, tgc=tgc: e.scalar_tensor_tensor(
                    self.R[:, mt, tgc * 512:(tgc + 1) * 512], self.pbank(mt % 2),
                    self.bfcol[:, mt:mt + 1], self.R[:, mt, tgc * 512:(tgc + 1) * 512],
                    op0=ALU.add, op1=ALU.add), inc="sv")
                self.pb_set(mt % 2, ev)
        self.bar()

    def plan_quant_tile(self, src_fn, ld_tok, pre=()):
        if ld_tok is not None:
            self.pending["vector"].append(ld_tok)
        for t in pre:
            self.wait("vector", t)
        self.on("vector", lambda e: e.tensor_reduce(
            self.qsc[:], src_fn().rearrange("p (blk c) -> p blk c", c=32),
            axis=AX.X, op=ALU.max, apply_absolute_value=True), inc="sv")
        self.on("vector", lambda e: e.tensor_scalar(
            self.qsc[:], self.qsc[:], 1.0 / 7.5, 1e-30, op0=ALU.mult, op1=ALU.max), inc="sv")
        self.on("vector", lambda e: e.reciprocal(self.qscr[:], self.qsc[:]), inc="sv")
        self.on("vector", lambda e: e.tensor_tensor(
            self.qy[:].rearrange("p (blk c) -> p blk c", c=32),
            src_fn().rearrange("p (blk c) -> p blk c", c=32),
            self.qscr[:].unsqueeze(2).broadcast_to((128, 32, 32)),
            op=ALU.mult), inc="sv")
        ytok = self.tok("sv")
        self.wait("scalar", ytok)
        atok = self.on("scalar", lambda e: e.activation(
            self.qa[:], self.qy[:], AF.Abs, bias=0.0, scale=1.0), inc="ss")
        self.wait("vector", atok)
        self.on("vector", lambda e: e.tensor_scalar(
            self.qm2[:], self.qa[:], 2.0, -4.0, op0=ALU.is_ge, op1=ALU.mult), inc="sv")
        self.on("vector", lambda e: e.tensor_scalar(
            self.qm4[:], self.qa[:], 4.0, -2.0, op0=ALU.is_ge, op1=ALU.mult), inc="sv")
        self.on("vector", lambda e: e.scalar_tensor_tensor(
            self.qd[:], self.qm2[:], 8.0, self.qm4[:],
            op0=ALU.add, op1=ALU.add), inc="sv")
        self.on("vector", lambda e: e.tensor_tensor(
            self.qzi[:], self.qy[:], self.qd[:], op=ALU.mult), inc="sv")
        self.on("vector", lambda e: e.tensor_scalar(
            self.qzi[:], self.qzi[:], 12582912.0, -12582912.0, op0=ALU.add, op1=ALU.add), inc="sv")
        self.on("vector", lambda e: e.tensor_scalar(
            self.qa[:], self.qm2[:], -0.03125, 0.125, op0=ALU.mult, op1=ALU.add), inc="sv")
        self.on("vector", lambda e: e.scalar_tensor_tensor(
            self.qa[:], self.qm4[:], -0.125, self.qa[:],
            op0=ALU.mult, op1=ALU.add), inc="sv")
        self.on("vector", lambda e: e.tensor_tensor(
            self.qy[:], self.qzi[:], self.qa[:], op=ALU.mult), inc="sv")
        q2 = self.on("vector", lambda e: e.tensor_tensor(
            self.qout[:].rearrange("p (blk c) -> p blk c", c=32),
            self.qy[:].rearrange("p (blk c) -> p blk c", c=32),
            self.qsc[:].unsqueeze(2).broadcast_to((128, 32, 32)),
            op=ALU.mult), inc="sv")
        return q2


def kernel(**inputs):
    from concourse.bass_utils import run_bass_kernel_spmd
    x = np.ascontiguousarray(np.asarray(inputs["L0_attn_x"], dtype=np.float32))
    mk = MK(nbatch=2, layers=(0, 1), sublayers=("attn", "ffn"))
    nc = bass.Bass("TRN2", target_bir_lowering=False)
    es = ExitStack()
    mk.build(nc, es)
    in_maps = [mk.in_map(x[2 * c:2 * c + 2], inputs) for c in range(8)]
    res = run_bass_kernel_spmd(nc, in_maps, core_ids=list(range(8)))
    out = np.concatenate([np.asarray(r["x_out"]) for r in res.results], axis=0)
    return out.astype(np.float32)


# version 14
# speedup vs baseline: 1.2703x; 1.0655x over previous
import numpy as np
from contextlib import ExitStack
import concourse.bass as bass
import concourse.mybir as mybir

F32 = mybir.dt.float32
BF16 = mybir.dt.bfloat16
F32R = mybir.dt.float32r
I32 = mybir.dt.int32
AF = mybir.ActivationFunctionType
ALU = mybir.AluOpType
AX = mybir.AxisListType

ENGINES = ("tensor", "vector", "scalar", "sync", "gpsimd")

D = 1024
T = 2048
NH = 16
DH = 64
NI = 8
NT = 16
NG = 4
NE = 8
HF = 4096
NFI = 32
NEG_BIG = -1.0e30
QS = 0.35355339059327373
LN_EPS = 1e-5


def make_consts():
    c = {}
    c["c_ident_f"] = np.eye(128, dtype=np.float32)
    c["c_ident_b"] = np.eye(128, dtype=np.float32)
    c["c_tri_b"] = np.triu(np.ones((128, 128), np.float32), 1) * NEG_BIG
    c["c_onesrow_f"] = np.ones((1, 128), dtype=np.float32)
    c["c_onesrow_r"] = np.ones((1, 128), dtype=np.float32)
    c["c_halfcol_b"] = np.full((128, 1), 0.5, dtype=np.float32)
    c["c_onescol_b"] = np.ones((128, 1), dtype=np.float32)
    c["c_iota_row_b"] = np.arange(T, dtype=np.float32)[None, :]
    c["c_half_row_b"] = np.full((1, 512), 0.5, dtype=np.float32)
    c["c_iotacol16"] = (np.arange(16)[None, :] * 128 + np.arange(128)[:, None]).astype(np.float32)
    c["c_epscol"] = np.full((128, 1), LN_EPS, dtype=np.float32)
    sel = np.zeros((8, 8 * 128), dtype=np.float32)
    for e in range(8):
        sel[e, e * 128:(e + 1) * 128] = 1.0
    c["c_sel"] = sel
    return c


CONST_DTYPES = {
    "c_ident_f": F32, "c_ident_b": BF16, "c_tri_b": BF16,
    "c_onesrow_f": F32, "c_onesrow_r": F32R, "c_halfcol_b": BF16, "c_onescol_b": BF16,
    "c_iota_row_b": BF16, "c_half_row_b": BF16,
    "c_iotacol16": F32, "c_sel": F32, "c_epscol": F32,
}


def np_for(dt):
    import ml_dtypes
    return {F32: np.float32, BF16: ml_dtypes.bfloat16, F32R: np.float32,
            I32: np.int32}[dt]


class MK:
    def __init__(self, nbatch=2, layers=(0, 1), sublayers=("attn", "ffn")):
        self.nbatch = nbatch
        self.layers = tuple(layers)
        self.sublayers = tuple(sublayers)

    def sem(self, name):
        if name not in self.sems:
            h = self.es.enter_context(self.nc.semaphore(name))
            self.sems[name] = [h, 0]
        return self.sems[name]

    def on(self, eng, fn, inc=None, n=1, waits=()):
        w = self.pending[eng] + list(waits)
        self.pending[eng] = []
        if eng in ("vector", "scalar"):
            own = self.OWN[eng]
            c = self.sem(own)[1]
            if c > 0:
                w.append((own, c))
            if inc is None:
                inc = own
        lw = self.lastwait[eng]
        best = {}
        for t in w:
            if t is None:
                continue
            s, v = t
            if v > best.get(s, -1):
                best[s] = v
        w2 = []
        for s, v in best.items():
            if v > lw.get(s, -1) and v > 0:
                w2.append((s, v))
                lw[s] = v
        tok = None
        if inc is not None:
            c = self.sem(inc)
            c[1] += n
            tok = (inc, c[1])
        self.ops[eng].append((tuple(w2), fn, inc, n))
        self.nops += 1
        return tok

    def tok(self, sem):
        return (sem, self.sems[sem][1])

    def dma(self, queue, stream, fn, self_wait=False):
        tok = self.on(queue, fn, inc="d_" + stream, n=16)
        if self_wait:
            self.on(queue, lambda e: e.nop(), waits=[tok])
        return tok

    def flush_closes(self):
        while self._bc_defer:
            self._bc_defer.pop(0)[1]()
        while self._norm_stages:
            self._norm_stages.pop(0)()

    def lp(self, fn):
        def wrapped(e):
            with self.nc.allow_low_precision(reason="f32r row for broadcast matmul"):
                return fn(e)
        return wrapped

    def wait(self, eng, tok):
        if tok is not None:
            self.pending[eng].append(tok)

    def waits(self, eng, toks):
        for t in toks:
            self.wait(eng, t)

    OWN = {"tensor": "spe", "vector": "sv", "scalar": "ss", "sync": "sio", "gpsimd": "sdw"}

    def bar(self, engines=ENGINES):
        snap = [(s, c[1]) for s, c in self.sems.items() if c[1] > 0]
        for e in engines:
            for s, v in snap:
                self.pending[e].append((s, v))

    def replay(self, ename, eng):
        sems = self.sems
        for w, fn, inc, n in self.ops[ename]:
            for s, v in w:
                eng.wait_ge(sems[s][0], v)
            instr = fn(eng)
            if inc is not None:
                instr.then_inc(sems[inc][0], n)

    def pbank(self, k, lo=0, hi=128, w0=0, w1=512):
        if k < 4:
            return self.pbA[lo:hi, k * 512 + w0: k * 512 + w1]
        if k < 8:
            return self.pbB[lo:hi, (k - 4) * 512 + w0: (k - 4) * 512 + w1]
        raise KeyError(k)

    def pb_wait(self, k):
        g = self.bank_guard.get(k)
        if g:
            if isinstance(g[0], str):
                self.pending["tensor"].append(g)
            else:
                for t in g:
                    self.pending["tensor"].append(t)

    def pb_set(self, k, tok):
        self.bank_guard[k] = tok

    def build(self, nc, es):
        self.nc = nc
        self.es = es
        self.ops = {e: [] for e in ENGINES}
        self.pending = {e: [] for e in ENGINES}
        self._bc_defer = []
        self._norm_stages = []
        self.lastwait = {e: {} for e in ENGINES}
        self.sems = {}
        self.bank_guard = {}
        self.guards = {}
        self.nops = 0
        for s in ("spe", "sv", "ss", "sio", "sdw"):
            self.sem(s)

        dram = nc.dram_tensor
        self.moe_cached = set()
        self.w1c = dram("w1cache", [NE, D, HF], BF16, kind="Internal").ap()
        self.w2c = dram("w2cache", [NE, HF, D], BF16, kind="Internal").ap()
        self.hx = dram("x_in", [self.nbatch, T, D], F32, kind="ExternalInput").ap()
        self.hout = dram("x_out", [self.nbatch, T, D], F32, kind="ExternalOutput").ap()
        self.hw = {}

        def win(name, shape):
            self.hw[name] = dram(name, shape, F32, kind="ExternalInput").ap()
        if 0 in self.layers:
            if "attn" in self.sublayers:
                win("L0_attn_w_attn", [D, 3 * D]); win("L0_attn_b_attn", [3 * D])
                win("L0_attn_w_proj", [D, D]); win("L0_attn_b_proj", [D])
            if "ffn" in self.sublayers:
                win("L0_ffn_Wr", [NE, D]); win("L0_ffn_W1", [NE, D, HF])
                win("L0_ffn_b1", [NE, HF]); win("L0_ffn_W2", [NE, HF, D])
                win("L0_ffn_b2", [NE, D])
        if 1 in self.layers:
            if "attn" in self.sublayers:
                win("L1_attn_W_kqv", [D, 3 * D]); win("L1_attn_b_kqv", [3 * D])
                win("L1_attn_W_proj", [D, D]); win("L1_attn_b_proj", [D])
                win("L1_attn_slopes", [NH])
            if "ffn" in self.sublayers:
                win("L1_ffn_weight", [D, D]); win("L1_ffn_bias", [D])
        self.hc = {}
        cvals = make_consts()
        for name, dt in CONST_DTYPES.items():
            self.hc[name] = dram(name, list(cvals[name].shape), dt, kind="ExternalInput").ap()

        sb = lambda name, shape, dt: es.enter_context(nc.sbuf_tensor(name, shape, dt))

        self.R = sb("R", [128, NI, T], F32)
        self.XB = sb("XB", [128, NI, T], BF16)
        self.yT = sb("yT", [128, NI, T], BF16)
        self.rows = sb("rows", [128, T], F32)
        self.rows2 = sb("rows2", [128, 512], F32R)
        self.lnb = sb("lnb", [128, 512], F32)
        self.bsc = self.lnb
        self.kcell = sb("kcell", [1, 2], BF16)
        self.nrcell = sb("nrcell", [128, 16], BF16)
        self.bcol = sb("bcol", [128, 24], F32)
        self.bqcol = sb("bqcol", [128, 16], F32)
        self.bkcol = sb("bkcol", [128, 16], F32)
        self.bpcol = sb("bpcol", [128, 8], F32)
        self.slope_bc = sb("slope_bc", [128, 16], F32)
        self.scol = sb("scol", [128, 32], F32)
        self.b24 = sb("b24", [32, 128], F32)
        self.bfcol = sb("bfcol", [128, 8], F32)
        self.pcol = sb("pcol", [128, 4], F32)
        self.qsc = sb("qsc", [128, D // 32], F32)
        self.qscr = sb("qscr", [128, D // 32], F32)
        self.consts = {}
        for name, dt in CONST_DTYPES.items():
            self.consts[name] = sb("s" + name, list(cvals[name].shape), dt)
        ARENA = 56320
        pad = (-nc.sbuf_base) % 32
        base0 = nc.sbuf_base + pad
        self.arena = sb("arena", [128, ARENA + pad], mybir.dt.uint8)
        at = lambda name, shape, dt, off: nc.alloc_sbuf_tensor_at(name, shape, dt, offset=base0 + off)
        self.vg = at("vg", [128, NT, 4 * 65], BF16, 0)
        self.Qh = at("Qh", [128, 2, T], BF16, 8320)
        self.Kh = at("Kh", [128, 2, T], BF16, 16512)
        self.asb = at("asb", [128, 4, 512], BF16, 24704)
        self.lnsq = at("lnsq", [128, T], BF16, 24704)
        self.wqk = at("wqk", [128, 2, 2, NI, 128], BF16, 28800)
        self.wv = at("wv", [128, NI, 256], BF16, 36992)
        self.vbias = at("vbias", [128, D], F32, 41088)
        self.wpj = at("wpj", [128, 2, NI, 128], BF16, 28800)
        self.tload = at("tload", [128, 2, D], F32, 0)
        self.w1s = at("w1s", [128, 2, NI, 256], BF16, 0)
        self.w2s = at("w2s", [128, 2, 16, 128], BF16, 8192)
        self.cbe = at("cbe", [128, 2, 512], F32, 16384)
        self.wrt = at("wrt", [128, NI, 8], F32, 20480)
        self.probs = at("probs", [128, NT, NE], F32, 20736)
        self.ptmp = at("ptmp", [128, NT, NE], F32, 21248)
        self.comb = at("comb", [128, NT, NE], F32, 21760)
        self.b1col = at("b1col", [128, NE, NFI], F32, 22272)
        self.b2col = at("b2col", [128, NE, NI], F32, 23296)
        self.h2 = at("h2", [128, NI, T], BF16, 23552)
        self.wdT = at("wdT", [128, NI, D], BF16, 0)
        self.wdc = nc.dram_tensor("wdcache", [128, NI * D], BF16, kind="Internal").ap()
        self.dscr = nc.dram_tensor("dbcast", [2, 1, 512], F32, kind="Internal").ap()
        self.qnat = at("qnat", [128, 2, D], F32, 16384)
        self.qy = at("qy", [128, D], F32, 24576)
        self.qa = at("qa", [128, D], F32, 28672)
        self.qd = at("qd", [128, D], F32, 32768)
        self.qm2 = at("qm2", [128, D], F32, 36864)
        self.qm4 = at("qm4", [128, D], F32, 40960)
        self.qzi = at("qzi", [128, D], F32, 45056)
        self.qout = at("qout", [128, D], BF16, 32768)
        self.pbA = es.enter_context(nc.psum_tensor("pbA", [128, 2048], F32))
        self.pbB = es.enter_context(nc.psum_tensor("pbB", [128, 2048], F32))
        self.ptb = self.pbB[:, 1536:2048].bitcast(BF16)

        self.plan_consts_load()
        for b in range(self.nbatch):
            self.plan_batch(b)
        fin = [(s, c[1]) for s, c in self.sems.items() if s.startswith("d_") and c[1] > 0]
        self.on("sync", lambda e: e.nop(), waits=fin)

        block = es.enter_context(nc.Block())
        for ename in ENGINES:
            getattr(block, ename)(lambda eng, _e=ename: self.replay(_e, eng))
        return nc

    def in_map(self, x_shard, weights):
        m = {"x_in": np.ascontiguousarray(x_shard, dtype=np.float32)}
        for k in self.hw:
            m[k] = np.ascontiguousarray(np.asarray(weights[k]), dtype=np.float32)
        cv = make_consts()
        for k, dt in CONST_DTYPES.items():
            m[k] = np.ascontiguousarray(cv[k].astype(np_for(dt)))
        return m

    def plan_consts_load(self):
        for name in CONST_DTYPES:
            self.dma("sync", "consts", lambda e, d=self.consts[name], s=self.hc[name]:
                     e.dma_start(d[:], s))
        self.bar()

    def plan_batch(self, b):
        self.plan_prologue(b)
        for L in self.layers:
            if "attn" in self.sublayers:
                self.plan_attn(b, L)
                self.plan_ln(b)
            if "ffn" in self.sublayers:
                if L == 0:
                    self.plan_moe(b)
                else:
                    self.plan_qffn(b)
                self.plan_ln(b, final=(L == self.layers[-1]))
        self.plan_epilogue(b)

    def transpose_f32(self, src_ap_fn, dst_ap_fn, bank, eng):
        idf = self.consts["c_ident_f"]
        self.pb_wait(bank)
        pe = self.on("tensor", lambda e: e.matmul(
            self.pbank(bank, w1=128), src_ap_fn(), idf[:], is_transpose=True,
            start=True, stop=True), inc="spe")
        self.wait(eng, pe)
        if eng == "vector":
            ev = self.on("vector", lambda e: e.tensor_copy(dst_ap_fn(), self.pbank(bank, w1=128)), inc="sv")
        else:
            ev = self.on("scalar", lambda e: e.copy(dst_ap_fn(), self.pbank(bank, w1=128)), inc="ss")
        self.pb_set(bank, ev)
        self._last_tp = pe
        return ev

    def plan_prologue(self, b):
        import os as _os
        _nt = int(_os.environ.get("MK_DEBUG_NT", str(NT)))
        self.wait("sync", self.guards.pop("wdc_store", None))
        guards = {}
        for tau in range(_nt):
            par = tau % 2
            g = guards.get(par)
            if g:
                self.pending["sync"].append(g)
            ld = self.dma("sync", f"x{par}", lambda e, p=par, t=tau: e.dma_start(
                self.tload[:, p, :], self.hx[b, t * 128:(t + 1) * 128, :]))
            self.wait("tensor", ld)
            for i in range(NI):
                bank = i % 2
                eng = "vector" if (i % 4) < 2 else "scalar"
                ev = self.transpose_f32(
                    lambda p=par, i=i: self.tload[:, p, i * 128:(i + 1) * 128],
                    lambda i=i, t=tau: self.R[:, i, t * 128:(t + 1) * 128],
                    bank, eng)
                eng2 = "scalar" if eng == "vector" else "vector"
                self.wait(eng2, ev)
                if eng2 == "vector":
                    self.on("vector", lambda e, i=i, t=tau: e.tensor_copy(
                        self.XB[:, i, t * 128:(t + 1) * 128],
                        self.R[:, i, t * 128:(t + 1) * 128]), inc="sv")
                else:
                    self.on("scalar", lambda e, i=i, t=tau: e.copy(
                        self.XB[:, i, t * 128:(t + 1) * 128],
                        self.R[:, i, t * 128:(t + 1) * 128]), inc="ss")
            guards[par] = self.tok("spe")
        self.bar()

    def plan_epilogue(self, b):
        st_guard = {}
        for tau in range(NT):
            par = tau % 2
            evs = []
            for i in range(NI):
                bank = i % 2
                eng = "vector" if (i % 4) < 2 else "scalar"
                g = st_guard.get(par)
                if g:
                    self.pending[eng].append(g)
                ev = self.transpose_f32(
                    lambda i=i, t=tau: self.R[:, i, t * 128:(t + 1) * 128],
                    lambda p=par, i=i: self.qnat[:, p, i * 128:(i + 1) * 128],
                    bank, eng)
                evs.append(ev)
            self.waits("sync", evs)
            st = self.dma("sync", f"out{par}", lambda e, p=par, t=tau: e.dma_start(
                self.hout[b, t * 128:(t + 1) * 128, :], self.qnat[:, p, :]))
            st_guard[par] = st
        self.bar()

    def bias_cols(self, hvec, n, dst, bank=0):
        nr = n // 128
        assert nr <= 32
        g = self.guards.get("b24")
        if g:
            self.pending["sync"].append(g)
        ld = self.dma("sync", "misc", lambda e, nr=nr: e.dma_start(
            self.b24[0:nr, :], hvec.rearrange("(j p) -> j p", p=128)), self_wait=True)
        self.wait("tensor", ld)
        idf = self.consts["c_ident_f"]
        self.pb_wait(bank)
        pe = self.on("tensor", lambda e, nr=nr: e.matmul(
            self.pbank(bank, w1=nr), self.b24[0:nr, :], idf[0:nr, 0:nr],
            is_transpose=True, start=True, stop=True), inc="spe")
        self.guards["b24"] = pe
        self.wait("vector", pe)
        ev = self.on("vector", lambda e, nr=nr: e.tensor_copy(dst[:, 0:nr], self.pbank(bank, w1=nr)), inc="sv")
        self.pb_set(bank, ev)
        return ev

    def plan_attn(self, b, L):
        alibi = (L == 1)
        W = self.hw["L1_attn_W_kqv" if alibi else "L0_attn_w_attn"]
        Bv = self.hw["L1_attn_b_kqv" if alibi else "L0_attn_b_attn"]
        WP = self.hw["L1_attn_W_proj" if alibi else "L0_attn_w_proj"]
        BP = self.hw["L1_attn_b_proj" if alibi else "L0_attn_b_proj"]
        qoff = D if alibi else 0
        koff = 0 if alibi else D
        voff = 2 * D
        onesr = self.consts["c_onesrow_f"]

        ev = self.bias_cols(Bv, 3 * D, self.bcol, bank=0)
        self.wait("vector", ev)
        for h in range(NH):
            jq, rq = divmod(qoff + 64 * h, 128)
            jk, rk = divmod(koff + 64 * h, 128)
            self.on("vector", lambda e, h=h, jq=jq, rq=rq: e.tensor_scalar(
                self.bqcol[0:64, h:h + 1], self.bcol[rq:rq + 64, jq:jq + 1], QS, None,
                op0=ALU.mult), inc="sv")
            self.on("vector", lambda e, h=h, jk=jk, rk=rk: e.tensor_scalar(
                self.bkcol[0:64, h:h + 1], self.bcol[rk:rk + 64, jk:jk + 1], QS, None,
                op0=ALU.mult), inc="sv")
        self.bias_cols(BP, D, self.bpcol, bank=1)
        ldv = self.dma("sync", "misc", lambda e: e.dma_start(
            self.rows[0:1, 0:D], Bv[voff:voff + D].rearrange("(o f) -> o f", o=1)), self_wait=True)
        self.wait("tensor", ldv)
        for jj in range(8):
            bank = 2 + jj % 2
            self.pb_wait(bank)
            pe = self.on("tensor", lambda e, jj=jj, bank=bank: e.matmul(
                self.pbank(bank, w1=128), onesr[:],
                self.rows[0:1, jj * 128:(jj + 1) * 128],
                start=True, stop=True), inc="spe")
            self.wait("vector", pe)
            evv = self.on("vector", lambda e, jj=jj, bank=bank: e.tensor_copy(
                self.vbias[:, jj * 128:(jj + 1) * 128], self.pbank(bank, w1=128)), inc="sv")
            self.pb_set(bank, evv)
        if alibi:
            self.pending["sync"].append(self.tok("spe"))
            lds = self.dma("sync", "misc", lambda e: e.dma_start(
                self.rows[0:1, 0:16], self.hw["L1_attn_slopes"].rearrange("(o s) -> o s", o=1)),
                self_wait=True)
            self.wait("tensor", lds)
            self.pb_wait(4)
            pe = self.on("tensor", lambda e: e.matmul(
                self.pbank(4, w1=16), onesr[:], self.rows[0:1, 0:16],
                start=True, stop=True), inc="spe")
            self.wait("vector", pe)
            self.on("vector", lambda e: e.tensor_copy(self.slope_bc[:], self.pbank(4, w1=16)), inc="sv")
            e2 = self.on("vector", lambda e: e.tensor_scalar(
                self.nrcell[:], self.pbank(4, w1=16), 1.0, 0.0, op0=ALU.mult, op1=ALU.max), inc="sv")
            self.pb_set(4, e2)
        self.on("vector", lambda e: e.memset(self.Kh[64:65, :], 1.0), inc="sv")
        self.bar()

        _nh = NH
        halfc = self.consts["c_halfcol_b"]
        hctx = {}
        for h in range(_nh + 1):
            if h < _nh:
                p = h % 2
                par = (h // 2) % 2
                hp = h % 2
                if hp == 0:
                    g = self.guards.get(("wqk", par))
                    if g:
                        self.pending["gpsimd"].append(g)
                    self.dma("gpsimd", f"wqk{par}", lambda e, h=h, pq=par: e.dma_start(
                        self.wqk[:, pq, 0, :, :],
                        W[:, qoff + 64 * h: qoff + 64 * (h + 2)].rearrange("(i p) f -> p i f", p=128)))
                    lk = self.dma("gpsimd", f"wqk{par}", lambda e, h=h, pq=par: e.dma_start(
                        self.wqk[:, pq, 1, :, :],
                        W[:, koff + 64 * h: koff + 64 * (h + 2)].rearrange("(i p) f -> p i f", p=128)))
                    self.guards[("wqk_tok", par)] = lk
                lk = self.guards[("wqk_tok", par)]
                if h % 4 == 0:
                    g = self.guards.get("wv")
                    if g:
                        self.pending["gpsimd"].append(g)
                    lv = self.dma("gpsimd", "wv", lambda e, h=h: e.dma_start(
                        self.wv[:],
                        W[:, voff + 64 * h: voff + 64 * (h + 4)].rearrange("(i p) f -> p i f", p=128)))
                sc = None
                if alibi:
                    sc = self.on("vector", lambda e, h=h, p=p: e.tensor_scalar(
                        self.scol[:, p * 16:(p + 1) * 16], self.consts["c_iotacol16"][:],
                        self.slope_bc[:, h:h + 1], None, op0=ALU.mult),
                        inc="sv", waits=[self.tok("ss")])
                hctx[h] = {"p": p, "rowtok": None, "sc": sc}
                qk_evacs = []
                for qk in (0, 1):
                    col = self.bqcol if qk == 0 else self.bkcol
                    dst = self.Qh if qk == 0 else self.Kh
                    self.wait("tensor", lk)
                    for g4 in range(NG):
                        self.pb_wait(g4)
                        for i in range(NI):
                            self.on("tensor", lambda e, pq=par, qk=qk, i=i, g4=g4, hp=hp: e.matmul(
                                self.pbank(g4, hi=64), self.wqk[:, pq, qk, i, hp * 64:(hp + 1) * 64],
                                self.XB[:, i, g4 * 512:(g4 + 1) * 512],
                                start=(i == 0), stop=(i == NI - 1)),
                                inc="spe" if i == NI - 1 else None)
                        mm = self.tok("spe")
                        self.wait("vector", mm)
                        evq = self.on("vector", lambda e, h=h, p=p, g4=g4, dst=dst, col=col: e.tensor_scalar(
                            dst[0:64, p, g4 * 512:(g4 + 1) * 512], self.pbank(g4, hi=64),
                            QS, col[0:64, h:h + 1], op0=ALU.mult, op1=ALU.add), inc="sv")
                        self.pb_set(g4, evq)
                        qk_evacs.append(evq)
                    self.guards[("wqk", par)] = self.tok("spe")

                def st0(h=h, p=p, qk_evacs=tuple(qk_evacs)):
                    self.waits("gpsimd", qk_evacs)
                    self._sk = self.on("gpsimd", lambda e: e.tensor_tensor(
                        self.Qh[64:128, p, :], self.Kh[0:64, p, :], self.Kh[0:64, p, :],
                        op=ALU.mult), inc="sdw")
                    self._sqq = self.on("gpsimd", lambda e: e.tensor_tensor(
                        self.yT[(h % 2) * 64:(h % 2) * 64 + 64, h // 2, :],
                        self.Qh[0:64, p, :], self.Qh[0:64, p, :], op=ALU.mult), inc="sdw")

                def st1(h=h, p=p, alibi=alibi):
                    self.wait("tensor", self._sk)
                    for g4 in range(NG):
                        self.pb_wait(g4)
                        self.on("tensor", lambda e, g4=g4: e.matmul(
                            self.pbank(g4, lo=0, hi=1), halfc[64:128, :],
                            self.Qh[64:128, p, g4 * 512:(g4 + 1) * 512], start=True, stop=True),
                            inc="spe" if g4 == NG - 1 else None)
                    mmk = self.tok("spe")
                    self.wait("vector", mmk)
                    self._rmx = self.on("vector", lambda e: e.tensor_reduce(
                        self.kcell[0:1, p:p + 1], self.pbA[0:1, :], axis=AX.X, op=ALU.max), inc="sv")

                def st2(h=h, p=p, alibi=alibi):
                    self.wait("tensor", self._rmx)
                    self.wait("tensor", self._sqq)
                    qb = (h % 2) * 64
                    for g4 in range(NG):
                        self.on("tensor", lambda e, g4=g4, qb=qb: e.matmul(
                            self.pbank(g4, lo=0, hi=1), halfc[qb:qb + 64, :],
                            self.yT[qb:qb + 64, h // 2, g4 * 512:(g4 + 1) * 512],
                            start=True, stop=False))
                        self.on("tensor", lambda e, g4=g4: e.matmul(
                            self.pbank(g4, lo=0, hi=1), self.kcell[0:1, p:p + 1],
                            self.consts["c_half_row_b"][0:1, 0:512],
                            start=False, stop=(not alibi)),
                            inc=None if alibi else ("spe" if g4 == NG - 1 else None))
                        if alibi:
                            self.on("tensor", lambda e, g4=g4: e.matmul(
                                self.pbank(g4, lo=0, hi=1), self.nrcell[0:1, h:h + 1],
                                self.consts["c_iota_row_b"][0:1, g4 * 512:(g4 + 1) * 512],
                                start=False, stop=True),
                                inc="spe" if g4 == NG - 1 else None)
                    mmq = self.tok("spe")
                    self.wait("scalar", mmq)
                    rowtok = self.on("scalar", lambda e: e.activation(
                        self.Qh[64:65, p, :], self.pbA[0:1, :], AF.Identity,
                        bias=0.0, scale=-1.0), inc="ss")
                    for g4 in range(NG):
                        self.pb_set(g4, rowtok)
                    hctx[h]["rowtok"] = rowtok
                self._norm_stages += [st0, st1, st2]
            if h > 0:
                hh = h - 1
                c = hctx[hh]
                p1 = c["p"]
                hloc = hh % 4
                self.wait("tensor", self.guards.get("vg_ready"))
                for G in range(NG):
                    yb = 6 + G % 2
                    self.pb_wait(yb)
                    ilist = list(range(4 * G + 3, -1, -1))
                    if alibi:
                        sched = [(idx,) for idx in range(len(ilist))]
                    elif G == 0:
                        sched = [(0,), (1,), (2,), (3,)]
                    else:
                        sched = [(idx,) for idx in range(4)] + [(a, a + 1) for a in range(4, len(ilist), 2)]
                    pending_pv = []
                    for grp in sched:
                        for ent in self._bc_defer:
                            ent[0] -= 1
                        while self._bc_defer and self._bc_defer[0][0] <= 0:
                            self._bc_defer.pop(0)[1]()
                        abanks = []
                        for gidx, idx in enumerate(grp):
                            i = ilist[idx]
                            diag = i - 4 * G
                            woff = 128 * diag if diag >= 0 else 0
                            wlen = 512 - woff
                            ab = 4 + (idx % 2)
                            abanks.append((idx, i, diag, woff, wlen, ab))
                            self.pb_wait(ab)
                            self.wait("tensor", c["rowtok"])
                            if alibi:
                                self.wait("tensor", c["sc"])
                            self.on("tensor", lambda e, i=i, G=G, ab=ab, woff=woff, wlen=wlen, diag=diag, p1=p1: e.matmul(
                                self.pbank(ab, w0=woff, w1=woff + wlen),
                                self.Kh[0:65, p1, i * 128:(i + 1) * 128],
                                self.Qh[0:65, p1, G * 512 + woff: G * 512 + woff + wlen],
                                start=True, stop=(diag < 0)), inc=None if diag >= 0 else "spe")
                            if diag >= 0:
                                self.on("tensor", lambda e, ab=ab, woff=woff: e.matmul(
                                    self.pbank(ab, w0=woff, w1=woff + 128),
                                    self.consts["c_tri_b"][:], self.consts["c_ident_b"][:],
                                    start=False, stop=True), inc="spe")
                        qkmm = self.tok("spe")
                        if len(grp) == 1:
                            idx, i, diag, woff, wlen, ab = abanks[0]
                            aslot = idx % 4
                            slot_g = self.guards.get(("asb", aslot))
                            if slot_g:
                                self.pending["scalar"].append(slot_g)
                            mstok = None
                            if idx == 0 and woff > 0:
                                if slot_g:
                                    self.pending["vector"].append(slot_g)
                                mstok = self.on("vector", lambda e, aslot=aslot, woff=woff: e.memset(
                                    self.asb[:, aslot, 0:woff], 0.0), inc="sv")
                            self.wait("scalar", qkmm)
                            bias = (lambda ee, i=i, p1=p1: None)
                            if alibi:
                                ex = self.on("scalar", lambda e, i=i, ab=ab, aslot=aslot, woff=woff, wlen=wlen, p1=p1:
                                             e.activation(self.asb[:, aslot, woff:woff + wlen],
                                                          self.pbank(ab, w0=woff, w1=woff + wlen),
                                                          AF.Exp, bias=self.scol[:, p1 * 16 + i:p1 * 16 + i + 1],
                                                          scale=1.0), inc="ss")
                            else:
                                ex = self.on("scalar", lambda e, i=i, ab=ab, aslot=aslot, woff=woff, wlen=wlen:
                                             e.activation(self.asb[:, aslot, woff:woff + wlen],
                                                          self.pbank(ab, w0=woff, w1=woff + wlen),
                                                          AF.Exp, bias=0.0, scale=1.0), inc="ss")
                            self.pb_set(ab, ex)
                            pairs = [(idx, i, woff, wlen, ab, aslot, ex, mstok)]
                        else:
                            (idxA, iA, _, _, _, abA), (idxB, iB, _, _, _, abB) = abanks
                            aslot = idxA % 4
                            assert abA == 4 and abB == 5 and aslot in (0, 2)
                            for s_ in (aslot, aslot + 1):
                                slot_g = self.guards.get(("asb", s_))
                                if slot_g:
                                    self.pending["scalar"].append(slot_g)
                            self.wait("scalar", qkmm)
                            ex = self.on("scalar", lambda e, aslot=aslot:
                                         e.activation(self.asb[:, aslot, :].rearrange("p a b -> p (a b)") if False else self.asb[:, aslot:aslot + 2, :].rearrange("p a b -> p (a b)"),
                                                      self.pbB[:, 0:1024],
                                                      AF.Exp, bias=0.0, scale=1.0), inc="ss")
                            self.pb_set(4, ex)
                            self.pb_set(5, ex)
                            pairs = [(idxA, iA, 0, 512, 4, aslot, ex, None),
                                     (idxB, iB, 0, 512, 5, aslot + 1, ex, None)]
                        for fn in pending_pv:
                            fn()
                        pending_pv = []
                        for idx, i, woff, wlen, ab, aslot, ex, mstok in pairs:
                            pw = 0 if idx == 0 else woff

                            def emit_pv(i=i, idx=idx, aslot=aslot, hloc=hloc, pw=pw, yb=yb, ex=ex, mstok=mstok):
                                self.wait("tensor", ex)
                                self.wait("tensor", mstok)
                                pv = self.on("tensor", lambda e: e.matmul(
                                    self.pbank(yb, hi=65, w0=pw, w1=512),
                                    self.vg[:, i, hloc * 65:(hloc + 1) * 65],
                                    self.asb[:, aslot, pw:512],
                                    start=(idx == 0), stop=(i == 0)), inc="spe")
                                self.guards[("asb", aslot)] = pv
                            pending_pv.append(emit_pv)
                    for fn in pending_pv:
                        fn()
                    pvmm = self.tok("spe")
                    if G == NG - 1 or G == (1 if hh % 2 else 0):
                        self.wait("scalar", pvmm)
                        rc = self.on("scalar", self.lp(lambda e, yb=yb: e.activation(
                            self.rows2[0:1, 0:512], self.pbank(yb, lo=64, hi=65),
                            AF.Ln, bias=0.0, scale=1.0)), inc="ss")
                        self.on("scalar", self.lp(lambda e: e.activation(
                            self.rows2[0:1, 0:512], self.rows2[0:1, 0:512],
                            AF.Exp, bias=0.0, scale=-1.0)), inc="ss")
                        rce = self.tok("ss")
                    else:
                        self.wait("vector", pvmm)
                        rc = self.on("vector", self.lp(lambda e, yb=yb: e.reciprocal(
                            self.rows2[0:1, 0:512], self.pbank(yb, lo=64, hi=65))), inc="sv")
                        rce = rc
                    def close_tail(hh=hh, G=G, yb=yb, rc=rc, rce=rce):
                        ab_bc = 4 + G % 2
                        self.wait("tensor", rce)
                        self.pb_wait(ab_bc)
                        bc = self.on("tensor", lambda e: e.matmul(
                            self.pbank(ab_bc, hi=64), self.consts["c_onesrow_r"][0:1, 0:64],
                            self.rows2[0:1, 0:512], start=True, stop=True), inc="spe")
                        self.wait("vector", bc)
                        sc2 = self.on("vector", lambda e: e.tensor_copy(
                            self.bsc[0:64, :], self.pbank(ab_bc, hi=64)), inc="sv")
                        self.pb_set(ab_bc, sc2)
                        self.wait("vector", rc)
                        ynorm = self.on("vector", lambda e: e.tensor_tensor(
                            self.yT[(hh % 2) * 64:(hh % 2) * 64 + 64, hh // 2, G * 512:(G + 1) * 512],
                            self.pbank(yb, hi=64), self.bsc[0:64, :], op=ALU.mult), inc="sv")
                        self.guards["bscw"] = ynorm
                        self.pb_set(yb, ynorm)
                    self._bc_defer.append([2, close_tail])
                    if self._norm_stages and G < NG - 1:
                        self._norm_stages.pop(0)()
            if h < _nh and h % 4 == 0:
                self.plan_vgen(b, lv, first=(h == 0), h0=h)
            if h == 0:
                while self._norm_stages:
                    self._norm_stages.pop(0)()
        self.flush_closes()
        self.bar()
        for ft in range(NI):
            par = ft % 2
            g = self.guards.get(("wpj", par))
            if g:
                self.pending["gpsimd"].append(g)
            lp = self.dma("gpsimd", f"wpj{par}", lambda e, ft=ft, p=par: e.dma_start(
                self.wpj[:, p, :, :],
                WP[:, ft * 128:(ft + 1) * 128].rearrange("(i p) f -> p i f", p=128)))
            for tg in range(NG):
                self.pb_wait(tg)
            self.wait("tensor", lp)
            for ic in range(NI):
                for tg in range(NG):
                    self.on("tensor", lambda e, ft=ft, p=par, ic=ic, tg=tg: e.matmul(
                        self.pbank(tg), self.wpj[:, p, ic, :],
                        self.yT[:, ic, tg * 512:(tg + 1) * 512],
                        start=(ic == 0), stop=(ic == NI - 1)),
                        inc="spe" if (ic == NI - 1 and tg == NG - 1) else None)
            mm = self.tok("spe")
            self.guards[("wpj", par)] = mm
            self.wait("vector", mm)
            for tg in range(NG):
                evp = self.on("vector", lambda e, ft=ft, tg=tg: e.scalar_tensor_tensor(
                    self.R[:, ft, tg * 512:(tg + 1) * 512], self.pbank(tg),
                    self.bpcol[:, ft:ft + 1], self.R[:, ft, tg * 512:(tg + 1) * 512],
                    op0=ALU.add, op1=ALU.add), inc="sv")
                self.pb_set(tg, evp)
        self.bar()

    def plan_vgen(self, b, lv_tok, first, h0=0):
        if first:
            ms = self.on("vector", lambda e: e.memset(
                self.vg.rearrange("p t (hh c) -> p t hh c", c=65)[:, :, :, 64:65], 1.0), inc="sv")
            self.wait("tensor", ms)
        ev = None
        for tau in range(NT):
            bank = tau % 2
            self.pb_wait(bank)
            self.wait("tensor", lv_tok)
            for i in range(NI):
                self.on("tensor", lambda e, tau=tau, i=i, bank=bank: e.matmul(
                    self.pbank(bank, w1=256), self.XB[:, i, tau * 128:(tau + 1) * 128],
                    self.wv[:, i, 0:256], start=(i == 0), stop=(i == NI - 1)),
                    inc="spe" if i == NI - 1 else None)
            mm = self.tok("spe")
            self.wait("vector", mm)
            ev = self.on("vector", lambda e, tau=tau, bank=bank, h0=h0: e.tensor_tensor(
                self.vg[:, tau, :].rearrange("p (hh c) -> p hh c", c=65)[:, :, 0:64],
                self.pbank(bank, w1=256).rearrange("p (hh c) -> p hh c", c=64),
                self.vbias[:, h0 * 64:(h0 + 4) * 64].rearrange("p (hh c) -> p hh c", c=64),
                op=ALU.add), inc="sv")
            self.pb_set(bank, ev)
        self.guards["vg_ready"] = ev
        self.guards["wv"] = self.tok("spe")

    def plan_ln(self, b, final=False):
        onescb = self.consts["c_onescol_b"]
        onesr = self.consts["c_onesrow_f"]
        self.bar()
        prev_sq = None
        for i in range(NI):
            srcx = self.on("vector", lambda e, i=i: e.tensor_copy(
                self.XB[:, i, :], self.R[:, i, :]), inc="sv")
            if prev_sq:
                self.pending["scalar"].append(prev_sq)
            srcs = self.on("scalar", lambda e, i=i: e.activation(
                self.lnsq[:, :], self.R[:, i, :], AF.Square, bias=0.0, scale=1.0), inc="ss")
            self.wait("tensor", srcx)
            self.wait("tensor", srcs)
            for g4 in range(NG):
                if i == 0:
                    self.pb_wait(g4)
                    self.pb_wait(4 + g4)
                self.on("tensor", lambda e, g4=g4, i=i: e.matmul(
                    self.pbank(g4, lo=0, hi=1), onescb[:],
                    self.XB[:, i, g4 * 512:(g4 + 1) * 512],
                    start=(i == 0), stop=(i == NI - 1)))
                self.on("tensor", lambda e, g4=g4, i=i: e.matmul(
                    self.pbank(4 + g4, lo=0, hi=1), onescb[:],
                    self.lnsq[:, g4 * 512:(g4 + 1) * 512],
                    start=(i == 0), stop=(i == NI - 1)),
                    inc="spe" if g4 == NG - 1 else None)
            prev_sq = self.tok("spe")
        mm = self.tok("spe")
        self.wait("scalar", mm)
        evt0 = self.on("scalar", lambda e: e.activation(
            self.rows[0:1, :], self.pbA[0:1, :], AF.Identity,
            bias=0.0, scale=1.0 / D), inc="ss")
        for g4 in range(NG):
            self.pb_set(g4, evt0)
        self.wait("vector", evt0)
        v1 = self.on("vector", lambda e: e.tensor_tensor(
            self.rows[64:65, :], self.rows[0:1, :], self.rows[0:1, :], op=ALU.mult), inc="sv")
        self.wait("vector", mm)
        v2 = self.on("vector", lambda e: e.scalar_tensor_tensor(
            self.rows[32:33, :], self.pbB[0:1, :], 1.0 / D, self.rows[64:65, :],
            op0=ALU.mult, op1=ALU.subtract), inc="sv")
        for g4 in range(NG):
            self.pb_set(4 + g4, v2)
        mmb = None
        for g4 in range(NG):
            self.wait("scalar", v2)
            self.wait("scalar", mmb)
            self.on("scalar", self.lp(lambda e, g4=g4: e.activation(
                self.rows2[0:1, 0:512], self.rows[32:33, g4 * 512:(g4 + 1) * 512], AF.Ln,
                bias=self.consts["c_epscol"][0:1, 0:1], scale=1.0)), inc="ss")
            s2 = self.on("scalar", self.lp(lambda e: e.activation(
                self.rows2[0:1, 0:512], self.rows2[0:1, 0:512], AF.Exp, bias=0.0, scale=-0.5)), inc="ss")
            self.wait("tensor", s2)
            self.pb_wait(4)
            self.on("tensor", lambda e, g4=g4: e.matmul(
                self.pbank(4), onesr[:],
                self.rows[0:1, g4 * 512:(g4 + 1) * 512],
                start=True, stop=True))
            self.pb_wait(5)
            self.on("tensor", lambda e: e.matmul(
                self.pbank(5), self.consts["c_onesrow_r"][:],
                self.rows2[0:1, 0:512],
                start=True, stop=True), inc="spe")
            mmb = self.tok("spe")
            self.wait("vector", mmb)
            t2 = None
            for i in range(NI):
                self.on("vector", lambda e, i=i, g4=g4: e.tensor_tensor(
                    self.lnb[:], self.R[:, i, g4 * 512:(g4 + 1) * 512], self.pbank(4),
                    op=ALU.subtract), inc="sv")
                t2 = self.on("vector", lambda e, i=i, g4=g4: e.tensor_tensor(
                    self.R[:, i, g4 * 512:(g4 + 1) * 512], self.lnb[:], self.pbank(5),
                    op=ALU.mult), inc="sv")
                if not final:
                    self.wait("scalar", t2)
                    self.on("scalar", lambda e, i=i, g4=g4: e.copy(
                        self.XB[:, i, g4 * 512:(g4 + 1) * 512],
                        self.R[:, i, g4 * 512:(g4 + 1) * 512]), inc="ss")
            self.pb_set(4, t2)
            self.pb_set(5, t2)
        self.bar()

    def plan_moe(self, b):
        Wr = self.hw["L0_ffn_Wr"]; W1 = self.hw["L0_ffn_W1"]; W2 = self.hw["L0_ffn_W2"]
        B1 = self.hw["L0_ffn_b1"]; B2 = self.hw["L0_ffn_b2"]
        idf = self.consts["c_ident_f"]
        selc = self.consts["c_sel"]
        for i in range(NI):
            g = self.guards.get("b24")
            if g:
                self.pending["sync"].append(g)
            ldi = self.dma("sync", "misc", lambda e, i=i: e.dma_start(
                self.b24[0:8, :], Wr[:, i * 128:(i + 1) * 128]), self_wait=True)
            self.wait("tensor", ldi)
            self.pb_wait(0)
            pe = self.on("tensor", lambda e: e.matmul(
                self.pbank(0, w1=8), self.b24[0:8, :], idf[0:8, 0:8],
                is_transpose=True, start=True, stop=True), inc="spe")
            self.guards["b24"] = pe
            self.wait("vector", pe)
            evw = self.on("vector", lambda e, i=i: e.tensor_copy(
                self.wrt[:, i, :], self.pbank(0, lo=0, hi=128, w1=8)), inc="sv")
            self.pb_set(0, evw)
        for e_ in range(NE):
            self.bias_cols(B1[e_, :], HF, self.b1col[:, e_, :], bank=1)
            self.bias_cols(B2[e_, :], D, self.b2col[:, e_, :], bank=2)
        self.bar()
        for tau in range(NT):
            self.pb_wait(3)
            for i in range(NI):
                self.on("tensor", lambda e, tau=tau, i=i: e.matmul(
                    self.pbank(3, w1=8), self.R[:, i, tau * 128:(tau + 1) * 128],
                    self.wrt[:, i, :], start=(i == 0), stop=(i == NI - 1)),
                    inc="spe" if i == NI - 1 else None)
            mm = self.tok("spe")
            self.wait("vector", mm)
            self.on("vector", lambda e, tau=tau: e.tensor_reduce(
                self.pcol[:, 0:1], self.pbank(3, w1=8), axis=AX.X, op=ALU.max), inc="sv")
            v1b = self.on("vector", lambda e: e.tensor_scalar(
                self.pcol[:, 1:2], self.pcol[:, 0:1], -1.0, None, op0=ALU.mult), inc="sv")
            self.wait("scalar", v1b)
            s1 = self.on("scalar", lambda e, tau=tau: e.activation(
                self.ptmp[:, tau, :], self.pbank(3, w1=8), AF.Exp,
                bias=self.pcol[:, 1:2], scale=1.0), inc="ss")
            self.pb_set(3, s1)
            self.wait("vector", s1)
            self.on("vector", lambda e, tau=tau: e.tensor_reduce(
                self.pcol[:, 2:3], self.ptmp[:, tau, :], axis=AX.X, op=ALU.add), inc="sv")
            self.on("vector", lambda e: e.reciprocal(
                self.pcol[:, 3:4], self.pcol[:, 2:3]), inc="sv")
            self.on("vector", lambda e, tau=tau: e.tensor_scalar(
                self.probs[:, tau, :], self.ptmp[:, tau, :], self.pcol[:, 3:4], None,
                op0=ALU.mult), inc="sv")
            self.on("vector", lambda e, tau=tau: e.tensor_reduce(
                self.pcol[:, 0:1], self.probs[:, tau, :], axis=AX.X, op=ALU.max), inc="sv")
            self.on("vector", lambda e, tau=tau: e.tensor_scalar(
                self.ptmp[:, tau, :], self.probs[:, tau, :], self.pcol[:, 0:1], None,
                op0=ALU.subtract), inc="sv")
            self.on("vector", lambda e, tau=tau: e.tensor_scalar(
                self.ptmp[:, tau, :], self.ptmp[:, tau, :], 0.0, None, op0=ALU.is_ge), inc="sv")
            self.on("vector", lambda e, tau=tau: e.scalar_tensor_tensor(
                self.ptmp[:, tau, :], self.ptmp[:, tau, :], -2.0, self.probs[:, tau, :],
                op0=ALU.mult, op1=ALU.add), inc="sv")
            self.on("vector", lambda e, tau=tau: e.tensor_reduce(
                self.pcol[:, 1:2], self.ptmp[:, tau, :], axis=AX.X, op=ALU.max), inc="sv")
            self.on("vector", lambda e, tau=tau: e.tensor_scalar(
                self.ptmp[:, tau, :], self.probs[:, tau, :], self.pcol[:, 1:2], None,
                op0=ALU.subtract), inc="sv")
            self.on("vector", lambda e, tau=tau: e.tensor_scalar(
                self.ptmp[:, tau, :], self.ptmp[:, tau, :], 0.0, None, op0=ALU.is_ge), inc="sv")
            v10 = self.on("vector", lambda e, tau=tau: e.tensor_tensor(
                self.comb[:, tau, :], self.probs[:, tau, :], self.ptmp[:, tau, :],
                op=ALU.mult), inc="sv")
            self.wait("tensor", v10)
            self.pb_wait(3)
            pt = self.on("tensor", lambda e, tau=tau: e.matmul(
                self.pbank(3, lo=0, hi=8, w1=128), self.comb[:, tau, :], idf[:],
                is_transpose=True, start=True, stop=True), inc="spe")
            self.wait("vector", pt)
            evt = self.on("vector", lambda e, tau=tau: e.tensor_copy(
                self.rows[0:8, tau * 128:(tau + 1) * 128], self.pbank(3, lo=0, hi=8, w1=128)), inc="sv")
            self.pb_set(3, evt)
        self.bar()
        yflat = self.yT.rearrange("p a t -> p (a t)")
        h2flat = self.h2.rearrange("p a t -> p (a t)")

        def h_ap(fi, tg):
            base = fi * 1024 + tg * 512
            tens = yflat if base < 16384 else h2flat
            return tens[:, base % 16384: base % 16384 + 512]

        selc = self.consts["c_sel"]
        for th in range(2):
            if self.guards.get("wcache_done") is None and "d_wcache" in self.sems and th == 1:
                self.guards["wcache_done"] = self.tok("d_wcache")
            for e_ in range(NE):
                evc = []
                for tg, bkc in ((0, 3), (1, 6)):
                    self.pb_wait(bkc)
                    mmc = self.on("tensor", lambda e, e_=e_, th=th, tg=tg, bkc=bkc: e.matmul(
                        self.pbank(bkc), selc[:, e_ * 128:(e_ + 1) * 128],
                        self.rows[0:8, th * 1024 + tg * 512: th * 1024 + (tg + 1) * 512],
                        start=True, stop=True), inc="spe")
                    self.wait("vector", mmc)
                    ec = self.on("vector", lambda e, tg=tg, bkc=bkc: e.tensor_copy(
                        self.cbe[:, tg, :], self.pbank(bkc)), inc="sv")
                    self.pb_set(bkc, ec)
                    evc.append(ec)
                for slab in range(16):
                    spar = slab % 2
                    g = self.guards.get(("w1s", spar))
                    if g:
                        self.pending["gpsimd"].append(g)
                        self.pending["sync"].append(g)
                    gst = self.guards.get(("w1st", spar))
                    if gst:
                        self.pending["gpsimd"].append(gst)
                        self.pending["sync"].append(gst)
                    ck = ("w1", e_, slab)
                    if ck not in self.moe_cached:
                        self.moe_cached.add(ck)
                        l1 = self.dma("gpsimd", f"w1s{spar}", lambda e, e_=e_, slab=slab, spar=spar: e.dma_start(
                            self.w1s[:, spar, :, :],
                            W1[e_, :, slab * 256:(slab + 1) * 256].rearrange("(i p) f -> p i f", p=128)))
                        self.wait("sync", l1)
                        st = self.dma("sync", "wcache", lambda e, e_=e_, slab=slab, spar=spar: e.dma_start(
                            self.w1c[e_, :, slab * 256:(slab + 1) * 256].rearrange("(i p) f -> p i f", p=128),
                            self.w1s[:, spar, :, :]))
                        self.guards[("w1st", spar)] = st
                    else:
                        self.wait("sync", self.guards.get("wcache_done"))
                        l1 = self.dma("sync", f"w1r{spar}", lambda e, e_=e_, slab=slab, spar=spar: e.dma_start(
                            self.w1s[:, spar, :, :],
                            self.w1c[e_, :, slab * 256:(slab + 1) * 256].rearrange("(i p) f -> p i f", p=128)))
                        self.guards[("w1st", spar)] = None
                    for f2 in range(2):
                        ft = slab * 2 + f2
                        bks = ((ft % 2) * 2, (ft % 2) * 2 + 1)
                        for tg in range(2):
                            self.pb_wait(bks[tg])
                        self.wait("tensor", l1)
                        for i in range(NI):
                            for tg in range(2):
                                self.on("tensor", lambda e, spar=spar, i=i, f2=f2, th=th, tg=tg, bk=bks[tg]: e.matmul(
                                    self.pbank(bk), self.w1s[:, spar, i, f2 * 128:(f2 + 1) * 128],
                                    self.XB[:, i, th * 1024 + tg * 512: th * 1024 + (tg + 1) * 512],
                                    start=(i == 0), stop=(i == NI - 1)),
                                    inc="spe" if (i == NI - 1 and tg == 1) else None)
                        mm = self.tok("spe")
                        self.guards[("w1s", spar)] = mm
                        self.wait("scalar", mm)
                        for tg in range(2):
                            ge = self.on("scalar", lambda e, e_=e_, ft=ft, tg=tg, bk=bks[tg]: e.activation(
                                h_ap(ft, tg), self.pbank(bk), AF.Gelu,
                                bias=self.b1col[:, e_, ft:ft + 1], scale=1.0), inc="ss")
                            self.pb_set(bks[tg], ge)
                h_ready = self.tok("ss")
                for dt in range(NI):
                    bank0 = 4 if dt % 2 == 0 else 0
                    ltoks = []
                    for fh in range(2):
                        wpar = fh
                        g = self.guards.get(("w2s", wpar))
                        if g:
                            self.pending["gpsimd"].append(g)
                            self.pending["sync"].append(g)
                        gst = self.guards.get(("w2st", wpar))
                        if gst:
                            self.pending["gpsimd"].append(gst)
                            self.pending["sync"].append(gst)
                        ck = ("w2", e_, dt, fh)
                        if ck not in self.moe_cached:
                            self.moe_cached.add(ck)
                            l2 = self.dma("gpsimd", f"w2s{wpar}", lambda e, e_=e_, dt=dt, fh=fh, wpar=wpar: e.dma_start(
                                self.w2s[:, wpar, :, :],
                                W2[e_, fh * 2048:(fh + 1) * 2048, dt * 128:(dt + 1) * 128]
                                .rearrange("(fc p) d -> p fc d", p=128)))
                            self.wait("sync", l2)
                            st = self.dma("sync", "wcache", lambda e, e_=e_, dt=dt, fh=fh, wpar=wpar: e.dma_start(
                                self.w2c[e_, fh * 2048:(fh + 1) * 2048, dt * 128:(dt + 1) * 128]
                                .rearrange("(fc p) d -> p fc d", p=128),
                                self.w2s[:, wpar, :, :]))
                            self.guards[("w2st", wpar)] = st
                        else:
                            self.wait("sync", self.guards.get("wcache_done"))
                            l2 = self.dma("sync", f"w2r{wpar}", lambda e, e_=e_, dt=dt, fh=fh, wpar=wpar: e.dma_start(
                                self.w2s[:, wpar, :, :],
                                self.w2c[e_, fh * 2048:(fh + 1) * 2048, dt * 128:(dt + 1) * 128]
                                .rearrange("(fc p) d -> p fc d", p=128)))
                            self.guards[("w2st", wpar)] = None
                        ltoks.append((l2, wpar))
                    for tg in range(2):
                        self.pb_wait(bank0 + tg)
                    self.wait("tensor", h_ready)
                    for fh in range(2):
                        l2, wpar = ltoks[fh]
                        self.wait("tensor", l2)
                        for fc in range(16):
                            for tg in range(2):
                                self.on("tensor", lambda e, wpar=wpar, fh=fh, fc=fc, tg=tg, bk=bank0 + tg, th=th: e.matmul(
                                    self.pbank(bk), self.w2s[:, wpar, fc, :],
                                    h_ap(fh * 16 + fc, tg), start=(fh == 0 and fc == 0),
                                    stop=(fh == 1 and fc == 15)),
                                    inc="spe" if (fc == 15 and tg == 1) else None)
                        self.guards[("w2s", wpar)] = self.tok("spe")
                    mm = self.tok("spe")
                    self.wait("vector", mm)
                    self.waits("vector", evc)
                    for tg in range(2):
                        self.on("vector", lambda e, e_=e_, dt=dt, tg=tg, bk=bank0 + tg: e.scalar_tensor_tensor(
                            self.lnb[:], self.pbank(bk), self.b2col[:, e_, dt:dt + 1],
                            self.cbe[:, tg, :], op0=ALU.add, op1=ALU.mult), inc="sv")
                        ts2 = self.on("vector", lambda e, dt=dt, tg=tg, th=th: e.tensor_tensor(
                            self.R[:, dt, th * 1024 + tg * 512: th * 1024 + (tg + 1) * 512],
                            self.R[:, dt, th * 1024 + tg * 512: th * 1024 + (tg + 1) * 512],
                            self.lnb[:], op=ALU.add), inc="sv")
                        self.pb_set(bank0 + tg, ts2)
        self.bar()

    def plan_qffn(self, b):
        WQ = self.hw["L1_ffn_weight"]; BQ = self.hw["L1_ffn_bias"]
        self.bias_cols(BQ, D, self.bfcol, bank=0)
        self.bar()
        if b > 0:
            ld = self.dma("sync", "misc", lambda e: e.dma_start(
                self.wdT[:, :, :].rearrange("p a b -> p (a b)"), self.wdc), self_wait=True)
            self.guards["wdt_ld"] = ld
        for r in (range(NI) if b == 0 else ()):
            g = self.guards.get("qnat0")
            if g:
                self.pending["sync"].append(g)
            ld = self.dma("sync", "misc", lambda e, r=r: e.dma_start(
                self.qnat[:, 0, :], WQ[r * 128:(r + 1) * 128, :]), self_wait=True)
            qtok = self.plan_quant_tile(lambda: self.qnat[:, 0, :], ld)
            self.guards["qnat0"] = qtok
            for i in range(NI):
                self.pb_wait(7)
                self.wait("tensor", qtok)
                pe = self.on("tensor", lambda e, i=i: e.matmul(
                    self.ptb[:, 0:128], self.qout[:, i * 128:(i + 1) * 128],
                    self.consts["c_ident_b"][:],
                    is_transpose=True, start=True, stop=True), inc="spe")
                self.wait("vector", pe)
                ev = self.on("vector", lambda e, i=i, r=r: e.tensor_copy(
                    self.wdT[:, i, r * 128:(r + 1) * 128], self.ptb[:, 0:128]), inc="sv")
                self.pb_set(7, ev)
            self.pending["vector"].append(self.tok("spe"))
        if b == 0:
            self.pending["sync"].append(self.tok("sv"))
            st = self.dma("sync", "misc", lambda e: e.dma_start(
                self.wdc, self.wdT[:, :, :].rearrange("p a b -> p (a b)")), self_wait=True)
            self.guards["wdc_store"] = st
        self.bar()
        for tgc in range(NG):
            for tau in range(4 * tgc, 4 * tgc + 4):
                par = tau % 2
                evs = []
                for i in range(NI):
                    bank = 4 + i % 2
                    eng = "vector" if i % 2 == 0 else "scalar"
                    g = self.guards.get(("qnat", par))
                    if g:
                        self.pending[eng].append(g)
                    ev = self.transpose_f32(
                        lambda i=i, t=tau: self.R[:, i, t * 128:(t + 1) * 128],
                        lambda p=par, i=i: self.qnat[:, p, i * 128:(i + 1) * 128],
                        bank, eng)
                    evs.append(ev)
                qtok = self.plan_quant_tile(lambda p=par: self.qnat[:, p, :], None, pre=evs)
                self.guards[("qnat", par)] = qtok
                for i in range(NI):
                    self.pb_wait(7)
                    self.wait("tensor", qtok)
                    pe = self.on("tensor", lambda e, i=i: e.matmul(
                        self.ptb[:, 0:128], self.qout[:, i * 128:(i + 1) * 128],
                        self.consts["c_ident_b"][:],
                        is_transpose=True, start=True, stop=True), inc="spe")
                    self.wait("vector", pe)
                    ev = self.on("vector", lambda e, i=i, t=tau: e.tensor_copy(
                        self.XB[:, i, t * 128:(t + 1) * 128], self.ptb[:, 0:128]), inc="sv")
                    self.pb_set(7, ev)
                self.pending["vector"].append(self.tok("spe"))
            xbtok = self.tok("sv")
            for mt in range(NI):
                bank = mt % 2
                self.pb_wait(bank)
                self.wait("tensor", xbtok)
                self.wait("tensor", self.guards.pop("wdt_ld", None))
                for i in range(NI):
                    self.on("tensor", lambda e, mt=mt, i=i, tgc=tgc: e.matmul(
                        self.pbank(mt % 2),
                        self.wdT[:, i, mt * 128:(mt + 1) * 128],
                        self.XB[:, i, tgc * 512:(tgc + 1) * 512],
                        start=(i == 0), stop=(i == NI - 1)),
                        inc="spe" if i == NI - 1 else None)
                mm = self.tok("spe")
                self.wait("vector", mm)
                ev = self.on("vector", lambda e, mt=mt, tgc=tgc: e.scalar_tensor_tensor(
                    self.R[:, mt, tgc * 512:(tgc + 1) * 512], self.pbank(mt % 2),
                    self.bfcol[:, mt:mt + 1], self.R[:, mt, tgc * 512:(tgc + 1) * 512],
                    op0=ALU.add, op1=ALU.add), inc="sv")
                self.pb_set(mt % 2, ev)
        self.bar()

    def plan_quant_tile(self, src_fn, ld_tok, pre=()):
        if ld_tok is not None:
            self.pending["vector"].append(ld_tok)
        for t in pre:
            self.wait("vector", t)
        self.on("vector", lambda e: e.tensor_reduce(
            self.qsc[:], src_fn().rearrange("p (blk c) -> p blk c", c=32),
            axis=AX.X, op=ALU.max, apply_absolute_value=True), inc="sv")
        self.on("vector", lambda e: e.tensor_scalar(
            self.qsc[:], self.qsc[:], 1.0 / 7.5, 1e-30, op0=ALU.mult, op1=ALU.max), inc="sv")
        self.on("vector", lambda e: e.reciprocal(self.qscr[:], self.qsc[:]), inc="sv")
        self.on("vector", lambda e: e.tensor_tensor(
            self.qy[:].rearrange("p (blk c) -> p blk c", c=32),
            src_fn().rearrange("p (blk c) -> p blk c", c=32),
            self.qscr[:].unsqueeze(2).broadcast_to((128, 32, 32)),
            op=ALU.mult), inc="sv")
        ytok = self.tok("sv")
        self.wait("scalar", ytok)
        atok = self.on("scalar", lambda e: e.activation(
            self.qa[:], self.qy[:], AF.Abs, bias=0.0, scale=1.0), inc="ss")
        self.wait("vector", atok)
        self.on("vector", lambda e: e.tensor_scalar(
            self.qm2[:], self.qa[:], 2.0, -4.0, op0=ALU.is_ge, op1=ALU.mult), inc="sv")
        self.on("vector", lambda e: e.tensor_scalar(
            self.qm4[:], self.qa[:], 4.0, -2.0, op0=ALU.is_ge, op1=ALU.mult), inc="sv")
        self.on("vector", lambda e: e.scalar_tensor_tensor(
            self.qd[:], self.qm2[:], 8.0, self.qm4[:],
            op0=ALU.add, op1=ALU.add), inc="sv")
        self.on("vector", lambda e: e.tensor_tensor(
            self.qzi[:], self.qy[:], self.qd[:], op=ALU.mult), inc="sv")
        self.on("vector", lambda e: e.tensor_scalar(
            self.qzi[:], self.qzi[:], 12582912.0, -12582912.0, op0=ALU.add, op1=ALU.add), inc="sv")
        self.on("vector", lambda e: e.tensor_scalar(
            self.qa[:], self.qm2[:], -0.03125, 0.125, op0=ALU.mult, op1=ALU.add), inc="sv")
        self.on("vector", lambda e: e.scalar_tensor_tensor(
            self.qa[:], self.qm4[:], -0.125, self.qa[:],
            op0=ALU.mult, op1=ALU.add), inc="sv")
        self.on("vector", lambda e: e.tensor_tensor(
            self.qy[:], self.qzi[:], self.qa[:], op=ALU.mult), inc="sv")
        q2 = self.on("vector", lambda e: e.tensor_tensor(
            self.qout[:].rearrange("p (blk c) -> p blk c", c=32),
            self.qy[:].rearrange("p (blk c) -> p blk c", c=32),
            self.qsc[:].unsqueeze(2).broadcast_to((128, 32, 32)),
            op=ALU.mult), inc="sv")
        return q2


def kernel(**inputs):
    from concourse.bass_utils import run_bass_kernel_spmd
    x = np.ascontiguousarray(np.asarray(inputs["L0_attn_x"], dtype=np.float32))
    mk = MK(nbatch=2, layers=(0, 1), sublayers=("attn", "ffn"))
    nc = bass.Bass("TRN2", target_bir_lowering=False)
    es = ExitStack()
    mk.build(nc, es)
    in_maps = [mk.in_map(x[2 * c:2 * c + 2], inputs) for c in range(8)]
    res = run_bass_kernel_spmd(nc, in_maps, core_ids=list(range(8)))
    out = np.concatenate([np.asarray(r["x_out"]) for r in res.results], axis=0)
    return out.astype(np.float32)
